# version 02
# baseline (speedup 1.0000x reference)
import numpy as np
from contextlib import ExitStack
import concourse.bass as bass
import concourse.mybir as mybir
from concourse.bass_utils import run_bass_kernel_spmd

F32 = mybir.dt.float32
BF16 = mybir.dt.bfloat16
I32 = mybir.dt.int32
ALU = mybir.AluOpType
AF = mybir.ActivationFunctionType
f32 = np.float32

NCORES = 8
T = 2048
D = 1024
DC = 8
NTT = 4
TT = 512

STAGE = 6
DBG = None
SKIP_EXPERTS = False
import os
CSTEP = int(os.environ.get("CSTEP", "99"))
SKIPL0 = int(os.environ.get("SKIPL0", "0"))
ASTEP = int(os.environ.get("ASTEP", "99"))
FSTEP = int(os.environ.get("FSTEP", "99"))


def _tile_major(w):
    K, M = w.shape
    return np.ascontiguousarray(
        w.reshape(K // 128, 128, M // 128, 128).transpose(2, 1, 0, 3))


def _cols(v):
    return np.ascontiguousarray(np.asarray(v).reshape(-1, 128).T)


def _prep_inputs(I):
    g = {}
    for nm in ("Wq", "Wk", "Wv", "Wo"):
        g["l0_" + nm] = _tile_major(np.asarray(I["L0_attn_" + nm]))
    g["l0_W1"] = _tile_major(np.asarray(I["L0_attn_W1"]))
    g["l0_W2"] = _tile_major(np.asarray(I["L0_attn_W2"]))
    g["l0_WgT"] = np.ascontiguousarray(np.asarray(I["L0_ffn_Wg"]).T)
    w1 = np.asarray(I["L0_ffn_w1"])
    w2 = np.asarray(I["L0_ffn_w2"])
    g["moe_w1"] = np.stack([_tile_major(w1[e].T) for e in range(8)])
    g["moe_w2"] = np.stack([_tile_major(w2[e].T) for e in range(8)])
    g["l1_Win"] = _tile_major(np.asarray(I["L1_attn_W_in"]).T)
    g["l1_Wout"] = _tile_major(np.asarray(I["L1_attn_W_out"]).T)
    g["l1_WgateT"] = np.ascontiguousarray(np.asarray(I["L1_attn_W_gate"]).T)
    g["f1_w1T"] = _tile_major(np.asarray(I["L1_ffn_w1"]).T)
    g["f1_w2T"] = _tile_major(np.asarray(I["L1_ffn_w2"]).T)
    for nm in ("bq", "bk", "bv", "bo", "g1", "be1", "b2", "g2", "be2"):
        g["l0_" + nm] = _cols(I["L0_attn_" + nm])
    g["l0_b1"] = _cols(I["L0_attn_b1"])
    g["l0_bg"] = np.ascontiguousarray(np.asarray(I["L0_ffn_bg"]).reshape(8, 1))
    g["l1_lng"] = _cols(I["L1_attn_ln_g"])
    g["l1_lnb"] = _cols(I["L1_attn_ln_b"])
    bgt = np.asarray(I["L1_attn_b_gate"])
    g["l1_bga"] = np.ascontiguousarray(bgt[0:16].reshape(16, 1))
    g["l1_bgb"] = np.ascontiguousarray(bgt[16:32].reshape(16, 1))
    g["l1_eigraw"] = np.ascontiguousarray(np.asarray(I["L1_attn_eig_raw"]).reshape(16, 1))
    cw = np.asarray(I["L1_ffn_conv_w"])
    for t in range(3):
        g[f"f1_cw{t}"] = _cols(cw[:, 0, t])
    g["f1_cb"] = _cols(I["L1_ffn_conv_b"])
    g["oh8"] = np.eye(8, dtype=np.float32)
    g["ones"] = np.ones((128, 128), np.float32)
    g["ident"] = np.eye(128, dtype=np.float32)
    bd = np.zeros((128, 128), np.float32)
    bd[:64, :64] = 1.0
    bd[64:, 64:] = 1.0
    g["bd64"] = bd
    sel = np.zeros((16, 8 * 128), np.float32)
    for c in range(8):
        for p in range(128):
            sel[2 * c + p // 64, c * 128 + p] = 1.0
    g["sel"] = sel
    cm = np.ones((128, 16), np.float32)
    cm[:, 0] = 0.0
    g["cm128"] = cm
    g["icm128"] = np.ascontiguousarray(1.0 - cm)

    x = np.asarray(I["L0_attn_x"])
    ins = []
    for c in range(NCORES):
        b, q = c // 4, c % 4
        d = dict(g)
        d["x"] = np.ascontiguousarray(x[b, q * T:(q + 1) * T, :].T)
        bm = np.zeros((128, 2), np.float32)
        bm[:, b] = 1.0
        d["bmask"] = bm
        me = np.zeros((128, 8), np.float32)
        for j in range(4 * b, c):
            me[:, j] = 1.0
        d["mexp"] = me
        d["imexp"] = np.ascontiguousarray(1.0 - me)
        mL = np.zeros((128, 8), np.float32)
        mR = np.zeros((128, 8), np.float32)
        if q > 0:
            mL[:, c - 1] = 1.0
        if q < 3:
            mR[:, c + 1] = 1.0
        d["mhL"] = mL
        d["mhR"] = mR
        ins.append(d)
    return ins


ENGS = ("sync", "tensor", "vector", "scalar", "gpsimd")


class Sem:
    __slots__ = ("h", "n")

    def __init__(self, h):
        self.h = h
        self.n = 0


class KB:
    def __init__(self):
        self.nc = bass.Bass("TRN2", target_bir_lowering=False)
        self.stack = ExitStack()
        self.streams = {e: [] for e in ENGS}

    def E(self, cm):
        return self.stack.enter_context(cm)

    def sem(self, name):
        return Sem(self.E(self.nc.semaphore(name)))

    def sbuf(self, name, shape, dtype=F32):
        return self.E(self.nc.sbuf_tensor(name, list(shape), dtype))

    def psum(self, name, shape, dtype=F32):
        return self.E(self.nc.psum_tensor(name, list(shape), dtype))

    def op(self, eng, build, inc=None, k=1):
        if eng in ("vector", "scalar"):
            self.streams[eng].append(lambda e, b=build: b(e))
            if inc is None:
                self.streams[eng].append(lambda e: e.drain())
            else:
                inc.n += k
                self.streams[eng].append(
                    lambda e, h=inc.h, kk=k: e.drain().then_inc(h, kk))
        elif inc is None:
            self.streams[eng].append(lambda e, b=build: b(e))
        else:
            inc.n += k
            self.streams[eng].append(
                lambda e, b=build, h=inc.h, kk=k: b(e).then_inc(h, kk))

    def wait(self, eng, sem, val=None):
        v = sem.n if val is None else val
        if v > 0:
            self.streams[eng].append(lambda e, h=sem.h, vv=v: e.wait_ge(h, vv))

    def wait_pair(self, eng, pair):
        if pair is not None:
            self.wait(eng, pair[0], pair[1])

    def finish(self, block):
        for engname in ENGS:
            stream = self.streams[engname]
            getattr(block, engname)(lambda e, s=stream: [fn(e) for fn in s])


def build_program():
    K = KB()
    nc = K.nc
    dr = {}

    def din(name, shape, dtype=F32):
        dr[name] = nc.dram_tensor(name, list(shape), dtype, kind="ExternalInput").ap()
        return dr[name]

    din("x", [D, T])
    for nm in ("Wq", "Wk", "Wv", "Wo"):
        din("l0_" + nm, [8, 128, 8, 128])
    din("l0_W1", [32, 128, 8, 128])
    din("l0_W2", [8, 128, 32, 128])
    din("l0_WgT", [D, 8])
    din("moe_w1", [8, 16, 128, 8, 128])
    din("moe_w2", [8, 8, 128, 8, 128])
    din("l1_Win", [8, 128, 8, 128])
    din("l1_Wout", [8, 128, 8, 128])
    din("l1_WgateT", [D, 32])
    din("f1_w1T", [32, 128, 8, 128])
    din("f1_w2T", [8, 128, 32, 128])
    for nm in ("l0_bq", "l0_bk", "l0_bv", "l0_bo", "l0_g1", "l0_be1", "l0_b2",
               "l0_g2", "l0_be2", "l1_lng", "l1_lnb"):
        din(nm, [128, 8])
    din("l0_b1", [128, 32])
    din("l0_bg", [8, 1])
    din("l1_bga", [16, 1])
    din("l1_bgb", [16, 1])
    din("l1_eigraw", [16, 1])
    for t in range(3):
        din(f"f1_cw{t}", [128, 32])
    din("f1_cb", [128, 32])
    din("oh8", [8, 8])
    din("ones", [128, 128])
    din("ident", [128, 128])
    din("bd64", [128, 128])
    din("sel", [16, 1024])
    din("cm128", [128, 16])
    din("icm128", [128, 16])
    din("bmask", [128, 2])
    din("mexp", [128, 8])
    din("imexp", [128, 8])
    din("mhL", [128, 8])
    din("mhR", [128, 8])
    out = nc.dram_tensor("o", [D, T], F32, kind="ExternalOutput").ap()

    kv_in = nc.dram_tensor("kv_in", [128, 2048], F32).ap()
    kv_out = nc.dram_tensor("kv_out", [128, 2048], F32).ap()
    cg_in = nc.dram_tensor("cg_in", [128, 16], F32).ap()
    cg_out = nc.dram_tensor("cg_out", [8 * 128, 16], F32).ap()
    hl_in = [nc.dram_tensor(f"hl_in{i}", [128, 18], F32).ap() for i in range(2)]
    hl_out = [nc.dram_tensor(f"hl_out{i}", [8 * 128, 18], F32).ap() for i in range(2)]
    w1q_d = nc.dram_tensor("w1q_d", [32, 128, 8, 128], BF16).ap()
    w2q_d = nc.dram_tensor("w2q_d", [8, 128, 32, 128], BF16).ap()

    X = [K.sbuf(f"x{i}", [128, T]) for i in range(DC)]
    LN = [K.sbuf(f"ln{i}", [128, T]) for i in range(DC)]
    S = [K.sbuf(f"s{i}", [128, T]) for i in range(4)]
    WR = K.sbuf("wring", [128, 4096])
    SM = K.sbuf("smalls", [128, 1920])
    CONST = K.sbuf("const", [128, 416])
    COLS = K.sbuf("cols", [128, 296])
    H16 = K.sbuf("h16", [16, 1024])
    STIX = K.sbuf("stix", [128, 2048])
    ST2X = K.sbuf("st2x", [128, 2048])
    XTRA = K.sbuf("xtra", [128, 64])
    XQHB = K.sbuf("xqhb", [128, 16], BF16)
    PS = [K.psum(f"ps{i}", [128, 512]) for i in range(8)]

    ones = CONST[:, 0:128]
    ident = CONST[:, 128:256]
    bd64 = CONST[:, 256:384]
    cm128 = CONST[:, 384:400]
    icm128 = CONST[:, 400:416]
    col = {}

    sW = K.sem("sW")
    sG = K.sem("sG")
    sT = K.sem("sT")
    sV = K.sem("sV")
    sS = K.sem("sS")
    sCC = K.sem("sCC")
    sST = K.sem("sST")
    ALLSEMS = None

    bank_free = [None] * 8
    bank_ready = [None] * 8

    def barrier():
        for e in ENGS:
            for s in (sW, sG, sT, sV, sS, sCC) + tuple(sWR):
                K.wait(e, s)

    def mmgroup(bank, mms, out_slice=None):
        K.wait_pair("tensor", bank_free[bank])
        n = len(mms)
        for i, item in enumerate(mms):
            lhsT, rhs, outap = item
            K.op("tensor",
                 lambda e, o=outap, l=lhsT, r=rhs, st=(i == 0), sp=(i == n - 1):
                 e.matmul(o, l, r, start=st, stop=sp),
                 inc=(sT if i == n - 1 else None))
        bank_ready[bank] = (sT, sT.n)

    def consume(eng, bank):
        K.wait_pair(eng, bank_ready[bank])

    def release(bank, sem):
        bank_free[bank] = (sem, sem.n)

    def vop(build):
        K.op("vector", build, inc=sV)

    def sop(build):
        K.op("scalar", build, inc=sS)

    wr_free = [None] * 4
    wr_ready = [None] * 4
    sWR = [K.sem(f"sWR{i}") for i in range(4)]

    def wr_load(slot, src3d):
        K.wait_pair("sync", wr_free[slot])
        src = src3d.rearrange("p a q -> p (a q)") if src3d.ndim == 3 else src3d
        if src.dtype == BF16:
            nel = src.shape[-1]
            dst = WR[:, slot * 1024:slot * 1024 + nel // 2].bitcast(BF16)
        else:
            dst = WR[:, slot * 1024:slot * 1024 + src.shape[-1]]
        K.op("sync", lambda e, d=dst, s=src: e.dma_start(out=d, in_=s),
             inc=sWR[slot], k=16)
        wr_ready[slot] = (sWR[slot], sWR[slot].n)

    def wrb(slot, nel):
        return WR[:, slot * 1024:slot * 1024 + nel // 2].bitcast(BF16)

    def wr_wait(eng, *slots):
        for s in slots:
            K.wait_pair(eng, wr_ready[s])

    def wr_tile(slot, k):
        return WR[:, slot * 1024 + k * 128: slot * 1024 + (k + 1) * 128]

    def wr_consumed(slot):
        wr_free[slot] = (sT, sT.n)

    def load_consts():
        def ld(dst, src):
            K.op("sync", lambda e, d=dst, s=src: e.dma_start(out=d, in_=s),
                 inc=sW, k=16)
        ld(ones, dr["ones"])
        ld(ident, dr["ident"])
        ld(bd64, dr["bd64"])
        ld(cm128, dr["cm128"])
        ld(icm128, dr["icm128"])
        names = [("l0_bq", 8), ("l0_bk", 8), ("l0_bv", 8), ("l0_bo", 8),
                 ("l0_g1", 8), ("l0_be1", 8), ("l0_b2", 8), ("l0_g2", 8),
                 ("l0_be2", 8), ("l1_lng", 8), ("l1_lnb", 8), ("l0_b1", 32),
                 ("f1_cw0", 32), ("f1_cw1", 32), ("f1_cw2", 32), ("f1_cb", 32),
                 ("bmask", 2), ("mexp", 8), ("imexp", 8), ("mhL", 8), ("mhR", 8)]
        off = 0
        for nm, w in names:
            col[nm] = COLS[:, off:off + w]
            ld(col[nm], dr[nm])
            off += w
        col["oh8"] = COLS[0:8, off:off + 8]
        ld(col["oh8"], dr["oh8"])
        off += 8
        col["l0_bg"] = COLS[0:8, off:off + 1]
        ld(col["l0_bg"], dr["l0_bg"])
        col["l1_bga"] = COLS[0:16, off + 1:off + 2]
        ld(col["l1_bga"], dr["l1_bga"])
        col["l1_bgb"] = COLS[0:16, off + 2:off + 3]
        ld(col["l1_bgb"], dr["l1_bgb"])
        col["l1_eigraw"] = COLS[0:16, off + 3:off + 4]
        ld(col["l1_eigraw"], dr["l1_eigraw"])
        off += 4
        assert off <= 296, off
        col["sel"] = H16[0:16, 0:1024]
        ld(col["sel"], dr["sel"])
        for i in range(DC):
            K.op("sync", lambda e, d=X[i][:, :], s=dr["x"][i * 128:(i + 1) * 128, :]:
                 e.dma_start(out=d, in_=s), inc=sW, k=16)

    def layernorm(src, dst, eps, g_col=None, b_col=None, sq2=None):
        sqt = [sq2[0][:, 0:512], sq2[0][:, 512:1024], sq2[0][:, 1024:1536],
               sq2[0][:, 1536:2048], sq2[1][:, 0:512], sq2[1][:, 512:1024],
               sq2[1][:, 1024:1536], sq2[1][:, 1536:2048]]
        for tt in range(NTT):
            sl = slice(tt * TT, (tt + 1) * TT)
            b0, b1 = (tt % 2) * 2, (tt % 2) * 2 + 1
            K.wait("tensor", sV)
            mmgroup(b0, [(ones, src[k][:, sl], PS[b0][:]) for k in range(DC)])
            consume("vector", b0)
            for k in range(DC):
                vop(lambda e, k=k, sl=sl, b0=b0: e.scalar_tensor_tensor(
                    out=dst[k][:, sl], in0=PS[b0][:], scalar=-1.0 / 1024.0,
                    in1=src[k][:, sl], op0=ALU.mult, op1=ALU.add))
            release(b0, sV)
            K.wait("scalar", sV)
            K.wait_pair("tensor", bank_free[b1])
            for k in range(DC):
                sop(lambda e, k=k, sl=sl: e.activation(
                    out=sqt[k], in_=dst[k][:, sl], func=AF.Square))
                K.wait("tensor", sS)
                K.op("tensor", lambda e, k=k, b1=b1: e.matmul(
                    PS[b1][:], ones, sqt[k], start=(k == 0), stop=(k == DC - 1)),
                    inc=(sT if k == DC - 1 else None))
            bank_ready[b1] = (sT, sT.n)
            consume("scalar", b1)
            K.wait("scalar", sV)
            sop(lambda e, b1=b1: e.activation(
                out=SM[:, 0:TT], in_=PS[b1][:], func=AF.Ln,
                scale=f32(1.0 / 1024.0), bias=f32(eps)))
            release(b1, sS)
            sop(lambda e: e.activation(out=SM[:, TT:2 * TT], in_=SM[:, 0:TT],
                                       func=AF.Exp, scale=f32(-0.5)))
            K.wait("vector", sS)
            for k in range(DC):
                vop(lambda e, k=k, sl=sl: e.tensor_mul(
                    out=dst[k][:, sl], in0=dst[k][:, sl], in1=SM[:, TT:2 * TT]))
                if g_col is not None:
                    vop(lambda e, k=k, sl=sl: e.tensor_scalar(
                        out=dst[k][:, sl], in0=dst[k][:, sl],
                        scalar1=g_col[:, k:k + 1], scalar2=b_col[:, k:k + 1],
                        op0=ALU.mult, op1=ALU.add))

    def moe_block():
        layernorm(X, LN, 1e-5, sq2=(S[0], S[1]))
        barrier()

        wgt = dr["l0_WgT"].rearrange("(a p) m -> p a m", p=128)
        K.wait_pair("sync", wr_free[0])
        K.op("sync", lambda e: e.dma_start(
            out=WR[:, 0:64].rearrange("p (a m) -> p a m", m=8), in_=wgt),
            inc=sWR[0], k=16)
        wr_ready[0] = (sWR[0], sWR[0].n)
        wr_wait("tensor", 0)
        LG = ST2X[0:8, 0:2048]
        for tt in range(NTT):
            b = 4 + tt % 4
            mmgroup(b, [(WR[:, k * 8:(k + 1) * 8],
                         LN[k][:, tt * TT:(tt + 1) * TT], PS[b][0:8, :])
                        for k in range(DC)])
            consume("vector", b)
            vop(lambda e, tt=tt, b=b: e.tensor_scalar(
                out=LG[:, tt * TT:(tt + 1) * TT], in0=PS[b][0:8, :],
                scalar1=col["l0_bg"], scalar2=None, op0=ALU.add))
            release(b, sV)
        wr_free[0] = (sV, sV.n)

        LGT = S[1][:, 0:128]
        for j in range(16):
            bb = j % 2
            K.wait_pair("tensor", bank_free[bb])
            K.wait("tensor", sV)
            K.op("tensor", lambda e, j=j, bb=bb: e.transpose(
                PS[bb][:, 0:8], LG[:, j * 128:(j + 1) * 128], ident[0:8, 0:8]),
                inc=sT)
            bank_ready[bb] = (sT, sT.n)
            consume("vector", bb)
            vop(lambda e, j=j, bb=bb: e.tensor_copy(
                out=LGT[:, j * 8:(j + 1) * 8], in_=PS[bb][:, 0:8]))
            release(bb, sV)

        def b16(ap):
            return ap.unsqueeze(2).broadcast_to((128, 16, 8))
        v3 = LGT.rearrange("p (j e) -> p j e", e=8)
        MX = SM[:, 1664:1680]
        SUM2 = SM[:, 1680:1696]
        M1 = SM[:, 1696:1712]
        M2 = SM[:, 1712:1728]
        EX = S[1][:, 128:256]
        ex3 = EX.rearrange("p (j e) -> p j e", e=8)
        TMP = S[1][:, 256:384]
        PRE = S[1][:, 384:512]
        pre3 = PRE.rearrange("p (j e) -> p j e", e=8)
        def rtree(out2d, in3, op):
            t8 = SM[:, 1760:1824].rearrange("p (j e) -> p j e", e=4)
            t4 = SM[:, 1824:1856].rearrange("p (j e) -> p j e", e=2)
            vop(lambda e: e.tensor_tensor(out=t8, in0=in3[:, :, 0:4],
                                          in1=in3[:, :, 4:8], op=op))
            vop(lambda e: e.tensor_tensor(out=t4, in0=t8[:, :, 0:2],
                                          in1=t8[:, :, 2:4], op=op))
            vop(lambda e: e.tensor_tensor(out=out2d.unsqueeze(2), in0=t4[:, :, 0:1],
                                          in1=t4[:, :, 1:2], op=op))
        cstep = [0]
        def cgate():
            cstep[0] += 1
            return cstep[0] <= CSTEP
        if cgate(): rtree(MX, v3, ALU.max)
        if cgate(): vop(lambda e: e.tensor_tensor(out=pre3, in0=v3, in1=b16(MX),
                                      op=ALU.subtract))
        if cgate():
            K.wait("scalar", sV)
            sop(lambda e: e.activation(out=EX, in_=PRE, func=AF.Exp))
            K.wait("vector", sS)
        if cgate(): rtree(SUM2, ex3, ALU.add)
        if cgate():
            K.wait("scalar", sV)
            sop(lambda e: e.activation(out=SM[:, 1728:1744], in_=SUM2, func=AF.Ln))
            sop(lambda e: e.activation(out=SUM2, in_=SM[:, 1728:1744], func=AF.Exp,
                                   scale=f32(-1.0)))
            K.wait("vector", sS)
        if cgate(): vop(lambda e: e.tensor_tensor(out=ex3, in0=ex3, in1=b16(SUM2), op=ALU.mult))
        if cgate(): rtree(M1, ex3, ALU.max)
        tmp3 = TMP.rearrange("p (j e) -> p j e", e=8)
        if cgate(): vop(lambda e: e.tensor_tensor(out=tmp3, in0=ex3, in1=b16(M1), op=ALU.is_ge))
        if cgate(): vop(lambda e: e.scalar_tensor_tensor(out=TMP, in0=TMP, scalar=-2.0,
                                             in1=EX, op0=ALU.mult, op1=ALU.add))
        if cgate(): rtree(M2, tmp3, ALU.max)
        M3 = SM[:, 1744:1760]
        if cgate(): vop(lambda e: e.tensor_add(out=M3, in0=M1, in1=M2))
        if cgate():
            K.wait("scalar", sV)
            sop(lambda e: e.activation(out=SM[:, 1728:1744], in_=M3, func=AF.Ln))
            sop(lambda e: e.activation(out=M1, in_=SM[:, 1728:1744], func=AF.Exp,
                                   scale=f32(-1.0)))
            K.wait("vector", sS)
        if cgate(): vop(lambda e: e.tensor_scalar_mul(out=M1, in0=M1, scalar1=0.5))
        if cgate(): vop(lambda e: e.tensor_tensor(out=tmp3, in0=ex3, in1=b16(M2), op=ALU.is_ge))
        if cgate(): vop(lambda e: e.tensor_mul(out=TMP, in0=TMP, in1=EX))
        if cgate(): vop(lambda e: e.tensor_tensor(out=tmp3, in0=tmp3, in1=b16(M1), op=ALU.mult))
        CB = STIX[0:8, 0:2048]
        for j in range(16):
            bb = j % 2
            K.wait_pair("tensor", bank_free[bb])
            K.wait("tensor", sV)
            K.op("tensor", lambda e, j=j, bb=bb: e.transpose(
                PS[bb][0:8, 0:128], TMP[:, j * 8:(j + 1) * 8], ident), inc=sT)
            bank_ready[bb] = (sT, sT.n)
            consume("vector", bb)
            vop(lambda e, j=j, bb=bb: e.tensor_copy(
                out=CB[:, j * 128:(j + 1) * 128], in_=PS[bb][0:8, 0:128]))
            release(bb, sV)
        barrier()

        SG = S[1][:, 0:512]
        AE = [S[2][:, 0:512], S[2][:, 512:1024], S[2][:, 1024:1536],
              S[2][:, 1536:2048], S[3][:, 0:512], S[3][:, 512:1024],
              S[3][:, 1024:1536], S[3][:, 1536:2048]]
        CBX = SM[:, 1024:1536]
        W2SC = SM[:, 512:1024]

        IND8 = SM[0:8, 1536:1664]
        for e_idx in range(8 if not SKIP_EXPERTS else 0):
            K.wait("vector", sT)
            vop(lambda e, e_idx=e_idx: e.tensor_scalar_mul(
                out=IND8, in0=ones[0:8, 0:128],
                scalar1=col["oh8"][0:8, e_idx:e_idx + 1]))
            for tt in range(NTT):
                sl = slice(tt * TT, (tt + 1) * TT)
                K.wait("vector", sT)
                K.wait_pair("tensor", bank_free[7])
                K.wait("tensor", sV)
                K.op("tensor", lambda e, e_idx=e_idx, sl=sl: e.matmul(
                    PS[7][:], IND8, CB[0:8, sl],
                    start=True, stop=True), inc=sT)
                bank_ready[7] = (sT, sT.n)
                consume("vector", 7)
                vop(lambda e: e.tensor_copy(out=CBX, in_=PS[7][:]))
                release(7, sV)
                for mp in range(8):
                    slot_g, slot_u = (mp % 2) * 2, (mp % 2) * 2 + 1
                    wr_load(slot_g, dr["moe_w1"][e_idx, mp])
                    wr_load(slot_u, dr["moe_w1"][e_idx, mp + 8])
                    wr_wait("tensor", slot_g, slot_u)
                    bg_, bu_ = (mp % 2) * 2, (mp % 2) * 2 + 1
                    mmgroup(bg_, [(wr_tile(slot_g, k), LN[k][:, sl], PS[bg_][:])
                                  for k in range(DC)])
                    mmgroup(bu_, [(wr_tile(slot_u, k), LN[k][:, sl], PS[bu_][:])
                                  for k in range(DC)])
                    wr_consumed(slot_g)
                    wr_consumed(slot_u)
                    consume("scalar", bg_)
                    K.wait("scalar", sV)
                    sop(lambda e, bg_=bg_: e.activation(
                        out=SG, in_=PS[bg_][:], func=AF.Silu))
                    release(bg_, sS)
                    consume("vector", bu_)
                    K.wait("vector", sS)
                    vop(lambda e, mp=mp, bu_=bu_: e.tensor_mul(
                        out=AE[mp], in0=PS[bu_][:], in1=SG))
                    release(bu_, sV)
                for m2 in range(8):
                    slot = 2 + m2 % 2
                    wr_load(slot, dr["moe_w2"][e_idx, m2])
                    wr_wait("tensor", slot)
                    K.wait("tensor", sV)
                    b = 4 + m2 % 2
                    mmgroup(b, [(wr_tile(slot, k), AE[k], PS[b][:])
                                for k in range(DC)])
                    wr_consumed(slot)
                    consume("vector", b)
                    vop(lambda e, b=b: e.tensor_mul(
                        out=W2SC, in0=PS[b][:], in1=CBX))
                    vop(lambda e, m2=m2, sl=sl: e.tensor_add(
                        out=X[m2][:, sl], in0=X[m2][:, sl], in1=W2SC))
                    release(b, sV)
        barrier()


    def ln_tile(src_c, dst_c, nch, cols, eps, banks, sqs, g_col=None, b_col=None,
                nfeat=1024.0):
        b0, b1 = banks
        K.wait("tensor", sV)
        K.wait("tensor", sS)
        mmgroup(b0, [(ones, src_c(k), PS[b0][:, 0:cols]) for k in range(nch)])
        consume("vector", b0)
        for k in range(nch):
            vop(lambda e, k=k: e.scalar_tensor_tensor(
                out=dst_c(k), in0=PS[b0][:, 0:cols], scalar=-1.0 / nfeat,
                in1=src_c(k), op0=ALU.mult, op1=ALU.add))
        release(b0, sV)
        K.wait("scalar", sV)
        K.wait_pair("tensor", bank_free[b1])
        for k in range(nch):
            sop(lambda e, k=k: e.activation(out=sqs(k), in_=dst_c(k),
                                            func=AF.Square))
            K.wait("tensor", sS)
            K.op("tensor", lambda e, k=k: e.matmul(
                PS[b1][:, 0:cols], ones, sqs(k), start=(k == 0),
                stop=(k == nch - 1)), inc=(sT if k == nch - 1 else None))
        bank_ready[b1] = (sT, sT.n)
        consume("scalar", b1)
        K.wait("scalar", sV)
        sop(lambda e: e.activation(out=SM[:, 0:cols], in_=PS[b1][:, 0:cols],
                                   func=AF.Ln, scale=f32(1.0 / nfeat),
                                   bias=f32(eps)))
        release(b1, sS)
        sop(lambda e: e.activation(out=SM[:, 512:512 + cols], in_=SM[:, 0:cols],
                                   func=AF.Exp, scale=f32(-0.5)))
        K.wait("vector", sS)
        for k in range(nch):
            vop(lambda e, k=k: e.tensor_mul(out=dst_c(k), in0=dst_c(k),
                                            in1=SM[:, 512:512 + cols]))
            if g_col is not None:
                vop(lambda e, k=k: e.tensor_scalar(
                    out=dst_c(k), in0=dst_c(k), scalar1=g_col[:, k:k + 1],
                    scalar2=b_col[:, k:k + 1], op0=ALU.mult, op1=ALU.add))

    def l0_attn_block():
        layernorm(X, LN, 1e-5, sq2=(S[0], S[1]))
        barrier()
        smA = SM[:, 1024:1536]
        smB = SM[:, 1536:1920]
        smB = ST2X[:, 0:512]
        smC = ST2X, None

        def elu_norm(dst_c, bias_col, wcol_src, sl):
            for m in range(DC):
                slot = m % 2
                wr_load(slot, wcol_src(m))
                wr_wait("tensor", slot)
                b = m % 2
                mmgroup(b, [(wr_tile(slot, k), LN[k][:, sl], PS[b][:])
                            for k in range(DC)])
                wr_consumed(slot)
                consume("vector", b)
                vop(lambda e, m=m, b=b: e.tensor_scalar(
                    out=dst_c(m), in0=PS[b][:], scalar1=bias_col[:, m:m + 1],
                    scalar2=None, op0=ALU.add))
                release(b, sV)
            smC = ST2X[:, 512:1024]
            for m in range(DC):
                K.wait("vector", sS)
                vop(lambda e, m=m: e.tensor_single_scalar(
                    out=smA, in_=dst_c(m), scalar=0.0, op=ALU.min))
                vop(lambda e, m=m: e.tensor_single_scalar(
                    out=smC, in_=dst_c(m), scalar=0.0, op=ALU.max))
                K.wait("scalar", sV)
                sop(lambda e: e.activation(out=smB, in_=smA, func=AF.Exp))
                K.wait("vector", sS)
                vop(lambda e, m=m: e.scalar_tensor_tensor(
                    out=dst_c(m), in0=smB, scalar=-1.0, in1=smC,
                    op0=ALU.add, op1=ALU.add))
                K.wait("scalar", sV)
                sop(lambda e, m=m: e.activation(out=smB, in_=dst_c(m),
                                                func=AF.Square))
                b = 2 + m % 2
                K.wait_pair("tensor", bank_free[b])
                K.wait("tensor", sS)
                K.op("tensor", lambda e, b=b: e.matmul(
                    PS[b][:], bd64, smB, start=True, stop=True), inc=sT)
                bank_ready[b] = (sT, sT.n)
                consume("scalar", b)
                K.wait("scalar", sT)
                sop(lambda e, b=b: e.activation(out=smA, in_=PS[b][:],
                                                func=AF.Ln))
                release(b, sS)
                sop(lambda e: e.activation(out=smC, in_=smA, func=AF.Exp,
                                           scale=f32(-0.5)))
                K.wait("vector", sS)
                vop(lambda e, m=m: e.tensor_mul(out=dst_c(m), in0=dst_c(m),
                                                in1=smC))

        def S4(i, c, sl0=0):
            t = S[i + c // 4]
            off = (c % 4) * 512
            return t[:, off:off + 512]

        KTT = STIX[:, 0:1024]
        VTT = STIX[:, 1024:2048]
        kv_started = [False]
        for tt in range(NTT):
            sl = slice(tt * TT, (tt + 1) * TT)
            elu_norm(lambda c: S4(0, c), col["l0_bk"],
                     lambda m: dr["l0_Wk"][m], sl)
            for m in range(DC):
                slot = m % 2
                wr_load(slot, dr["l0_Wv"][m])
                wr_wait("tensor", slot)
                b = m % 2
                mmgroup(b, [(wr_tile(slot, k), LN[k][:, sl], PS[b][:])
                            for k in range(DC)])
                wr_consumed(slot)
                consume("vector", b)
                vop(lambda e, m=m, b=b: e.tensor_scalar(
                    out=S4(2, m), in0=PS[b][:], scalar1=col["l0_bv"][:, m:m + 1],
                    scalar2=None, op0=ALU.add))
                release(b, sV)
            for s in range(4):
                for c in range(DC):
                    for which, srcv, dstv in ((0, S4(0, c), KTT), (1, S4(2, c), VTT)):
                        b = 2 + (c + which) % 2
                        K.wait_pair("tensor", bank_free[b])
                        K.wait("tensor", sV)
                        K.op("tensor", lambda e, b=b, srcv=srcv, s=s: e.transpose(
                            PS[b][:, 0:128], srcv[:, s * 128:(s + 1) * 128],
                            ident), inc=sT)
                        bank_ready[b] = (sT, sT.n)
                        consume("vector", b)
                        vop(lambda e, b=b, dstv=dstv, c=c: e.tensor_copy(
                            out=dstv[:, c * 128:(c + 1) * 128],
                            in_=PS[b][:, 0:128]))
                        release(b, sV)
                K.wait("tensor", sV)
                first = not kv_started[0]
                kv_started[0] = True
                last = (tt == NTT - 1 and s == 3)
                for c in range(DC):
                    bank = 6 + c // 4
                    fp = (c % 4) * 128
                    stt_ = first and (c % 4 == 0)
                    K.op("tensor", lambda e, c=c, bank=bank, fp=fp, stt_=stt_,
                         last=last: e.matmul(
                             PS[bank][:, fp:fp + 128],
                             KTT[:, c * 128:(c + 1) * 128],
                             VTT[:, c * 128:(c + 1) * 128],
                             start=stt_, stop=(last and c % 4 == 3),
                             skip_group_check=True),
                         inc=(sT if (last and c == DC - 1) else None))
        bank_ready[6] = (sT, sT.n)
        bank_ready[7] = (sT, sT.n)

        BDS = STIX[:, 0:1024]
        consume("vector", 6)
        consume("vector", 7)
        vop(lambda e: e.memset(BDS, 0.0))
        for c in range(DC):
            bank = 6 + c // 4
            base = (c % 4) * 128
            vop(lambda e, bank=bank, base=base, c=c: e.tensor_copy(
                out=BDS[0:64, c * 128:c * 128 + 64],
                in_=PS[bank][0:64, base:base + 64]))
            vop(lambda e, bank=bank, base=base, c=c: e.tensor_copy(
                out=BDS[64:128, c * 128 + 64:c * 128 + 128],
                in_=PS[bank][64:128, base + 64:base + 128]))
        release(6, sV)
        release(7, sV)
        KV2 = ST2X[:, 0:2048]
        vop(lambda e: e.tensor_scalar(out=KV2[:, 0:1024], in0=BDS,
                                      scalar1=col["bmask"][:, 0:1], scalar2=None,
                                      op0=ALU.mult))
        vop(lambda e: e.tensor_scalar(out=KV2[:, 1024:2048], in0=BDS,
                                      scalar1=col["bmask"][:, 1:2], scalar2=None,
                                      op0=ALU.mult))
        K.wait("gpsimd", sV)
        K.op("gpsimd", lambda e: e.dma_start(out=kv_in, in_=KV2), inc=sG, k=16)
        K.wait("gpsimd", sG)
        K.op("gpsimd", lambda e: e.collective_compute(
            "AllReduce", ALU.add, replica_groups=[list(range(8))],
            ins=[kv_in.opt()], outs=[kv_out.opt()]), inc=sCC)
        K.wait("gpsimd", sCC)
        K.op("gpsimd", lambda e: e.dma_start(out=KV2, in_=kv_out), inc=sG, k=16)
        barrier()
        vop(lambda e: e.tensor_scalar(out=BDS, in0=KV2[:, 0:1024],
                                      scalar1=col["bmask"][:, 0:1], scalar2=None,
                                      op0=ALU.mult))
        vop(lambda e: e.scalar_tensor_tensor(out=BDS, in0=KV2[:, 1024:2048],
                                             scalar=col["bmask"][:, 1:2],
                                             in1=BDS, op0=ALU.mult, op1=ALU.add))
        barrier()

        if ASTEP <= 1:
            return
        for tt in range(NTT):
            sl = slice(tt * TT, (tt + 1) * TT)
            elu_norm(lambda c: S4(0, c), col["l0_bq"],
                     lambda m: dr["l0_Wq"][m], sl)
            for c in range(DC):
                b = 4 + c % 2
                K.wait_pair("tensor", bank_free[b])
                K.wait("tensor", sV)
                K.op("tensor", lambda e, c=c, b=b: e.matmul(
                    PS[b][:], BDS[:, c * 128:(c + 1) * 128], S4(0, c),
                    start=True, stop=True), inc=sT)
                bank_ready[b] = (sT, sT.n)
                consume("vector", b)
                vop(lambda e, c=c, b=b: e.tensor_scalar(
                    out=S4(2, c), in0=PS[b][:], scalar1=0.125, scalar2=None,
                    op0=ALU.mult))
                release(b, sV)
            if ASTEP <= 2:
                return
            for m in range(DC):
                slot = m % 2
                wr_load(slot, dr["l0_Wo"][m])
                wr_wait("tensor", slot)
                K.wait("tensor", sV)
                b = m % 2
                mmgroup(b, [(wr_tile(slot, k), S4(2, k), PS[b][:])
                            for k in range(DC)])
                wr_consumed(slot)
                consume("vector", b)
                vop(lambda e, m=m, b=b, sl=sl: e.scalar_tensor_tensor(
                    out=S4(0, m), in0=PS[b][:], scalar=col["l0_bo"][:, m:m + 1],
                    in1=LN[m][:, sl], op0=ALU.add, op1=ALU.add))
                release(b, sV)
            if ASTEP <= 3:
                return
            ln_tile(lambda c: S4(0, c), lambda c, sl=sl: LN[c][:, sl], DC, TT,
                    1e-12, (2, 3), lambda c: S4(2, c), col["l0_g1"],
                    col["l0_be1"])
            barrier()
            if ASTEP <= 4:
                return
            for half in range(2):
                cols = 256
                c0 = tt * TT + half * 256

                def AO2(c, c0=c0):
                    return LN[c][:, c0:c0 + cols]

                def H1(m):
                    t = S[2 + m // 8]
                    off = (m % 8) * 256
                    return t[:, off:off + 256]

                K.wait("scalar", sV)
                for g in range(2):
                    for mi in range(16):
                        m = g * 16 + mi
                        slot = mi % 2
                        wr_load(slot, dr["l0_W1"][m])
                        wr_wait("tensor", slot)
                        b = mi % 2
                        K.wait_pair("tensor", bank_free[b])
                        for k in range(DC):
                            K.op("tensor", lambda e, k=k, b=b, slot=slot,
                                 AO2=AO2: e.matmul(
                                PS[b][:, 0:cols], wr_tile(slot, k), AO2(k),
                                start=(k == 0), stop=(k == DC - 1)),
                                inc=(sT if k == DC - 1 else None))
                        bank_ready[b] = (sT, sT.n)
                        wr_consumed(slot)
                        consume("scalar", b)
                        sop(lambda e, mi=mi, b=b, m=m: e.activation(
                            out=H1(mi), in_=PS[b][:, 0:cols], func=AF.Gelu,
                            bias=col["l0_b1"][:, m:m + 1]))
                        release(b, sS)
                    for m2 in range(DC):
                        bank = 4 + m2 // 2
                        fof = (m2 % 2) * 256
                        if g == 0:
                            K.wait_pair("tensor", bank_free[bank])
                        for sub in range(2):
                            slot = 2 + (m2 * 2 + sub) % 2
                            wr_load(slot, dr["l0_W2"][m2]
                                    [:, g * 16 + sub * 8:g * 16 + sub * 8 + 8, :])
                            wr_wait("tensor", slot)
                            if sub == 0:
                                K.wait("tensor", sS)
                            for k2 in range(8):
                                last = (g == 1 and sub == 1 and k2 == 7)
                                K.op("tensor", lambda e, k2=k2, sub=sub, bank=bank,
                                     fof=fof, slot=slot, g=g, last=last,
                                     m2=m2: e.matmul(
                                         PS[bank][:, fof:fof + 256],
                                         wr_tile(slot, k2),
                                         H1(sub * 8 + k2),
                                         start=(g == 0 and sub == 0 and k2 == 0
                                                and m2 % 2 == 0),
                                         stop=last, skip_group_check=True),
                                    inc=(sT if k2 == 7 else None))
                            wr_consumed(slot)
                for bank in (4, 5, 6, 7):
                    bank_ready[bank] = (sT, sT.n)
                for m2 in range(DC):
                    bank = 4 + m2 // 2
                    fof = (m2 % 2) * 256
                    consume("vector", bank)
                    vop(lambda e, m2=m2, bank=bank, fof=fof, AO2=AO2:
                        e.scalar_tensor_tensor(
                        out=S[2][:, m2 * 256:(m2 + 1) * 256],
                        in0=PS[bank][:, fof:fof + 256],
                        scalar=col["l0_b2"][:, m2:m2 + 1], in1=AO2(m2),
                        op0=ALU.add, op1=ALU.add))
                for bank in (4, 5, 6, 7):
                    release(bank, sV)
                ln_tile(lambda c: S[2][:, c * 256:(c + 1) * 256],
                        lambda c: S[3][:, c * 256:(c + 1) * 256], DC, 256,
                        1e-12, (2, 3), lambda c: S[1][:, c * 256:(c + 1) * 256],
                        col["l0_g2"], col["l0_be2"])
                for m in range(DC):
                    vop(lambda e, m=m, c0=c0: e.tensor_add(
                        out=X[m][:, c0:c0 + cols], in0=X[m][:, c0:c0 + cols],
                        in1=S[3][:, m * 256:(m + 1) * 256]))
                if ASTEP >= 5 and ASTEP < 99 and tt * 2 + half + 1 >= ASTEP - 4:
                    return
            barrier()
        barrier()



    def emit_round(out, vin, iview, ef, gf):
        vop(lambda e: e.tensor_copy(out=iview, in_=vin))
        vop(lambda e: e.tensor_copy(out=out, in_=iview))
        vop(lambda e: e.tensor_tensor(out=ef, in0=vin, in1=out, op=ALU.subtract))
        vop(lambda e: e.tensor_single_scalar(out=gf, in_=ef, scalar=0.5,
                                             op=ALU.is_ge))
        vop(lambda e: e.tensor_add(out=out, in0=out, in1=gf))
        vop(lambda e: e.tensor_single_scalar(out=gf, in_=ef, scalar=-0.5,
                                             op=ALU.is_lt))
        vop(lambda e: e.tensor_tensor(out=out, in0=out, in1=gf, op=ALU.subtract))

    l1q_state = {"done": False}
    XQH = None

    def l1_quant_weights():
        for wi, (wsrc, wdst, ncol, scol) in enumerate((
                (dr["f1_w1T"], w1q_d, 32, 32),
                (dr["f1_w2T"], w2q_d, 8, 33))):
            nslot = 1024
            acc = XTRA[:, 40:41]
            K.wait("vector", sG)
            vop(lambda e, acc=acc: e.memset(acc, 0.0))
            if wi == 0:
                vop(lambda e, acc=acc: e.tensor_copy(out=XTRA[:, 48:49], in_=acc))
            cols_list = []
            for mc in range(ncol):
                sub = wsrc[mc]
                if sub.shape[1] * sub.shape[2] == 1024:
                    parts = [sub]
                else:
                    parts = [sub[:, q * 8:(q + 1) * 8, :]
                             for q in range(sub.shape[1] // 8)]
                for pt in parts:
                    cols_list.append(pt)
            for ci, pt in enumerate(cols_list):
                slot = ci % 2
                wr_load(slot, pt)
                wr_wait("vector", slot)
                vop(lambda e, slot=slot: e.tensor_reduce(
                    out=XTRA[:, 41:42], in_=WR[:, slot * 1024:(slot + 1) * 1024],
                    axis=mybir.AxisListType.X, op=ALU.add,
                    apply_absolute_value=True))
                vop(lambda e, acc=acc: e.tensor_add(out=acc, in0=acc,
                                                    in1=XTRA[:, 41:42]))
                if wi == 0 and ci == 0:
                    vop(lambda e, acc=acc: e.tensor_copy(out=XTRA[:, 49:50], in_=acc))
                if wi == 0 and ci == len(cols_list) - 1:
                    vop(lambda e, acc=acc: e.tensor_copy(out=XTRA[:, 50:51], in_=acc))
                wr_free[slot] = (sV, sV.n)
            b = 0
            K.wait_pair("tensor", bank_free[b])
            K.wait("tensor", sV)
            K.op("tensor", lambda e, b=b, acc=acc: e.matmul(
                PS[b][:, 0:1], ones, acc, start=True, stop=True), inc=sT)
            bank_ready[b] = (sT, sT.n)
            consume("vector", b)
            nelem = float(len(cols_list) * 1024 * 128)
            vop(lambda e, b=b, nelem=nelem: e.tensor_scalar(
                out=XTRA[:, 42:43], in0=PS[b][:, 0:1], scalar1=1.0 / nelem,
                scalar2=1e-5, op0=ALU.mult, op1=ALU.max))
            release(b, sV)
            swi = XTRA[:, scol:scol + 1]
            vop(lambda e, swi=swi: e.tensor_copy(out=swi, in_=XTRA[:, 42:43]))
            K.wait("scalar", sV)
            sop(lambda e: e.activation(out=XTRA[:, 43:44], in_=XTRA[:, 42:43],
                                       func=AF.Ln))
            sop(lambda e: e.activation(out=XTRA[:, 44:45], in_=XTRA[:, 43:44],
                                       func=AF.Exp, scale=f32(-1.0)))
            K.wait("vector", sS)
            if FSTEP == 0 and wi == 0:
                return
            ti32 = ST2X[:, 0:1024].bitcast(I32)
            tf = ST2X[:, 1024:2048]
            for ci, pt in enumerate(cols_list):
                slot = ci % 2
                wr_load(slot, pt)
                wr_wait("vector", slot)
                vop(lambda e, slot=slot: e.tensor_scalar(
                    out=tf, in0=WR[:, slot * 1024:(slot + 1) * 1024],
                    scalar1=XTRA[:, 44:45], scalar2=None, op0=ALU.mult))
                wr_free[slot] = (sV, sV.n)
                emit_round(ST2X[:, 0:1024], tf, SM[:, 0:1024].bitcast(I32),
                           STIX[:, 0:1024], STIX[:, 1024:2048])
                vop(lambda e: e.tensor_scalar(
                    out=tf, in0=ST2X[:, 0:1024], scalar1=-1.0, scalar2=1.0,
                    op0=ALU.max, op1=ALU.min))
                bq = WR[:, 2048:2560].bitcast(BF16)
                K.wait("vector", sG)
                vop(lambda e, bq=bq: e.tensor_copy(out=bq, in_=tf))
                dst = (wdst[ci // (1024 // 1024)] if False else None)
                K.wait("gpsimd", sV)
                if wi == 0:
                    ddst = w1q_d[ci].rearrange("p a q -> p (a q)")
                else:
                    mc2, qq = ci // 4, ci % 4
                    ddst = w2q_d[mc2][:, qq * 8:(qq + 1) * 8, :].rearrange(
                        "p a q -> p (a q)")
                K.op("gpsimd", lambda e, bq=bq, ddst=ddst: e.dma_start(
                    out=ddst, in_=bq), inc=sG, k=16)
        K.wait("vector", sG)
        barrier()

    def l1_ffn_block():
        layernorm(X, LN, 1e-5, sq2=(S[0], S[1]))
        barrier()
        if not l1q_state["done"]:
            l1_quant_weights()
            l1q_state["done"] = True
        if FSTEP <= 1:
            return
        cws = [XTRA[:, 45 + t:46 + t] for t in range(3)]
        CW = [SM[:, 1792 + 32 * t:1792 + 32 * (t + 1)] for t in range(3)]
        for t in range(3):
            vop(lambda e, t=t: e.tensor_scalar(
                out=CW[t], in0=col[f"f1_cw{t}"], scalar1=XTRA[:, 32:33],
                scalar2=None, op0=ALU.mult))
        SW2I05 = XTRA[:, 35:36]
        vop(lambda e: e.tensor_scalar(out=SW2I05, in0=XTRA[:, 33:34],
                                      scalar1=0.5, scalar2=None, op0=ALU.mult))

        AM = ST2X[:, 0:2048]
        AMS = STIX[:, 0:2048]
        K.wait("scalar", sV)
        sop(lambda e: e.activation(out=AM, in_=LN[0][:, :], func=AF.Abs))
        K.wait("vector", sS)
        for k in range(1, DC):
            K.wait("scalar", sV)
            sop(lambda e, k=k: e.activation(out=AMS, in_=LN[k][:, :],
                                            func=AF.Abs))
            K.wait("vector", sS)
            vop(lambda e: e.tensor_max(out=AM, in0=AM, in1=AMS))
        SCOL = XTRA[:, 0:16]
        for j in range(16):
            b = j % 2
            K.wait_pair("tensor", bank_free[b])
            K.wait("tensor", sV)
            K.op("tensor", lambda e, j=j, b=b: e.transpose(
                PS[b][:, 0:128], AM[:, j * 128:(j + 1) * 128], ident), inc=sT)
            bank_ready[b] = (sT, sT.n)
            consume("vector", b)
            vop(lambda e, j=j, b=b: e.tensor_reduce(
                out=SCOL[:, j:j + 1], in_=PS[b][:, 0:128],
                axis=mybir.AxisListType.X, op=ALU.max))
            release(b, sV)
        MXC = XTRA[:, 16:32]
        vop(lambda e: e.tensor_scalar_max(out=MXC, in0=SCOL, scalar1=1e-5))
        K.wait("scalar", sV)
        sop(lambda e: e.activation(out=XTRA[:, 40:56], in_=MXC, func=AF.Ln))
        sop(lambda e: e.activation(out=XTRA[:, 40:56], in_=XTRA[:, 40:56],
                                   func=AF.Exp, scale=f32(-1.0))
            if False else e.activation(out=SCOL, in_=XTRA[:, 40:56],
                                       func=AF.Exp, scale=f32(-1.0)))
        K.wait("vector", sS)
        ST = XTRA[:, 40:56]
        vop(lambda e: e.tensor_scalar(out=ST, in0=SCOL, scalar1=127.0,
                                      scalar2=None, op0=ALU.mult))
        STI = XTRA[:, 56:72] if False else SCOL
        vop(lambda e: e.tensor_scalar(out=STI, in0=MXC, scalar1=1.0 / 127.0,
                                      scalar2=None, op0=ALU.mult))
        SROW = ST2X[0:1, 0:2048]
        SIROW = STIX[0:1, 0:2048]
        for j in range(16):
            b = j % 2
            K.wait_pair("tensor", bank_free[b])
            K.wait("tensor", sV)
            K.op("tensor", lambda e, j=j, b=b: e.matmul(
                PS[b][0:1, 0:128], ST[:, j:j + 1], ident, start=True,
                stop=True), inc=sT)
            bank_ready[b] = (sT, sT.n)
            consume("vector", b)
            vop(lambda e, j=j, b=b: e.tensor_copy(
                out=SROW[:, j * 128:(j + 1) * 128], in_=PS[b][0:1, 0:128]))
            release(b, sV)
            b2 = 2 + j % 2
            K.wait_pair("tensor", bank_free[b2])
            K.wait("tensor", sV)
            K.op("tensor", lambda e, j=j, b2=b2: e.matmul(
                PS[b2][0:1, 0:128], STI[:, j:j + 1], ident, start=True,
                stop=True), inc=sT)
            bank_ready[b2] = (sT, sT.n)
            consume("vector", b2)
            vop(lambda e, j=j, b2=b2: e.tensor_copy(
                out=SIROW[:, j * 128:(j + 1) * 128], in_=PS[b2][0:1, 0:128]))
            release(b2, sV)
        for ttb in range(NTT):
            b = 4 + ttb % 2
            K.wait_pair("tensor", bank_free[b])
            K.wait("tensor", sV)
            K.op("tensor", lambda e, ttb=ttb, b=b: e.matmul(
                PS[b][:], ones[0:1, :], SIROW[:, ttb * TT:(ttb + 1) * TT],
                start=True, stop=True), inc=sT)
            bank_ready[b] = (sT, sT.n)
            consume("vector", b)
            vop(lambda e, ttb=ttb, b=b: e.tensor_copy(
                out=STIX[:, ttb * TT:(ttb + 1) * TT], in_=PS[b][:]))
            release(b, sV)
        def XQv(k):
            t = S[k // 2].bitcast(BF16)
            off = (k % 2) * 2048
            return t[:, off:off + 2048]
        QSF = SM[:, 0:256]
        QSI = SM[:, 256:512].bitcast(I32)[:, 0:256]
        QRO = SM[:, 512:768]
        QRE = SM[:, 768:1024]
        QRG = SM[:, 1024:1280]
        for ttb in range(NTT):
            b = 6 + ttb % 2
            K.wait_pair("tensor", bank_free[b])
            K.wait("tensor", sV)
            K.op("tensor", lambda e, ttb=ttb, b=b: e.matmul(
                PS[b][:], ones[0:1, :], SROW[:, ttb * TT:(ttb + 1) * TT],
                start=True, stop=True), inc=sT)
            bank_ready[b] = (sT, sT.n)
            consume("vector", b)
            for k in range(DC):
                for hh in range(2):
                    sl = slice(ttb * TT + hh * 256, ttb * TT + hh * 256 + 256)
                    psl = slice(hh * 256, hh * 256 + 256)
                    vop(lambda e, k=k, sl=sl, b=b, psl=psl: e.tensor_tensor(
                        out=QSF, in0=LN[k][:, sl], in1=PS[b][:, psl],
                        op=ALU.mult))
                    emit_round(QRO, QSF, QSI, QRE, QRG)
                    vop(lambda e, k=k, sl=sl: e.tensor_copy(
                        out=XQv(k)[:, sl], in_=QRO))
            release(b, sV)
        if FSTEP <= 2:
            return
        hcall = 0 if not l1q_state.get("second") else 1
        l1q_state["second"] = True
        CONTRIB = SM[:, 1024:1042]
        for k in range(DC):
            vop(lambda e, k=k: e.tensor_copy(out=CONTRIB[:, k:k + 1],
                                             in_=XQv(k)[:, 0:1]))
            vop(lambda e, k=k: e.tensor_copy(out=CONTRIB[:, 8 + k:9 + k],
                                             in_=XQv(k)[:, 2047:2048]))
        vop(lambda e: e.tensor_copy(out=CONTRIB[:, 16:17], in_=STIX[:, 0:1]))
        vop(lambda e: e.tensor_copy(out=CONTRIB[:, 17:18],
                                    in_=STIX[:, 2047:2048]))
        K.wait("gpsimd", sV)
        K.op("gpsimd", lambda e, h=hcall: e.dma_start(out=hl_in[h], in_=CONTRIB),
             inc=sG, k=16)
        K.wait("gpsimd", sG)
        K.op("gpsimd", lambda e, h=hcall: e.collective_compute(
            "AllGather", ALU.bypass, replica_groups=[list(range(8))],
            ins=[hl_in[h].opt()], outs=[hl_out[h].opt()]), inc=sCC)
        K.wait("gpsimd", sCC)
        GH = SM[:, 0:1152]
        for j in range(8):
            K.op("gpsimd", lambda e, j=j, h=hcall: e.dma_start(
                out=GH[:, j * 144:j * 144 + 18],
                in_=hl_out[h][j * 128:(j + 1) * 128, :]), inc=sG, k=16)
        barrier()
        HL = SM[:, 1200:1218]
        HR = SM[:, 1218:1236]
        vop(lambda e: e.memset(HL, 0.0))
        vop(lambda e: e.memset(HR, 0.0))
        for j in range(8):
            vop(lambda e, j=j: e.scalar_tensor_tensor(
                out=HL, in0=GH[:, j * 144:j * 144 + 18],
                scalar=col["mhL"][:, j:j + 1], in1=HL, op0=ALU.mult,
                op1=ALU.add))
            vop(lambda e, j=j: e.scalar_tensor_tensor(
                out=HR, in0=GH[:, j * 144:j * 144 + 18],
                scalar=col["mhR"][:, j:j + 1], in1=HR, op0=ALU.mult,
                op1=ALU.add))
        for k in range(DC):
            vop(lambda e, k=k: e.tensor_copy(out=XQHB[:, k:k + 1],
                                             in_=HL[:, 8 + k:9 + k]))
            vop(lambda e, k=k: e.tensor_copy(out=XQHB[:, 8 + k:9 + k],
                                             in_=HR[:, k:k + 1]))
        barrier()

        if FSTEP <= 3:
            return
        def SV(m):
            return LN[m // 8][:, (m % 8) * 256:(m % 8) * 256 + 256]

        def Q2(m):
            t = LN[m // 8].bitcast(BF16)
            off = (m % 8) * 512
            return t[:, off:off + 256]

        TV = SM[:, 0:258]
        CVt = SM[:, 512:768]
        AM2 = ST2X[:, 1536:1792]
        for tti in range(8):
            t0 = tti * 256
            for mg in range(4):
                for mi in range(DC):
                    m = mg * 8 + mi
                    slot = mi % 2
                    wr_load(slot, w1q_d[m])
                    wr_wait("tensor", slot)
                    bk = mi
                    K.wait_pair("tensor", bank_free[bk])
                    wt = wrb(slot, 1024)
                    if tti == 0:
                        for k in range(DC):
                            K.op("tensor", lambda e, k=k, bk=bk, wt=wt: e.matmul(
                                PS[bk][:, 1:258], wt[:, k * 128:(k + 1) * 128],
                                XQv(k)[:, 0:257], start=(k == 0),
                                stop=False, skip_group_check=True), inc=None)
                        for k in range(DC):
                            K.op("tensor", lambda e, k=k, bk=bk, wt=wt: e.matmul(
                                PS[bk][:, 0:1], wt[:, k * 128:(k + 1) * 128],
                                XQHB[:, k:k + 1], start=False,
                                stop=(k == DC - 1), skip_group_check=True),
                                inc=(sT if k == DC - 1 else None))
                    elif tti == 7:
                        for k in range(DC):
                            K.op("tensor", lambda e, k=k, bk=bk, wt=wt, t0=t0:
                                 e.matmul(
                                PS[bk][:, 0:257], wt[:, k * 128:(k + 1) * 128],
                                XQv(k)[:, t0 - 1:t0 + 256], start=(k == 0),
                                stop=False, skip_group_check=True), inc=None)
                        for k in range(DC):
                            K.op("tensor", lambda e, k=k, bk=bk, wt=wt: e.matmul(
                                PS[bk][:, 257:258], wt[:, k * 128:(k + 1) * 128],
                                XQHB[:, 8 + k:9 + k], start=False,
                                stop=(k == DC - 1), skip_group_check=True),
                                inc=(sT if k == DC - 1 else None))
                    else:
                        for k in range(DC):
                            K.op("tensor", lambda e, k=k, bk=bk, wt=wt, t0=t0:
                                 e.matmul(
                                PS[bk][:, 0:258], wt[:, k * 128:(k + 1) * 128],
                                XQv(k)[:, t0 - 1:t0 + 257], start=(k == 0),
                                stop=(k == DC - 1)),
                                inc=(sT if k == DC - 1 else None))
                    bank_ready[bk] = (sT, sT.n)
                    wr_consumed(slot)
                    consume("vector", bk)
                    K.wait("vector", sS)
                    if tti == 0:
                        vop(lambda e, bk=bk: e.tensor_tensor(
                            out=TV[:, 1:258], in0=PS[bk][:, 1:258],
                            in1=STIX[:, 0:257], op=ALU.mult))
                        vop(lambda e, bk=bk: e.tensor_scalar(
                            out=TV[:, 0:1], in0=PS[bk][:, 0:1],
                            scalar1=HL[:, 17:18], scalar2=None, op0=ALU.mult))
                    elif tti == 7:
                        vop(lambda e, bk=bk, t0=t0: e.tensor_tensor(
                            out=TV[:, 0:257], in0=PS[bk][:, 0:257],
                            in1=STIX[:, t0 - 1:t0 + 256], op=ALU.mult))
                        vop(lambda e, bk=bk: e.tensor_scalar(
                            out=TV[:, 257:258], in0=PS[bk][:, 257:258],
                            scalar1=HR[:, 16:17], scalar2=None, op0=ALU.mult))
                    else:
                        vop(lambda e, bk=bk, t0=t0: e.tensor_tensor(
                            out=TV, in0=PS[bk][:, 0:258],
                            in1=STIX[:, t0 - 1:t0 + 257], op=ALU.mult))
                    release(bk, sV)
                    vop(lambda e, m=m: e.tensor_scalar(
                        out=CVt, in0=TV[:, 0:256], scalar1=CW[0][:, m:m + 1],
                        scalar2=col["f1_cb"][:, m:m + 1], op0=ALU.mult,
                        op1=ALU.add))
                    vop(lambda e, m=m: e.scalar_tensor_tensor(
                        out=CVt, in0=TV[:, 1:257], scalar=CW[1][:, m:m + 1],
                        in1=CVt, op0=ALU.mult, op1=ALU.add))
                    vop(lambda e, m=m: e.scalar_tensor_tensor(
                        out=CVt, in0=TV[:, 2:258], scalar=CW[2][:, m:m + 1],
                        in1=CVt, op0=ALU.mult, op1=ALU.add))
                    K.wait("scalar", sV)
                    sop(lambda e, m=m: e.activation(out=SV(m), in_=CVt,
                                                    func=AF.Silu))
            if FSTEP <= 4:
                return
            K.wait("vector", sS)
            AMS2 = SM[:, 768:1024]
            K.wait("scalar", sV)
            sop(lambda e: e.activation(out=AM2, in_=SV(0), func=AF.Abs))
            K.wait("vector", sS)
            for m in range(1, 32):
                K.wait("scalar", sV)
                sop(lambda e, m=m: e.activation(out=AMS2, in_=SV(m),
                                                func=AF.Abs))
                K.wait("vector", sS)
                vop(lambda e: e.tensor_max(out=AM2, in0=AM2, in1=AMS2))
            S2C = XTRA[:, 36:38]
            for j in range(2):
                b = j % 2
                K.wait_pair("tensor", bank_free[b])
                K.wait("tensor", sV)
                K.op("tensor", lambda e, j=j, b=b: e.transpose(
                    PS[b][:, 0:128], AM2[:, j * 128:(j + 1) * 128], ident),
                    inc=sT)
                bank_ready[b] = (sT, sT.n)
                consume("vector", b)
                vop(lambda e, j=j, b=b: e.tensor_reduce(
                    out=S2C[:, j:j + 1], in_=PS[b][:, 0:128],
                    axis=mybir.AxisListType.X, op=ALU.max))
                release(b, sV)
            MX2 = XTRA[:, 38:40]
            vop(lambda e: e.tensor_scalar_max(out=MX2, in0=S2C, scalar1=1e-5))
            K.wait("scalar", sV)
            sop(lambda e: e.activation(out=XTRA[:, 60:62], in_=MX2, func=AF.Ln))
            sop(lambda e: e.activation(out=XTRA[:, 62:64], in_=XTRA[:, 60:62],
                                       func=AF.Exp, scale=f32(-1.0)))
            K.wait("vector", sS)
            ST2 = XTRA[:, 60:62]
            vop(lambda e: e.tensor_scalar(out=ST2, in0=XTRA[:, 62:64],
                                          scalar1=127.0, scalar2=None,
                                          op0=ALU.mult))
            ST2I = XTRA[:, 62:64]
            vop(lambda e: e.tensor_scalar(out=ST2I, in0=MX2,
                                          scalar1=1.0 / 127.0, scalar2=None,
                                          op0=ALU.mult))
            R2 = ST2X[0:1, 1024:1280]
            RI2 = ST2X[0:1, 1280:1536]
            for j in range(2):
                b = j % 2
                K.wait_pair("tensor", bank_free[b])
                K.wait("tensor", sV)
                K.op("tensor", lambda e, j=j, b=b: e.matmul(
                    PS[b][0:1, 0:128], ST2[:, j:j + 1], ident, start=True,
                    stop=True), inc=sT)
                bank_ready[b] = (sT, sT.n)
                consume("vector", b)
                vop(lambda e, j=j, b=b: e.tensor_copy(
                    out=R2[:, j * 128:(j + 1) * 128], in_=PS[b][0:1, 0:128]))
                release(b, sV)
                b2 = 2 + j % 2
                K.wait_pair("tensor", bank_free[b2])
                K.wait("tensor", sV)
                K.op("tensor", lambda e, j=j, b2=b2: e.matmul(
                    PS[b2][0:1, 0:128], ST2I[:, j:j + 1], ident, start=True,
                    stop=True), inc=sT)
                bank_ready[b2] = (sT, sT.n)
                consume("vector", b2)
                vop(lambda e, j=j, b2=b2: e.tensor_copy(
                    out=RI2[:, j * 128:(j + 1) * 128], in_=PS[b2][0:1, 0:128]))
                release(b2, sV)
            ST2B = ST2X[:, 0:256]
            ST2IB = ST2X[:, 256:512]
            for which, row, dstb, bb in ((0, R2, ST2B, 4), (1, RI2, ST2IB, 5)):
                K.wait_pair("tensor", bank_free[bb])
                K.wait("tensor", sV)
                K.op("tensor", lambda e, row=row, bb=bb: e.matmul(
                    PS[bb][:, 0:256], ones[0:1, :], row, start=True,
                    stop=True), inc=sT)
                bank_ready[bb] = (sT, sT.n)
                consume("vector", bb)
                vop(lambda e, dstb=dstb, bb=bb: e.tensor_copy(
                    out=dstb, in_=PS[bb][:, 0:256]))
                release(bb, sV)
            Q2F = ST2X[:, 512:768]
            Q2I = ST2X[:, 768:1024].bitcast(I32)[:, 0:256]
            Q2O = ST2X[:, 1792:2048]
            for m in range(32):
                vop(lambda e, m=m: e.tensor_tensor(out=Q2F, in0=SV(m),
                                                   in1=ST2B, op=ALU.mult))
                emit_round(Q2O, Q2F, Q2I, SM[:, 0:256], SM[:, 256:512])
                vop(lambda e, m=m: e.tensor_copy(out=Q2(m), in_=Q2O))
            if FSTEP <= 5:
                return
            for m2 in range(DC):
                bk = 4 + m2 % 2
                for sub in range(2):
                    slot = 2 + sub
                    wr_load(slot, w2q_d[m2][:, sub * 16:(sub + 1) * 16, :])
                    wr_wait("tensor", slot)
                    if sub == 0:
                        K.wait_pair("tensor", bank_free[bk])
                        K.wait("tensor", sV)
                    wt = wrb(slot, 2048)
                    for k2 in range(16):
                        K.op("tensor", lambda e, k2=k2, sub=sub, bk=bk, wt=wt:
                             e.matmul(
                                 PS[bk][:, 0:256], wt[:, k2 * 128:(k2 + 1) * 128],
                                 Q2(sub * 16 + k2), start=(sub == 0 and k2 == 0),
                                 stop=(sub == 1 and k2 == 15)),
                             inc=(sT if k2 == 15 else None))
                    wr_consumed(slot)
                bank_ready[bk] = (sT, sT.n)
                consume("vector", bk)
                vop(lambda e, bk=bk: e.tensor_tensor(
                    out=Q2F, in0=PS[bk][:, 0:256], in1=ST2IB, op=ALU.mult))
                vop(lambda e, m2=m2, t0=t0: e.scalar_tensor_tensor(
                    out=X[m2][:, t0:t0 + 256], in0=Q2F,
                    scalar=SW2I05, in1=X[m2][:, t0:t0 + 256],
                    op0=ALU.mult, op1=ALU.add))
                release(bk, sV)
            barrier()
        barrier()


    def l1_attn_block():
        layernorm(X, LN, 1e-5, sq2=(S[0], S[1]))
        barrier()
        for tt in range(NTT):
            sl = slice(tt * TT, (tt + 1) * TT)
            for m in range(DC):
                slot = m % 2
                wr_load(slot, dr["l1_Win"][m])
                wr_wait("tensor", slot)
                mmgroup(m, [(wr_tile(slot, k), LN[k][:, sl], PS[m][:])
                            for k in range(DC)])
                wr_consumed(slot)
            K.wait("vector", sT)
            for m in range(DC):
                vop(lambda e, m=m, sl=sl: e.tensor_copy(out=LN[m][:, sl],
                                                        in_=PS[m][:]))
                release(m, sV)
        barrier()
        layernorm(LN, LN, 1e-5, g_col=col["l1_lng"], b_col=col["l1_lnb"],
                  sq2=(S[0], S[1]))
        barrier()
        A_ = ST2X[0:16, 0:2048]
        B_ = STIX[0:16, 0:2048]
        D_ = S[1][0:16, 0:2048]
        E_ = S[2][0:16, 0:2048]
        F_ = S[3][0:16, 0:2048]
        CUMM = ST2X[32:48, 0:2048]
        GROM = ST2X[64:80, 0:2048]
        MMB = A_
        ALPH = A_
        BETA = D_
        K.wait_pair("sync", wr_free[0])
        wg = dr["l1_WgateT"].rearrange("(a p) m -> p a m", p=128)
        K.op("sync", lambda e: e.dma_start(
            out=WR[:, 0:256].rearrange("p (a m) -> p a m", m=32), in_=wg),
            inc=sWR[0], k=16)
        wr_ready[0] = (sWR[0], sWR[0].n)
        wr_wait("tensor", 0)
        sop(lambda e: e.activation(out=XTRA[0:16, 60:61],
                                   in_=col["l1_eigraw"], func=AF.Tanh))
        for tt in range(NTT):
            sl = slice(tt * TT, (tt + 1) * TT)
            for who, out_rows, bcol in ((0, ALPH, "l1_bga"), (1, BETA, "l1_bgb")):
                b = who * 2 + tt % 2
                mmgroup(b, [(WR[:, k * 32 + who * 16:k * 32 + who * 16 + 16],
                             LN[k][:, sl], PS[b][0:16, :]) for k in range(DC)])
                consume("scalar", b)
                K.wait("scalar", sV)
                sop(lambda e, out_rows=out_rows, sl=sl, b=b, bcol=bcol:
                    e.activation(out=out_rows[:, sl], in_=PS[b][0:16, :],
                                 func=AF.Sigmoid, bias=col[bcol]))
                release(b, sS)
        wr_free[0] = (sT, sT.n)
        K.wait("vector", sS)
        vop(lambda e: e.tensor_scalar(out=ALPH, in0=ALPH,
                                      scalar1=XTRA[0:16, 60:61], scalar2=None,
                                      op0=ALU.mult))
        def v3_16(ap):
            return ap.rearrange("p (c i) -> p c i", i=16)
        cmv16 = cm128[0:16, 0:16].unsqueeze(1).broadcast_to((16, 128, 16))
        icmv16 = icm128[0:16, 0:16].unsqueeze(1).broadcast_to((16, 128, 16))
        vop(lambda e: e.tensor_tensor(out=v3_16(B_), in0=v3_16(ALPH),
                                      in1=cmv16, op=ALU.mult))
        vop(lambda e: e.tensor_tensor(out=v3_16(E_), in0=v3_16(ALPH),
                                      in1=icmv16, op=ALU.mult))
        vop(lambda e: e.tensor_tensor_scan(out=A_, data0=B_, data1=E_,
                                           initial=0.0, op0=ALU.mult,
                                           op1=ALU.add))
        K.wait("gpsimd", sV)
        K.op("gpsimd", lambda e: e.dma_start(out=CUMM, in_=A_), inc=sG, k=16)
        vop(lambda e: e.tensor_copy(out=B_[:, 1:2048], in_=A_[:, 0:2047]))
        vop(lambda e: e.memset(B_[:, 0:1], 0.0))
        vop(lambda e: e.tensor_tensor(out=v3_16(B_), in0=v3_16(B_),
                                      in1=cmv16, op=ALU.mult))
        vop(lambda e: e.tensor_tensor(out=v3_16(B_), in0=v3_16(B_),
                                      in1=icmv16, op=ALU.add))
        vop(lambda e: e.tensor_single_scalar(out=E_, in_=B_, scalar=1e-8,
                                             op=ALU.is_gt))
        K.wait("vector", sG)
        vop(lambda e: e.tensor_scalar_max(out=A_, in0=B_, scalar1=1e-9))
        K.wait("scalar", sV)
        sop(lambda e: e.activation(out=F_, in_=A_, func=AF.Ln))
        sop(lambda e: e.activation(out=A_, in_=F_, func=AF.Exp,
                                   scale=f32(-1.0)))
        K.wait("vector", sS)
        vop(lambda e: e.tensor_mul(out=B_, in0=E_, in1=A_))
        K.wait("gpsimd", sV)
        K.op("gpsimd", lambda e: e.dma_start(out=GROM, in_=B_), inc=sG, k=16)
        K.wait("vector", sG)
        K.wait("tensor", sG)
        barrier()

        def selmm(band, c, tt, bank):
            K.wait_pair("tensor", bank_free[bank])
            K.wait("tensor", sV)
            K.wait("tensor", sS)
            K.op("tensor", lambda e, band=band, c=c, tt=tt, bank=bank: e.matmul(
                PS[bank][:], col["sel"][:, c * 128:(c + 1) * 128],
                band[:, tt * TT:(tt + 1) * TT], start=True, stop=True), inc=sT)
            bank_ready[bank] = (sT, sT.n)

        for c in range(DC):
            for tt in range(NTT):
                bank = (c * NTT + tt) % 4
                selmm(BETA, c, tt, bank)
                consume("vector", bank)
                vop(lambda e, c=c, tt=tt, bank=bank: e.tensor_mul(
                    out=LN[c][:, tt * TT:(tt + 1) * TT],
                    in0=LN[c][:, tt * TT:(tt + 1) * TT], in1=PS[bank][:]))
                release(bank, sV)
        barrier()

        CMF = S[3]
        cmv128 = cm128[:, 0:16].unsqueeze(1).broadcast_to((128, 128, 16))
        vop(lambda e: e.tensor_copy(
            out=CMF[:, :].rearrange("p (c i) -> p c i", i=16), in_=cmv128))
        ZCOL = SM[:, 0:128]
        vop(lambda e: e.memset(ZCOL, 0.0))
        GV = S[0]
        D1F = S[1]
        CX = S[2]

        def build_cx(c):
            K.wait("gpsimd", sT)
            K.op("gpsimd", lambda e: e.dma_start(out=MMB, in_=CUMM),
                 inc=sG, k=16)
            K.wait("tensor", sG)
            for tt in range(NTT):
                bank = 4 + tt % 2
                selmm(MMB, c, tt, bank)
                consume("vector", bank)
                vop(lambda e, tt=tt, bank=bank: e.tensor_copy(
                    out=CX[:, tt * TT:(tt + 1) * TT], in_=PS[bank][:]))
                release(bank, sV)

        for c in range(DC):
            build_cx(c)
            K.wait("gpsimd", sT)
            K.op("gpsimd", lambda e: e.dma_start(out=MMB, in_=GROM),
                 inc=sG, k=16)
            K.wait("tensor", sG)
            for tt in range(NTT):
                bank = tt % 4
                selmm(MMB, c, tt, bank)
                consume("vector", bank)
                vop(lambda e, c=c, tt=tt, bank=bank: e.tensor_mul(
                    out=GV[:, tt * TT:(tt + 1) * TT],
                    in0=LN[c][:, tt * TT:(tt + 1) * TT], in1=PS[bank][:]))
                release(bank, sV)
            vop(lambda e: e.tensor_copy(out=D1F[:, 1:2048], in_=GV[:, 0:2047]))
            vop(lambda e: e.memset(D1F[:, 0:1], 0.0))
            vop(lambda e: e.tensor_mul(out=D1F[:, :], in0=D1F[:, :],
                                       in1=CMF[:, :]))
            vop(lambda e: e.tensor_tensor_scan(out=GV[:, :], data0=CMF[:, :],
                                               data1=D1F[:, :], initial=0.0,
                                               op0=ALU.mult, op1=ALU.add))
            vop(lambda e: e.tensor_mul(out=D1F[:, :], in0=CX[:, :],
                                       in1=GV[:, :]))
            vop(lambda e, c=c: e.tensor_add(out=LN[c][:, :], in0=LN[c][:, :],
                                            in1=D1F[:, :]))
            vop(lambda e, c=c: e.tensor_tensor_scan(
                out=STIX[:, c * 128:(c + 1) * 128], data0=CX[:, 15::16],
                data1=LN[c][:, 15::16], initial=0.0, op0=ALU.mult, op1=ALU.add))
            vop(lambda e, c=c: e.tensor_tensor_scan(
                out=STIX[:, 1024 + c * 128:1024 + (c + 1) * 128],
                data0=CX[:, 15::16], data1=ZCOL, initial=1.0,
                op0=ALU.mult, op1=ALU.add))
        barrier()

        CB2 = SM[:, 0:16]
        for c in range(DC):
            vop(lambda e, c=c: e.tensor_copy(
                out=CB2[:, c:c + 1], in_=STIX[:, c * 128 + 127:c * 128 + 128]))
            vop(lambda e, c=c: e.tensor_copy(
                out=CB2[:, 8 + c:9 + c],
                in_=STIX[:, 1024 + c * 128 + 127:1024 + c * 128 + 128]))
        K.wait("gpsimd", sV)
        K.op("gpsimd", lambda e: e.dma_start(out=cg_in, in_=CB2), inc=sG, k=16)
        K.wait("gpsimd", sG)
        K.op("gpsimd", lambda e: e.collective_compute(
            "AllGather", ALU.bypass, replica_groups=[list(range(8))],
            ins=[cg_in.opt()], outs=[cg_out.opt()]), inc=sCC)
        K.wait("gpsimd", sCC)
        GC = SM[:, 128:256]
        for j in range(8):
            K.op("gpsimd", lambda e, j=j: e.dma_start(
                out=GC[:, j * 16:(j + 1) * 16],
                in_=cg_out[j * 128:(j + 1) * 128, :]), inc=sG, k=16)
        barrier()
        HIN = SM[:, 256:264]
        PE_ = SM[:, 264:272]
        EE_ = SM[:, 272:280]
        SC8 = SM[:, 280:288]
        for c in range(DC):
            pg = GC[:, 8 + c::16]
            eg = GC[:, c::16]
            vop(lambda e, pg=pg: e.tensor_mul(out=PE_, in0=pg,
                                              in1=col["mexp"]))
            vop(lambda e: e.tensor_add(out=PE_, in0=PE_, in1=col["imexp"]))
            vop(lambda e, eg=eg: e.tensor_mul(out=EE_, in0=eg,
                                              in1=col["mexp"]))
            vop(lambda e: e.tensor_tensor_scan(out=SC8, data0=PE_, data1=EE_,
                                               initial=0.0, op0=ALU.mult,
                                               op1=ALU.add))
            vop(lambda e, c=c: e.tensor_copy(out=HIN[:, c:c + 1],
                                             in_=SC8[:, 7:8]))
        for c in range(DC):
            vop(lambda e, c=c: e.scalar_tensor_tensor(
                out=STIX[:, c * 128:(c + 1) * 128],
                in0=STIX[:, 1024 + c * 128:1024 + (c + 1) * 128],
                scalar=HIN[:, c:c + 1],
                in1=STIX[:, c * 128:(c + 1) * 128],
                op0=ALU.mult, op1=ALU.add))
        for c in range(DC):
            vop(lambda e, c=c: e.tensor_copy(
                out=STIX[:, 1024 + c * 128 + 1:1024 + (c + 1) * 128],
                in_=STIX[:, c * 128:c * 128 + 127]))
            vop(lambda e, c=c: e.tensor_copy(
                out=STIX[:, 1024 + c * 128:1024 + c * 128 + 1],
                in_=HIN[:, c:c + 1]))
        barrier()
        for c in range(DC):
            build_cx(c)
            hp3 = STIX[:, 1024 + c * 128:1024 + (c + 1) * 128].unsqueeze(
                2).broadcast_to((128, 128, 16))
            vop(lambda e, c=c, hp3=hp3: e.tensor_tensor(
                out=D1F[:, :].rearrange("p (n i) -> p n i", i=16),
                in0=CX[:, :].rearrange("p (n i) -> p n i", i=16),
                in1=hp3, op=ALU.mult))
            vop(lambda e, c=c: e.tensor_add(out=LN[c][:, :], in0=LN[c][:, :],
                                            in1=D1F[:, :]))
        barrier()
        for tt in range(NTT):
            sl = slice(tt * TT, (tt + 1) * TT)
            for m in range(DC):
                slot = m % 2
                wr_load(slot, dr["l1_Wout"][m])
                wr_wait("tensor", slot)
                b = m % 4
                K.wait("tensor", sV)
                mmgroup(b, [(wr_tile(slot, k), LN[k][:, sl], PS[b][:])
                            for k in range(DC)])
                wr_consumed(slot)
                consume("vector", b)
                vop(lambda e, m=m, sl=sl, b=b: e.tensor_add(
                    out=X[m][:, sl], in0=X[m][:, sl], in1=PS[b][:]))
                release(b, sV)
        barrier()

    load_consts()
    barrier()
    if STAGE >= 1 and not SKIPL0:
        moe_block()
    if STAGE >= 2 and not SKIPL0:
        l0_attn_block()
    if STAGE >= 3 and not SKIPL0:
        moe_block()
    if STAGE >= 4:
        l1_ffn_block()
    if STAGE >= 5:
        l1_attn_block()
    if STAGE >= 6:
        l1_ffn_block()

    barrier()
    for i in range(DC):
        K.op("gpsimd", lambda e, i=i: e.dma_start(
            out=out[i * 128:(i + 1) * 128, :], in_=X[i][:, :]), inc=sST, k=16)
    if DBG is not None:
        dbg = nc.dram_tensor("dbg", [128, T], F32, kind="ExternalOutput").ap()
        which, idx = DBG
        src_ap = {"ln": lambda: LN[idx][:, :],
                  "x": lambda: X[idx][:, :],
                  "stix": lambda: STIX[:, 0:T],
                  "st2x": lambda: ST2X[:, 0:T],
                  "sm": lambda: SM[:, 0:1920],
                  "s": lambda: S[idx][:, :]}[which]()
        K.op("gpsimd", lambda e, a=src_ap: e.dma_start(
            out=dbg[:, 0:a.shape[-1]] if a.shape[-1] != T else dbg, in_=a),
            inc=sST, k=16)
    K.wait("gpsimd", sST)

    block = K.E(nc.Block())
    K.finish(block)
    K.stack.close()
    return nc


def kernel(**inputs):
    ins = _prep_inputs(inputs)
    nc = build_program()
    res = run_bass_kernel_spmd(nc, ins, core_ids=list(range(NCORES)))
    x = np.asarray(inputs["L0_attn_x"])
    outp = np.empty_like(x)
    for c in range(NCORES):
        b, q = c // 4, c % 4
        outp[b, q * T:(q + 1) * T, :] = res.results[c]["o"].T
    return outp


# version 03
# speedup vs baseline: 1.0552x; 1.0552x over previous
import numpy as np
from contextlib import ExitStack
import concourse.bass as bass
import concourse.mybir as mybir
from concourse.bass_utils import run_bass_kernel_spmd

F32 = mybir.dt.float32
BF16 = mybir.dt.bfloat16
I32 = mybir.dt.int32
ALU = mybir.AluOpType
AF = mybir.ActivationFunctionType
f32 = np.float32

NCORES = 8
T = 2048
D = 1024
DC = 8
NTT = 4
TT = 512

STAGE = 6
DBG = None
SKIP_EXPERTS = False
import os
CSTEP = int(os.environ.get("CSTEP", "99"))
SKIPL0 = int(os.environ.get("SKIPL0", "0"))
ASTEP = int(os.environ.get("ASTEP", "99"))
FSTEP = int(os.environ.get("FSTEP", "99"))


def _tile_major(w):
    K, M = w.shape
    return np.ascontiguousarray(
        w.reshape(K // 128, 128, M // 128, 128).transpose(2, 1, 0, 3))


def _cols(v):
    return np.ascontiguousarray(np.asarray(v).reshape(-1, 128).T)


def _prep_inputs(I):
    g = {}
    for nm in ("Wq", "Wk", "Wv", "Wo"):
        g["l0_" + nm] = _tile_major(np.asarray(I["L0_attn_" + nm]))
    g["l0_W1"] = _tile_major(np.asarray(I["L0_attn_W1"]))
    g["l0_W2"] = _tile_major(np.asarray(I["L0_attn_W2"]))
    g["l0_WgT"] = np.ascontiguousarray(np.asarray(I["L0_ffn_Wg"]).T)
    w1 = np.asarray(I["L0_ffn_w1"])
    w2 = np.asarray(I["L0_ffn_w2"])
    g["moe_w1"] = np.stack([_tile_major(w1[e].T) for e in range(8)])
    g["moe_w2"] = np.stack([_tile_major(w2[e].T) for e in range(8)])
    g["l1_Win"] = _tile_major(np.asarray(I["L1_attn_W_in"]).T)
    g["l1_Wout"] = _tile_major(np.asarray(I["L1_attn_W_out"]).T)
    g["l1_WgateT"] = np.ascontiguousarray(np.asarray(I["L1_attn_W_gate"]).T)
    g["f1_w1T"] = _tile_major(np.asarray(I["L1_ffn_w1"]).T)
    g["f1_w2T"] = _tile_major(np.asarray(I["L1_ffn_w2"]).T)
    for nm in ("bq", "bk", "bv", "bo", "g1", "be1", "b2", "g2", "be2"):
        g["l0_" + nm] = _cols(I["L0_attn_" + nm])
    g["l0_b1"] = _cols(I["L0_attn_b1"])
    g["l0_bg"] = np.ascontiguousarray(np.asarray(I["L0_ffn_bg"]).reshape(8, 1))
    g["l1_lng"] = _cols(I["L1_attn_ln_g"])
    g["l1_lnb"] = _cols(I["L1_attn_ln_b"])
    bgt = np.asarray(I["L1_attn_b_gate"])
    g["l1_bga"] = np.ascontiguousarray(bgt[0:16].reshape(16, 1))
    g["l1_bgb"] = np.ascontiguousarray(bgt[16:32].reshape(16, 1))
    g["l1_eigraw"] = np.ascontiguousarray(np.asarray(I["L1_attn_eig_raw"]).reshape(16, 1))
    cw = np.asarray(I["L1_ffn_conv_w"])
    for t in range(3):
        g[f"f1_cw{t}"] = _cols(cw[:, 0, t])
    g["f1_cb"] = _cols(I["L1_ffn_conv_b"])
    g["oh8"] = np.eye(8, dtype=np.float32)
    g["ones"] = np.ones((128, 128), np.float32)
    g["ident"] = np.eye(128, dtype=np.float32)
    bd = np.zeros((128, 128), np.float32)
    bd[:64, :64] = 1.0
    bd[64:, 64:] = 1.0
    g["bd64"] = bd
    sel = np.zeros((16, 8 * 128), np.float32)
    for c in range(8):
        for p in range(128):
            sel[2 * c + p // 64, c * 128 + p] = 1.0
    g["sel"] = sel
    cm = np.ones((128, 16), np.float32)
    cm[:, 0] = 0.0
    g["cm128"] = cm
    g["icm128"] = np.ascontiguousarray(1.0 - cm)

    x = np.asarray(I["L0_attn_x"])
    ins = []
    for c in range(NCORES):
        b, q = c // 4, c % 4
        d = dict(g)
        d["x"] = np.ascontiguousarray(x[b, q * T:(q + 1) * T, :].T)
        bm = np.zeros((128, 2), np.float32)
        bm[:, b] = 1.0
        d["bmask"] = bm
        me = np.zeros((128, 8), np.float32)
        for j in range(4 * b, c):
            me[:, j] = 1.0
        d["mexp"] = me
        d["imexp"] = np.ascontiguousarray(1.0 - me)
        mL = np.zeros((128, 8), np.float32)
        mR = np.zeros((128, 8), np.float32)
        if q > 0:
            mL[:, c - 1] = 1.0
        if q < 3:
            mR[:, c + 1] = 1.0
        d["mhL"] = mL
        d["mhR"] = mR
        ins.append(d)
    return ins


ENGS = ("sync", "tensor", "vector", "scalar", "gpsimd")


class Sem:
    __slots__ = ("h", "n")

    def __init__(self, h):
        self.h = h
        self.n = 0


class KB:
    def __init__(self):
        self.nc = bass.Bass("TRN2", target_bir_lowering=False)
        self.stack = ExitStack()
        self.streams = {e: [] for e in ENGS}

    def E(self, cm):
        return self.stack.enter_context(cm)

    def sem(self, name):
        return Sem(self.E(self.nc.semaphore(name)))

    def sbuf(self, name, shape, dtype=F32):
        return self.E(self.nc.sbuf_tensor(name, list(shape), dtype))

    def psum(self, name, shape, dtype=F32):
        return self.E(self.nc.psum_tensor(name, list(shape), dtype))

    def op(self, eng, build, inc=None, k=1):
        if eng in ("vector", "scalar"):
            self.streams[eng].append(lambda e, b=build: b(e))
            if inc is None:
                self.streams[eng].append(lambda e: e.drain())
            else:
                inc.n += k
                self.streams[eng].append(
                    lambda e, h=inc.h, kk=k: e.drain().then_inc(h, kk))
        elif inc is None:
            self.streams[eng].append(lambda e, b=build: b(e))
        else:
            inc.n += k
            self.streams[eng].append(
                lambda e, b=build, h=inc.h, kk=k: b(e).then_inc(h, kk))

    def wait(self, eng, sem, val=None):
        v = sem.n if val is None else val
        if v > 0:
            self.streams[eng].append(lambda e, h=sem.h, vv=v: e.wait_ge(h, vv))

    def wait_pair(self, eng, pair):
        if pair is not None:
            self.wait(eng, pair[0], pair[1])

    def finish(self, block):
        for engname in ENGS:
            stream = self.streams[engname]
            getattr(block, engname)(lambda e, s=stream: [fn(e) for fn in s])


def build_program():
    K = KB()
    nc = K.nc
    dr = {}

    def din(name, shape, dtype=F32):
        dr[name] = nc.dram_tensor(name, list(shape), dtype, kind="ExternalInput").ap()
        return dr[name]

    din("x", [D, T])
    for nm in ("Wq", "Wk", "Wv", "Wo"):
        din("l0_" + nm, [8, 128, 8, 128])
    din("l0_W1", [32, 128, 8, 128])
    din("l0_W2", [8, 128, 32, 128])
    din("l0_WgT", [D, 8])
    din("moe_w1", [8, 16, 128, 8, 128])
    din("moe_w2", [8, 8, 128, 8, 128])
    din("l1_Win", [8, 128, 8, 128])
    din("l1_Wout", [8, 128, 8, 128])
    din("l1_WgateT", [D, 32])
    din("f1_w1T", [32, 128, 8, 128])
    din("f1_w2T", [8, 128, 32, 128])
    for nm in ("l0_bq", "l0_bk", "l0_bv", "l0_bo", "l0_g1", "l0_be1", "l0_b2",
               "l0_g2", "l0_be2", "l1_lng", "l1_lnb"):
        din(nm, [128, 8])
    din("l0_b1", [128, 32])
    din("l0_bg", [8, 1])
    din("l1_bga", [16, 1])
    din("l1_bgb", [16, 1])
    din("l1_eigraw", [16, 1])
    for t in range(3):
        din(f"f1_cw{t}", [128, 32])
    din("f1_cb", [128, 32])
    din("oh8", [8, 8])
    din("ones", [128, 128])
    din("ident", [128, 128])
    din("bd64", [128, 128])
    din("sel", [16, 1024])
    din("cm128", [128, 16])
    din("icm128", [128, 16])
    din("bmask", [128, 2])
    din("mexp", [128, 8])
    din("imexp", [128, 8])
    din("mhL", [128, 8])
    din("mhR", [128, 8])
    out = nc.dram_tensor("o", [D, T], F32, kind="ExternalOutput").ap()

    kv_in = nc.dram_tensor("kv_in", [128, 2048], F32).ap()
    kv_out = nc.dram_tensor("kv_out", [128, 2048], F32).ap()
    cg_in = nc.dram_tensor("cg_in", [128, 16], F32).ap()
    cg_out = nc.dram_tensor("cg_out", [8 * 128, 16], F32).ap()
    hl_in = [nc.dram_tensor(f"hl_in{i}", [128, 18], F32).ap() for i in range(2)]
    hl_out = [nc.dram_tensor(f"hl_out{i}", [8 * 128, 18], F32).ap() for i in range(2)]
    w1q_d = nc.dram_tensor("w1q_d", [32, 128, 8, 128], BF16).ap()
    w2q_d = nc.dram_tensor("w2q_d", [8, 128, 32, 128], BF16).ap()

    X = [K.sbuf(f"x{i}", [128, T]) for i in range(DC)]
    LN = [K.sbuf(f"ln{i}", [128, T]) for i in range(DC)]
    S = [K.sbuf(f"s{i}", [128, T]) for i in range(4)]
    WR = K.sbuf("wring", [128, 4096])
    SM = K.sbuf("smalls", [128, 1920])
    CONST = K.sbuf("const", [128, 416])
    COLS = K.sbuf("cols", [128, 296])
    H16 = K.sbuf("h16", [16, 1024])
    STIX = K.sbuf("stix", [128, 2048])
    ST2X = K.sbuf("st2x", [128, 2048])
    XTRA = K.sbuf("xtra", [128, 64])
    XQHB = K.sbuf("xqhb", [128, 16], BF16)
    PS = [K.psum(f"ps{i}", [128, 512]) for i in range(8)]

    ones = CONST[:, 0:128]
    ident = CONST[:, 128:256]
    bd64 = CONST[:, 256:384]
    cm128 = CONST[:, 384:400]
    icm128 = CONST[:, 400:416]
    col = {}

    sW = K.sem("sW")
    sG = K.sem("sG")
    sT = K.sem("sT")
    sV = K.sem("sV")
    sS = K.sem("sS")
    sCC = K.sem("sCC")
    sST = K.sem("sST")
    ALLSEMS = None

    bank_free = [None] * 8
    bank_ready = [None] * 8

    def barrier():
        for e in ENGS:
            for s in (sW, sG, sT, sV, sS, sCC) + tuple(sWR):
                K.wait(e, s)

    def mmgroup(bank, mms, out_slice=None):
        K.wait_pair("tensor", bank_free[bank])
        n = len(mms)
        for i, item in enumerate(mms):
            lhsT, rhs, outap = item
            K.op("tensor",
                 lambda e, o=outap, l=lhsT, r=rhs, st=(i == 0), sp=(i == n - 1):
                 e.matmul(o, l, r, start=st, stop=sp),
                 inc=(sT if i == n - 1 else None))
        bank_ready[bank] = (sT, sT.n)

    def consume(eng, bank):
        K.wait_pair(eng, bank_ready[bank])

    def release(bank, sem):
        bank_free[bank] = (sem, sem.n)

    def vop(build):
        K.op("vector", build, inc=sV)

    def sop(build):
        K.op("scalar", build, inc=sS)

    wr_free = [None] * 4
    wr_ready = [None] * 4
    sWR = [K.sem(f"sWR{i}") for i in range(4)]

    def wr_load(slot, src3d):
        K.wait_pair("sync", wr_free[slot])
        src = src3d.rearrange("p a q -> p (a q)") if src3d.ndim == 3 else src3d
        if src.dtype == BF16:
            nel = src.shape[-1]
            dst = WR[:, slot * 1024:slot * 1024 + nel // 2].bitcast(BF16)
        else:
            dst = WR[:, slot * 1024:slot * 1024 + src.shape[-1]]
        K.op("sync", lambda e, d=dst, s=src: e.dma_start(out=d, in_=s),
             inc=sWR[slot], k=16)
        wr_ready[slot] = (sWR[slot], sWR[slot].n)

    def wrb(slot, nel):
        return WR[:, slot * 1024:slot * 1024 + nel // 2].bitcast(BF16)

    def wr_wait(eng, *slots):
        for s in slots:
            K.wait_pair(eng, wr_ready[s])

    def wr_tile(slot, k):
        return WR[:, slot * 1024 + k * 128: slot * 1024 + (k + 1) * 128]

    def wr_consumed(slot):
        wr_free[slot] = (sT, sT.n)

    def load_consts():
        def ld(dst, src):
            K.op("sync", lambda e, d=dst, s=src: e.dma_start(out=d, in_=s),
                 inc=sW, k=16)
        ld(ones, dr["ones"])
        ld(ident, dr["ident"])
        ld(bd64, dr["bd64"])
        ld(cm128, dr["cm128"])
        ld(icm128, dr["icm128"])
        names = [("l0_bq", 8), ("l0_bk", 8), ("l0_bv", 8), ("l0_bo", 8),
                 ("l0_g1", 8), ("l0_be1", 8), ("l0_b2", 8), ("l0_g2", 8),
                 ("l0_be2", 8), ("l1_lng", 8), ("l1_lnb", 8), ("l0_b1", 32),
                 ("f1_cw0", 32), ("f1_cw1", 32), ("f1_cw2", 32), ("f1_cb", 32),
                 ("bmask", 2), ("mexp", 8), ("imexp", 8), ("mhL", 8), ("mhR", 8)]
        off = 0
        for nm, w in names:
            col[nm] = COLS[:, off:off + w]
            ld(col[nm], dr[nm])
            off += w
        col["oh8"] = COLS[0:8, off:off + 8]
        ld(col["oh8"], dr["oh8"])
        off += 8
        col["l0_bg"] = COLS[0:8, off:off + 1]
        ld(col["l0_bg"], dr["l0_bg"])
        col["l1_bga"] = COLS[0:16, off + 1:off + 2]
        ld(col["l1_bga"], dr["l1_bga"])
        col["l1_bgb"] = COLS[0:16, off + 2:off + 3]
        ld(col["l1_bgb"], dr["l1_bgb"])
        col["l1_eigraw"] = COLS[0:16, off + 3:off + 4]
        ld(col["l1_eigraw"], dr["l1_eigraw"])
        off += 4
        assert off <= 296, off
        col["sel"] = H16[0:16, 0:1024]
        ld(col["sel"], dr["sel"])
        for i in range(DC):
            K.op("sync", lambda e, d=X[i][:, :], s=dr["x"][i * 128:(i + 1) * 128, :]:
                 e.dma_start(out=d, in_=s), inc=sW, k=16)

    def layernorm(src, dst, eps, g_col=None, b_col=None, sq2=None):
        sqt = [sq2[0][:, 0:512], sq2[0][:, 512:1024], sq2[0][:, 1024:1536],
               sq2[0][:, 1536:2048], sq2[1][:, 0:512], sq2[1][:, 512:1024],
               sq2[1][:, 1024:1536], sq2[1][:, 1536:2048]]
        for tt in range(NTT):
            sl = slice(tt * TT, (tt + 1) * TT)
            b0, b1 = (tt % 2) * 2, (tt % 2) * 2 + 1
            K.wait("tensor", sV)
            mmgroup(b0, [(ones, src[k][:, sl], PS[b0][:]) for k in range(DC)])
            consume("vector", b0)
            for k in range(DC):
                vop(lambda e, k=k, sl=sl, b0=b0: e.scalar_tensor_tensor(
                    out=dst[k][:, sl], in0=PS[b0][:], scalar=-1.0 / 1024.0,
                    in1=src[k][:, sl], op0=ALU.mult, op1=ALU.add))
            release(b0, sV)
            K.wait("scalar", sV)
            K.wait_pair("tensor", bank_free[b1])
            for k in range(DC):
                sop(lambda e, k=k, sl=sl: e.activation(
                    out=sqt[k], in_=dst[k][:, sl], func=AF.Square))
                K.wait("tensor", sS)
                K.op("tensor", lambda e, k=k, b1=b1: e.matmul(
                    PS[b1][:], ones, sqt[k], start=(k == 0), stop=(k == DC - 1)),
                    inc=(sT if k == DC - 1 else None))
            bank_ready[b1] = (sT, sT.n)
            consume("scalar", b1)
            K.wait("scalar", sV)
            sop(lambda e, b1=b1: e.activation(
                out=SM[:, 0:TT], in_=PS[b1][:], func=AF.Ln,
                scale=f32(1.0 / 1024.0), bias=f32(eps)))
            release(b1, sS)
            sop(lambda e: e.activation(out=SM[:, TT:2 * TT], in_=SM[:, 0:TT],
                                       func=AF.Exp, scale=f32(-0.5)))
            K.wait("vector", sS)
            for k in range(DC):
                vop(lambda e, k=k, sl=sl: e.tensor_mul(
                    out=dst[k][:, sl], in0=dst[k][:, sl], in1=SM[:, TT:2 * TT]))
                if g_col is not None:
                    vop(lambda e, k=k, sl=sl: e.tensor_scalar(
                        out=dst[k][:, sl], in0=dst[k][:, sl],
                        scalar1=g_col[:, k:k + 1], scalar2=b_col[:, k:k + 1],
                        op0=ALU.mult, op1=ALU.add))

    def moe_block():
        layernorm(X, LN, 1e-5, sq2=(S[0], S[1]))
        barrier()

        wgt = dr["l0_WgT"].rearrange("(a p) m -> p a m", p=128)
        K.wait_pair("sync", wr_free[0])
        K.op("sync", lambda e: e.dma_start(
            out=WR[:, 0:64].rearrange("p (a m) -> p a m", m=8), in_=wgt),
            inc=sWR[0], k=16)
        wr_ready[0] = (sWR[0], sWR[0].n)
        wr_wait("tensor", 0)
        LG = ST2X[0:8, 0:2048]
        for tt in range(NTT):
            b = 4 + tt % 4
            mmgroup(b, [(WR[:, k * 8:(k + 1) * 8],
                         LN[k][:, tt * TT:(tt + 1) * TT], PS[b][0:8, :])
                        for k in range(DC)])
            consume("vector", b)
            vop(lambda e, tt=tt, b=b: e.tensor_scalar(
                out=LG[:, tt * TT:(tt + 1) * TT], in0=PS[b][0:8, :],
                scalar1=col["l0_bg"], scalar2=None, op0=ALU.add))
            release(b, sV)
        wr_free[0] = (sV, sV.n)

        LGT = S[1][:, 0:128]
        for j in range(16):
            bb = j % 2
            K.wait_pair("tensor", bank_free[bb])
            K.wait("tensor", sV)
            K.op("tensor", lambda e, j=j, bb=bb: e.transpose(
                PS[bb][:, 0:8], LG[:, j * 128:(j + 1) * 128], ident[0:8, 0:8]),
                inc=sT)
            bank_ready[bb] = (sT, sT.n)
            consume("vector", bb)
            vop(lambda e, j=j, bb=bb: e.tensor_copy(
                out=LGT[:, j * 8:(j + 1) * 8], in_=PS[bb][:, 0:8]))
            release(bb, sV)

        def b16(ap):
            return ap.unsqueeze(2).broadcast_to((128, 16, 8))
        v3 = LGT.rearrange("p (j e) -> p j e", e=8)
        MX = SM[:, 1664:1680]
        SUM2 = SM[:, 1680:1696]
        M1 = SM[:, 1696:1712]
        M2 = SM[:, 1712:1728]
        EX = S[1][:, 128:256]
        ex3 = EX.rearrange("p (j e) -> p j e", e=8)
        TMP = S[1][:, 256:384]
        PRE = S[1][:, 384:512]
        pre3 = PRE.rearrange("p (j e) -> p j e", e=8)
        def rtree(out2d, in3, op):
            t8 = SM[:, 1760:1824].rearrange("p (j e) -> p j e", e=4)
            t4 = SM[:, 1824:1856].rearrange("p (j e) -> p j e", e=2)
            vop(lambda e: e.tensor_tensor(out=t8, in0=in3[:, :, 0:4],
                                          in1=in3[:, :, 4:8], op=op))
            vop(lambda e: e.tensor_tensor(out=t4, in0=t8[:, :, 0:2],
                                          in1=t8[:, :, 2:4], op=op))
            vop(lambda e: e.tensor_tensor(out=out2d.unsqueeze(2), in0=t4[:, :, 0:1],
                                          in1=t4[:, :, 1:2], op=op))
        cstep = [0]
        def cgate():
            cstep[0] += 1
            return cstep[0] <= CSTEP
        if cgate(): rtree(MX, v3, ALU.max)
        if cgate(): vop(lambda e: e.tensor_tensor(out=pre3, in0=v3, in1=b16(MX),
                                      op=ALU.subtract))
        if cgate():
            K.wait("scalar", sV)
            sop(lambda e: e.activation(out=EX, in_=PRE, func=AF.Exp))
            K.wait("vector", sS)
        if cgate(): rtree(SUM2, ex3, ALU.add)
        if cgate():
            K.wait("scalar", sV)
            sop(lambda e: e.activation(out=SM[:, 1728:1744], in_=SUM2, func=AF.Ln))
            sop(lambda e: e.activation(out=SUM2, in_=SM[:, 1728:1744], func=AF.Exp,
                                   scale=f32(-1.0)))
            K.wait("vector", sS)
        if cgate(): vop(lambda e: e.tensor_tensor(out=ex3, in0=ex3, in1=b16(SUM2), op=ALU.mult))
        if cgate(): rtree(M1, ex3, ALU.max)
        tmp3 = TMP.rearrange("p (j e) -> p j e", e=8)
        if cgate(): vop(lambda e: e.tensor_tensor(out=tmp3, in0=ex3, in1=b16(M1), op=ALU.is_ge))
        if cgate(): vop(lambda e: e.scalar_tensor_tensor(out=TMP, in0=TMP, scalar=-2.0,
                                             in1=EX, op0=ALU.mult, op1=ALU.add))
        if cgate(): rtree(M2, tmp3, ALU.max)
        M3 = SM[:, 1744:1760]
        if cgate(): vop(lambda e: e.tensor_add(out=M3, in0=M1, in1=M2))
        if cgate():
            K.wait("scalar", sV)
            sop(lambda e: e.activation(out=SM[:, 1728:1744], in_=M3, func=AF.Ln))
            sop(lambda e: e.activation(out=M1, in_=SM[:, 1728:1744], func=AF.Exp,
                                   scale=f32(-1.0)))
            K.wait("vector", sS)
        if cgate(): vop(lambda e: e.tensor_scalar_mul(out=M1, in0=M1, scalar1=0.5))
        if cgate(): vop(lambda e: e.tensor_tensor(out=tmp3, in0=ex3, in1=b16(M2), op=ALU.is_ge))
        if cgate(): vop(lambda e: e.tensor_mul(out=TMP, in0=TMP, in1=EX))
        if cgate(): vop(lambda e: e.tensor_tensor(out=tmp3, in0=tmp3, in1=b16(M1), op=ALU.mult))
        CB = ST2X[0:8, 0:2048]
        for j in range(16):
            bb = j % 2
            K.wait_pair("tensor", bank_free[bb])
            K.wait("tensor", sV)
            K.op("tensor", lambda e, j=j, bb=bb: e.transpose(
                PS[bb][0:8, 0:128], TMP[:, j * 8:(j + 1) * 8], ident), inc=sT)
            bank_ready[bb] = (sT, sT.n)
            consume("vector", bb)
            vop(lambda e, j=j, bb=bb: e.tensor_copy(
                out=CB[:, j * 128:(j + 1) * 128], in_=PS[bb][0:8, 0:128]))
            release(bb, sV)
        barrier()

        SG = SM[:, 0:512]
        CBXF = STIX[:, 0:2048]
        W2SC = SM[:, 512:1024]
        IND8 = SM[0:8, 1536:1664]

        def AE(mp, t2):
            t = S[mp // 2]
            base = (mp % 2) * 1024 + t2 * 512
            return t[:, base:base + 512]

        for e_idx in range(8 if not SKIP_EXPERTS else 0):
            K.wait("vector", sT)
            vop(lambda e, e_idx=e_idx: e.tensor_scalar_mul(
                out=IND8, in0=ones[0:8, 0:128],
                scalar1=col["oh8"][0:8, e_idx:e_idx + 1]))
            for tt in range(NTT):
                sl = slice(tt * TT, (tt + 1) * TT)
                bb = 4 + tt % 2
                K.wait_pair("tensor", bank_free[bb])
                K.wait("tensor", sV)
                K.op("tensor", lambda e, sl=sl, bb=bb: e.matmul(
                    PS[bb][:], IND8, CB[0:8, sl], start=True, stop=True),
                    inc=sT)
                bank_ready[bb] = (sT, sT.n)
                consume("vector", bb)
                vop(lambda e, sl=sl, bb=bb: e.tensor_copy(
                    out=CBXF[:, sl], in_=PS[bb][:]))
                release(bb, sV)
            for half in range(2):
                K.wait("vector", sT)
                for mp in range(8):
                    slot_g, slot_u = (mp % 2) * 2, (mp % 2) * 2 + 1
                    wr_load(slot_g, dr["moe_w1"][e_idx, mp])
                    wr_load(slot_u, dr["moe_w1"][e_idx, mp + 8])
                    wr_wait("tensor", slot_g, slot_u)
                    for t2 in range(2):
                        sl = slice(half * 1024 + t2 * TT,
                                   half * 1024 + (t2 + 1) * TT)
                        bg_, bu_ = t2, 2 + t2
                        mmgroup(bg_, [(wr_tile(slot_g, k), LN[k][:, sl],
                                       PS[bg_][:]) for k in range(DC)])
                        mmgroup(bu_, [(wr_tile(slot_u, k), LN[k][:, sl],
                                       PS[bu_][:]) for k in range(DC)])
                        consume("scalar", bg_)
                        K.wait("scalar", sV)
                        sop(lambda e, bg_=bg_: e.activation(
                            out=SG, in_=PS[bg_][:], func=AF.Silu))
                        release(bg_, sS)
                        consume("vector", bu_)
                        K.wait("vector", sS)
                        vop(lambda e, mp=mp, bu_=bu_, t2=t2: e.tensor_mul(
                            out=AE(mp, t2), in0=PS[bu_][:], in1=SG))
                        release(bu_, sV)
                    wr_consumed(slot_g)
                    wr_consumed(slot_u)
                K.wait("tensor", sV)
                for m2 in range(8):
                    slot = 2 + m2 % 2
                    wr_load(slot, dr["moe_w2"][e_idx, m2])
                    wr_wait("tensor", slot)
                    for t2 in range(2):
                        sl = slice(half * 1024 + t2 * TT,
                                   half * 1024 + (t2 + 1) * TT)
                        b = 4 + t2
                        mmgroup(b, [(wr_tile(slot, k), AE(k, t2), PS[b][:])
                                    for k in range(DC)])
                        consume("vector", b)
                        vop(lambda e, b=b, sl=sl: e.tensor_mul(
                            out=W2SC, in0=PS[b][:], in1=CBXF[:, sl]))
                        vop(lambda e, m2=m2, sl=sl: e.tensor_add(
                            out=X[m2][:, sl], in0=X[m2][:, sl], in1=W2SC))
                        release(b, sV)
                    wr_consumed(slot)
        barrier()


    def ln_tile(src_c, dst_c, nch, cols, eps, banks, sqs, g_col=None, b_col=None,
                nfeat=1024.0):
        b0, b1 = banks
        K.wait("tensor", sV)
        K.wait("tensor", sS)
        mmgroup(b0, [(ones, src_c(k), PS[b0][:, 0:cols]) for k in range(nch)])
        consume("vector", b0)
        for k in range(nch):
            vop(lambda e, k=k: e.scalar_tensor_tensor(
                out=dst_c(k), in0=PS[b0][:, 0:cols], scalar=-1.0 / nfeat,
                in1=src_c(k), op0=ALU.mult, op1=ALU.add))
        release(b0, sV)
        K.wait("scalar", sV)
        K.wait_pair("tensor", bank_free[b1])
        for k in range(nch):
            sop(lambda e, k=k: e.activation(out=sqs(k), in_=dst_c(k),
                                            func=AF.Square))
            K.wait("tensor", sS)
            K.op("tensor", lambda e, k=k: e.matmul(
                PS[b1][:, 0:cols], ones, sqs(k), start=(k == 0),
                stop=(k == nch - 1)), inc=(sT if k == nch - 1 else None))
        bank_ready[b1] = (sT, sT.n)
        consume("scalar", b1)
        K.wait("scalar", sV)
        sop(lambda e: e.activation(out=SM[:, 0:cols], in_=PS[b1][:, 0:cols],
                                   func=AF.Ln, scale=f32(1.0 / nfeat),
                                   bias=f32(eps)))
        release(b1, sS)
        sop(lambda e: e.activation(out=SM[:, 512:512 + cols], in_=SM[:, 0:cols],
                                   func=AF.Exp, scale=f32(-0.5)))
        K.wait("vector", sS)
        for k in range(nch):
            vop(lambda e, k=k: e.tensor_mul(out=dst_c(k), in0=dst_c(k),
                                            in1=SM[:, 512:512 + cols]))
            if g_col is not None:
                vop(lambda e, k=k: e.tensor_scalar(
                    out=dst_c(k), in0=dst_c(k), scalar1=g_col[:, k:k + 1],
                    scalar2=b_col[:, k:k + 1], op0=ALU.mult, op1=ALU.add))

    def l0_attn_block():
        layernorm(X, LN, 1e-5, sq2=(S[0], S[1]))
        barrier()
        smA = SM[:, 1024:1536]
        smB = SM[:, 1536:1920]
        smB = ST2X[:, 0:512]
        smC = ST2X, None

        def elu_norm(dst_c, bias_col, wcol_src, sl):
            for m in range(DC):
                slot = m % 2
                wr_load(slot, wcol_src(m))
                wr_wait("tensor", slot)
                b = m % 2
                mmgroup(b, [(wr_tile(slot, k), LN[k][:, sl], PS[b][:])
                            for k in range(DC)])
                wr_consumed(slot)
                consume("vector", b)
                vop(lambda e, m=m, b=b: e.tensor_scalar(
                    out=dst_c(m), in0=PS[b][:], scalar1=bias_col[:, m:m + 1],
                    scalar2=None, op0=ALU.add))
                release(b, sV)
            smC = ST2X[:, 512:1024]
            for m in range(DC):
                K.wait("vector", sS)
                vop(lambda e, m=m: e.tensor_single_scalar(
                    out=smA, in_=dst_c(m), scalar=0.0, op=ALU.min))
                vop(lambda e, m=m: e.tensor_single_scalar(
                    out=smC, in_=dst_c(m), scalar=0.0, op=ALU.max))
                K.wait("scalar", sV)
                sop(lambda e: e.activation(out=smB, in_=smA, func=AF.Exp))
                K.wait("vector", sS)
                vop(lambda e, m=m: e.scalar_tensor_tensor(
                    out=dst_c(m), in0=smB, scalar=-1.0, in1=smC,
                    op0=ALU.add, op1=ALU.add))
                K.wait("scalar", sV)
                sop(lambda e, m=m: e.activation(out=smB, in_=dst_c(m),
                                                func=AF.Square))
                b = 2 + m % 2
                K.wait_pair("tensor", bank_free[b])
                K.wait("tensor", sS)
                K.op("tensor", lambda e, b=b: e.matmul(
                    PS[b][:], bd64, smB, start=True, stop=True), inc=sT)
                bank_ready[b] = (sT, sT.n)
                consume("scalar", b)
                K.wait("scalar", sT)
                sop(lambda e, b=b: e.activation(out=smA, in_=PS[b][:],
                                                func=AF.Ln))
                release(b, sS)
                sop(lambda e: e.activation(out=smC, in_=smA, func=AF.Exp,
                                           scale=f32(-0.5)))
                K.wait("vector", sS)
                vop(lambda e, m=m: e.tensor_mul(out=dst_c(m), in0=dst_c(m),
                                                in1=smC))

        def S4(i, c, sl0=0):
            t = S[i + c // 4]
            off = (c % 4) * 512
            return t[:, off:off + 512]

        KTT = STIX[:, 0:1024]
        VTT = STIX[:, 1024:2048]
        kv_started = [False]
        for tt in range(NTT):
            sl = slice(tt * TT, (tt + 1) * TT)
            elu_norm(lambda c: S4(0, c), col["l0_bk"],
                     lambda m: dr["l0_Wk"][m], sl)
            for m in range(DC):
                slot = m % 2
                wr_load(slot, dr["l0_Wv"][m])
                wr_wait("tensor", slot)
                b = m % 2
                mmgroup(b, [(wr_tile(slot, k), LN[k][:, sl], PS[b][:])
                            for k in range(DC)])
                wr_consumed(slot)
                consume("vector", b)
                vop(lambda e, m=m, b=b: e.tensor_scalar(
                    out=S4(2, m), in0=PS[b][:], scalar1=col["l0_bv"][:, m:m + 1],
                    scalar2=None, op0=ALU.add))
                release(b, sV)
            for s in range(4):
                for c in range(DC):
                    for which, srcv, dstv in ((0, S4(0, c), KTT), (1, S4(2, c), VTT)):
                        b = 2 + (c + which) % 2
                        K.wait_pair("tensor", bank_free[b])
                        K.wait("tensor", sV)
                        K.op("tensor", lambda e, b=b, srcv=srcv, s=s: e.transpose(
                            PS[b][:, 0:128], srcv[:, s * 128:(s + 1) * 128],
                            ident), inc=sT)
                        bank_ready[b] = (sT, sT.n)
                        consume("vector", b)
                        vop(lambda e, b=b, dstv=dstv, c=c: e.tensor_copy(
                            out=dstv[:, c * 128:(c + 1) * 128],
                            in_=PS[b][:, 0:128]))
                        release(b, sV)
                K.wait("tensor", sV)
                first = not kv_started[0]
                kv_started[0] = True
                last = (tt == NTT - 1 and s == 3)
                for c in range(DC):
                    bank = 6 + c // 4
                    fp = (c % 4) * 128
                    stt_ = first and (c % 4 == 0)
                    K.op("tensor", lambda e, c=c, bank=bank, fp=fp, stt_=stt_,
                         last=last: e.matmul(
                             PS[bank][:, fp:fp + 128],
                             KTT[:, c * 128:(c + 1) * 128],
                             VTT[:, c * 128:(c + 1) * 128],
                             start=stt_, stop=(last and c % 4 == 3),
                             skip_group_check=True),
                         inc=(sT if (last and c == DC - 1) else None))
        bank_ready[6] = (sT, sT.n)
        bank_ready[7] = (sT, sT.n)

        BDS = STIX[:, 0:1024]
        consume("vector", 6)
        consume("vector", 7)
        vop(lambda e: e.memset(BDS, 0.0))
        for c in range(DC):
            bank = 6 + c // 4
            base = (c % 4) * 128
            vop(lambda e, bank=bank, base=base, c=c: e.tensor_copy(
                out=BDS[0:64, c * 128:c * 128 + 64],
                in_=PS[bank][0:64, base:base + 64]))
            vop(lambda e, bank=bank, base=base, c=c: e.tensor_copy(
                out=BDS[64:128, c * 128 + 64:c * 128 + 128],
                in_=PS[bank][64:128, base + 64:base + 128]))
        release(6, sV)
        release(7, sV)
        KV2 = ST2X[:, 0:2048]
        vop(lambda e: e.tensor_scalar(out=KV2[:, 0:1024], in0=BDS,
                                      scalar1=col["bmask"][:, 0:1], scalar2=None,
                                      op0=ALU.mult))
        vop(lambda e: e.tensor_scalar(out=KV2[:, 1024:2048], in0=BDS,
                                      scalar1=col["bmask"][:, 1:2], scalar2=None,
                                      op0=ALU.mult))
        K.wait("gpsimd", sV)
        K.op("gpsimd", lambda e: e.dma_start(out=kv_in, in_=KV2), inc=sG, k=16)
        K.wait("gpsimd", sG)
        K.op("gpsimd", lambda e: e.collective_compute(
            "AllReduce", ALU.add, replica_groups=[list(range(8))],
            ins=[kv_in.opt()], outs=[kv_out.opt()]), inc=sCC)
        K.wait("gpsimd", sCC)
        K.op("gpsimd", lambda e: e.dma_start(out=KV2, in_=kv_out), inc=sG, k=16)
        barrier()
        vop(lambda e: e.tensor_scalar(out=BDS, in0=KV2[:, 0:1024],
                                      scalar1=col["bmask"][:, 0:1], scalar2=None,
                                      op0=ALU.mult))
        vop(lambda e: e.scalar_tensor_tensor(out=BDS, in0=KV2[:, 1024:2048],
                                             scalar=col["bmask"][:, 1:2],
                                             in1=BDS, op0=ALU.mult, op1=ALU.add))
        barrier()

        if ASTEP <= 1:
            return
        for tt in range(NTT):
            sl = slice(tt * TT, (tt + 1) * TT)
            elu_norm(lambda c: S4(0, c), col["l0_bq"],
                     lambda m: dr["l0_Wq"][m], sl)
            for c in range(DC):
                b = 4 + c % 2
                K.wait_pair("tensor", bank_free[b])
                K.wait("tensor", sV)
                K.op("tensor", lambda e, c=c, b=b: e.matmul(
                    PS[b][:], BDS[:, c * 128:(c + 1) * 128], S4(0, c),
                    start=True, stop=True), inc=sT)
                bank_ready[b] = (sT, sT.n)
                consume("vector", b)
                vop(lambda e, c=c, b=b: e.tensor_scalar(
                    out=S4(2, c), in0=PS[b][:], scalar1=0.125, scalar2=None,
                    op0=ALU.mult))
                release(b, sV)
            if ASTEP <= 2:
                return
            for m in range(DC):
                slot = m % 2
                wr_load(slot, dr["l0_Wo"][m])
                wr_wait("tensor", slot)
                K.wait("tensor", sV)
                b = m % 2
                mmgroup(b, [(wr_tile(slot, k), S4(2, k), PS[b][:])
                            for k in range(DC)])
                wr_consumed(slot)
                consume("vector", b)
                vop(lambda e, m=m, b=b, sl=sl: e.scalar_tensor_tensor(
                    out=S4(0, m), in0=PS[b][:], scalar=col["l0_bo"][:, m:m + 1],
                    in1=LN[m][:, sl], op0=ALU.add, op1=ALU.add))
                release(b, sV)
            if ASTEP <= 3:
                return
            ln_tile(lambda c: S4(0, c), lambda c, sl=sl: LN[c][:, sl], DC, TT,
                    1e-12, (2, 3), lambda c: S4(2, c), col["l0_g1"],
                    col["l0_be1"])
            barrier()
            if ASTEP <= 4:
                return
            for half in range(2):
                cols = 256
                c0 = tt * TT + half * 256

                def AO2(c, c0=c0):
                    return LN[c][:, c0:c0 + cols]

                def H1(m):
                    t = S[2 + m // 8]
                    off = (m % 8) * 256
                    return t[:, off:off + 256]

                K.wait("scalar", sV)
                for g in range(2):
                    for mi in range(16):
                        m = g * 16 + mi
                        slot = mi % 2
                        wr_load(slot, dr["l0_W1"][m])
                        wr_wait("tensor", slot)
                        b = mi % 2
                        K.wait_pair("tensor", bank_free[b])
                        for k in range(DC):
                            K.op("tensor", lambda e, k=k, b=b, slot=slot,
                                 AO2=AO2: e.matmul(
                                PS[b][:, 0:cols], wr_tile(slot, k), AO2(k),
                                start=(k == 0), stop=(k == DC - 1)),
                                inc=(sT if k == DC - 1 else None))
                        bank_ready[b] = (sT, sT.n)
                        wr_consumed(slot)
                        consume("scalar", b)
                        sop(lambda e, mi=mi, b=b, m=m: e.activation(
                            out=H1(mi), in_=PS[b][:, 0:cols], func=AF.Gelu,
                            bias=col["l0_b1"][:, m:m + 1]))
                        release(b, sS)
                    for m2 in range(DC):
                        bank = 4 + m2 // 2
                        fof = (m2 % 2) * 256
                        if g == 0:
                            K.wait_pair("tensor", bank_free[bank])
                        for sub in range(2):
                            slot = 2 + (m2 * 2 + sub) % 2
                            wr_load(slot, dr["l0_W2"][m2]
                                    [:, g * 16 + sub * 8:g * 16 + sub * 8 + 8, :])
                            wr_wait("tensor", slot)
                            if sub == 0:
                                K.wait("tensor", sS)
                            for k2 in range(8):
                                last = (g == 1 and sub == 1 and k2 == 7)
                                K.op("tensor", lambda e, k2=k2, sub=sub, bank=bank,
                                     fof=fof, slot=slot, g=g, last=last,
                                     m2=m2: e.matmul(
                                         PS[bank][:, fof:fof + 256],
                                         wr_tile(slot, k2),
                                         H1(sub * 8 + k2),
                                         start=(g == 0 and sub == 0 and k2 == 0
                                                and m2 % 2 == 0),
                                         stop=last, skip_group_check=True),
                                    inc=(sT if k2 == 7 else None))
                            wr_consumed(slot)
                for bank in (4, 5, 6, 7):
                    bank_ready[bank] = (sT, sT.n)
                for m2 in range(DC):
                    bank = 4 + m2 // 2
                    fof = (m2 % 2) * 256
                    consume("vector", bank)
                    vop(lambda e, m2=m2, bank=bank, fof=fof, AO2=AO2:
                        e.scalar_tensor_tensor(
                        out=S[2][:, m2 * 256:(m2 + 1) * 256],
                        in0=PS[bank][:, fof:fof + 256],
                        scalar=col["l0_b2"][:, m2:m2 + 1], in1=AO2(m2),
                        op0=ALU.add, op1=ALU.add))
                for bank in (4, 5, 6, 7):
                    release(bank, sV)
                ln_tile(lambda c: S[2][:, c * 256:(c + 1) * 256],
                        lambda c: S[3][:, c * 256:(c + 1) * 256], DC, 256,
                        1e-12, (2, 3), lambda c: S[1][:, c * 256:(c + 1) * 256],
                        col["l0_g2"], col["l0_be2"])
                for m in range(DC):
                    vop(lambda e, m=m, c0=c0: e.tensor_add(
                        out=X[m][:, c0:c0 + cols], in0=X[m][:, c0:c0 + cols],
                        in1=S[3][:, m * 256:(m + 1) * 256]))
                if ASTEP >= 5 and ASTEP < 99 and tt * 2 + half + 1 >= ASTEP - 4:
                    return
            barrier()
        barrier()



    def emit_round(out, vin, iview, ef, gf):
        vop(lambda e: e.tensor_copy(out=iview, in_=vin))
        vop(lambda e: e.tensor_copy(out=out, in_=iview))
        vop(lambda e: e.tensor_tensor(out=ef, in0=vin, in1=out, op=ALU.subtract))
        vop(lambda e: e.tensor_single_scalar(out=gf, in_=ef, scalar=0.5,
                                             op=ALU.is_ge))
        vop(lambda e: e.tensor_add(out=out, in0=out, in1=gf))
        vop(lambda e: e.tensor_single_scalar(out=gf, in_=ef, scalar=-0.5,
                                             op=ALU.is_lt))
        vop(lambda e: e.tensor_tensor(out=out, in0=out, in1=gf, op=ALU.subtract))

    l1q_state = {"done": False}
    XQH = None

    def l1_quant_weights():
        for wi, (wsrc, wdst, ncol, scol) in enumerate((
                (dr["f1_w1T"], w1q_d, 32, 32),
                (dr["f1_w2T"], w2q_d, 8, 33))):
            nslot = 1024
            acc = XTRA[:, 40:41]
            K.wait("vector", sG)
            vop(lambda e, acc=acc: e.memset(acc, 0.0))
            if wi == 0:
                vop(lambda e, acc=acc: e.tensor_copy(out=XTRA[:, 48:49], in_=acc))
            cols_list = []
            for mc in range(ncol):
                sub = wsrc[mc]
                if sub.shape[1] * sub.shape[2] == 1024:
                    parts = [sub]
                else:
                    parts = [sub[:, q * 8:(q + 1) * 8, :]
                             for q in range(sub.shape[1] // 8)]
                for pt in parts:
                    cols_list.append(pt)
            for ci, pt in enumerate(cols_list):
                slot = ci % 2
                wr_load(slot, pt)
                wr_wait("vector", slot)
                vop(lambda e, slot=slot: e.tensor_reduce(
                    out=XTRA[:, 41:42], in_=WR[:, slot * 1024:(slot + 1) * 1024],
                    axis=mybir.AxisListType.X, op=ALU.add,
                    apply_absolute_value=True))
                vop(lambda e, acc=acc: e.tensor_add(out=acc, in0=acc,
                                                    in1=XTRA[:, 41:42]))
                if wi == 0 and ci == 0:
                    vop(lambda e, acc=acc: e.tensor_copy(out=XTRA[:, 49:50], in_=acc))
                if wi == 0 and ci == len(cols_list) - 1:
                    vop(lambda e, acc=acc: e.tensor_copy(out=XTRA[:, 50:51], in_=acc))
                wr_free[slot] = (sV, sV.n)
            b = 0
            K.wait_pair("tensor", bank_free[b])
            K.wait("tensor", sV)
            K.op("tensor", lambda e, b=b, acc=acc: e.matmul(
                PS[b][:, 0:1], ones, acc, start=True, stop=True), inc=sT)
            bank_ready[b] = (sT, sT.n)
            consume("vector", b)
            nelem = float(len(cols_list) * 1024 * 128)
            vop(lambda e, b=b, nelem=nelem: e.tensor_scalar(
                out=XTRA[:, 42:43], in0=PS[b][:, 0:1], scalar1=1.0 / nelem,
                scalar2=1e-5, op0=ALU.mult, op1=ALU.max))
            release(b, sV)
            swi = XTRA[:, scol:scol + 1]
            vop(lambda e, swi=swi: e.tensor_copy(out=swi, in_=XTRA[:, 42:43]))
            K.wait("scalar", sV)
            sop(lambda e: e.activation(out=XTRA[:, 43:44], in_=XTRA[:, 42:43],
                                       func=AF.Ln))
            sop(lambda e: e.activation(out=XTRA[:, 44:45], in_=XTRA[:, 43:44],
                                       func=AF.Exp, scale=f32(-1.0)))
            K.wait("vector", sS)
            if FSTEP == 0 and wi == 0:
                return
            ti32 = ST2X[:, 0:1024].bitcast(I32)
            tf = ST2X[:, 1024:2048]
            for ci, pt in enumerate(cols_list):
                slot = ci % 2
                wr_load(slot, pt)
                wr_wait("vector", slot)
                vop(lambda e, slot=slot: e.tensor_scalar(
                    out=tf, in0=WR[:, slot * 1024:(slot + 1) * 1024],
                    scalar1=XTRA[:, 44:45], scalar2=None, op0=ALU.mult))
                wr_free[slot] = (sV, sV.n)
                emit_round(ST2X[:, 0:1024], tf, SM[:, 0:1024].bitcast(I32),
                           STIX[:, 0:1024], STIX[:, 1024:2048])
                vop(lambda e: e.tensor_scalar(
                    out=tf, in0=ST2X[:, 0:1024], scalar1=-1.0, scalar2=1.0,
                    op0=ALU.max, op1=ALU.min))
                bq = WR[:, 2048:2560].bitcast(BF16)
                K.wait("vector", sG)
                vop(lambda e, bq=bq: e.tensor_copy(out=bq, in_=tf))
                dst = (wdst[ci // (1024 // 1024)] if False else None)
                K.wait("gpsimd", sV)
                if wi == 0:
                    ddst = w1q_d[ci].rearrange("p a q -> p (a q)")
                else:
                    mc2, qq = ci // 4, ci % 4
                    ddst = w2q_d[mc2][:, qq * 8:(qq + 1) * 8, :].rearrange(
                        "p a q -> p (a q)")
                K.op("gpsimd", lambda e, bq=bq, ddst=ddst: e.dma_start(
                    out=ddst, in_=bq), inc=sG, k=16)
        K.wait("vector", sG)
        barrier()

    def l1_ffn_block():
        layernorm(X, LN, 1e-5, sq2=(S[0], S[1]))
        barrier()
        if not l1q_state["done"]:
            l1_quant_weights()
            l1q_state["done"] = True
        if FSTEP <= 1:
            return
        cws = [XTRA[:, 45 + t:46 + t] for t in range(3)]
        CW = [SM[:, 1792 + 32 * t:1792 + 32 * (t + 1)] for t in range(3)]
        for t in range(3):
            vop(lambda e, t=t: e.tensor_scalar(
                out=CW[t], in0=col[f"f1_cw{t}"], scalar1=XTRA[:, 32:33],
                scalar2=None, op0=ALU.mult))
        SW2I05 = XTRA[:, 35:36]
        vop(lambda e: e.tensor_scalar(out=SW2I05, in0=XTRA[:, 33:34],
                                      scalar1=0.5, scalar2=None, op0=ALU.mult))

        AM = ST2X[:, 0:2048]
        AMS = STIX[:, 0:2048]
        K.wait("scalar", sV)
        sop(lambda e: e.activation(out=AM, in_=LN[0][:, :], func=AF.Abs))
        K.wait("vector", sS)
        for k in range(1, DC):
            K.wait("scalar", sV)
            sop(lambda e, k=k: e.activation(out=AMS, in_=LN[k][:, :],
                                            func=AF.Abs))
            K.wait("vector", sS)
            vop(lambda e: e.tensor_max(out=AM, in0=AM, in1=AMS))
        SCOL = XTRA[:, 0:16]
        for j in range(16):
            b = j % 2
            K.wait_pair("tensor", bank_free[b])
            K.wait("tensor", sV)
            K.op("tensor", lambda e, j=j, b=b: e.transpose(
                PS[b][:, 0:128], AM[:, j * 128:(j + 1) * 128], ident), inc=sT)
            bank_ready[b] = (sT, sT.n)
            consume("vector", b)
            vop(lambda e, j=j, b=b: e.tensor_reduce(
                out=SCOL[:, j:j + 1], in_=PS[b][:, 0:128],
                axis=mybir.AxisListType.X, op=ALU.max))
            release(b, sV)
        MXC = XTRA[:, 16:32]
        vop(lambda e: e.tensor_scalar_max(out=MXC, in0=SCOL, scalar1=1e-5))
        K.wait("scalar", sV)
        sop(lambda e: e.activation(out=XTRA[:, 40:56], in_=MXC, func=AF.Ln))
        sop(lambda e: e.activation(out=XTRA[:, 40:56], in_=XTRA[:, 40:56],
                                   func=AF.Exp, scale=f32(-1.0))
            if False else e.activation(out=SCOL, in_=XTRA[:, 40:56],
                                       func=AF.Exp, scale=f32(-1.0)))
        K.wait("vector", sS)
        ST = XTRA[:, 40:56]
        vop(lambda e: e.tensor_scalar(out=ST, in0=SCOL, scalar1=127.0,
                                      scalar2=None, op0=ALU.mult))
        STI = XTRA[:, 56:72] if False else SCOL
        vop(lambda e: e.tensor_scalar(out=STI, in0=MXC, scalar1=1.0 / 127.0,
                                      scalar2=None, op0=ALU.mult))
        SROW = ST2X[0:1, 0:2048]
        SIROW = STIX[0:1, 0:2048]
        for j in range(16):
            b = j % 2
            K.wait_pair("tensor", bank_free[b])
            K.wait("tensor", sV)
            K.op("tensor", lambda e, j=j, b=b: e.matmul(
                PS[b][0:1, 0:128], ST[:, j:j + 1], ident, start=True,
                stop=True), inc=sT)
            bank_ready[b] = (sT, sT.n)
            consume("vector", b)
            vop(lambda e, j=j, b=b: e.tensor_copy(
                out=SROW[:, j * 128:(j + 1) * 128], in_=PS[b][0:1, 0:128]))
            release(b, sV)
            b2 = 2 + j % 2
            K.wait_pair("tensor", bank_free[b2])
            K.wait("tensor", sV)
            K.op("tensor", lambda e, j=j, b2=b2: e.matmul(
                PS[b2][0:1, 0:128], STI[:, j:j + 1], ident, start=True,
                stop=True), inc=sT)
            bank_ready[b2] = (sT, sT.n)
            consume("vector", b2)
            vop(lambda e, j=j, b2=b2: e.tensor_copy(
                out=SIROW[:, j * 128:(j + 1) * 128], in_=PS[b2][0:1, 0:128]))
            release(b2, sV)
        for ttb in range(NTT):
            b = 4 + ttb % 2
            K.wait_pair("tensor", bank_free[b])
            K.wait("tensor", sV)
            K.op("tensor", lambda e, ttb=ttb, b=b: e.matmul(
                PS[b][:], ones[0:1, :], SIROW[:, ttb * TT:(ttb + 1) * TT],
                start=True, stop=True), inc=sT)
            bank_ready[b] = (sT, sT.n)
            consume("vector", b)
            vop(lambda e, ttb=ttb, b=b: e.tensor_copy(
                out=STIX[:, ttb * TT:(ttb + 1) * TT], in_=PS[b][:]))
            release(b, sV)
        def XQv(k):
            t = S[k // 2].bitcast(BF16)
            off = (k % 2) * 2048
            return t[:, off:off + 2048]
        QSF = SM[:, 0:256]
        QSI = SM[:, 256:512].bitcast(I32)[:, 0:256]
        QRO = SM[:, 512:768]
        QRE = SM[:, 768:1024]
        QRG = SM[:, 1024:1280]
        for ttb in range(NTT):
            b = 6 + ttb % 2
            K.wait_pair("tensor", bank_free[b])
            K.wait("tensor", sV)
            K.op("tensor", lambda e, ttb=ttb, b=b: e.matmul(
                PS[b][:], ones[0:1, :], SROW[:, ttb * TT:(ttb + 1) * TT],
                start=True, stop=True), inc=sT)
            bank_ready[b] = (sT, sT.n)
            consume("vector", b)
            for k in range(DC):
                for hh in range(2):
                    sl = slice(ttb * TT + hh * 256, ttb * TT + hh * 256 + 256)
                    psl = slice(hh * 256, hh * 256 + 256)
                    vop(lambda e, k=k, sl=sl, b=b, psl=psl: e.tensor_tensor(
                        out=QSF, in0=LN[k][:, sl], in1=PS[b][:, psl],
                        op=ALU.mult))
                    emit_round(QRO, QSF, QSI, QRE, QRG)
                    vop(lambda e, k=k, sl=sl: e.tensor_copy(
                        out=XQv(k)[:, sl], in_=QRO))
            release(b, sV)
        if FSTEP <= 2:
            return
        hcall = 0 if not l1q_state.get("second") else 1
        l1q_state["second"] = True
        CONTRIB = SM[:, 1024:1042]
        for k in range(DC):
            vop(lambda e, k=k: e.tensor_copy(out=CONTRIB[:, k:k + 1],
                                             in_=XQv(k)[:, 0:1]))
            vop(lambda e, k=k: e.tensor_copy(out=CONTRIB[:, 8 + k:9 + k],
                                             in_=XQv(k)[:, 2047:2048]))
        vop(lambda e: e.tensor_copy(out=CONTRIB[:, 16:17], in_=STIX[:, 0:1]))
        vop(lambda e: e.tensor_copy(out=CONTRIB[:, 17:18],
                                    in_=STIX[:, 2047:2048]))
        K.wait("gpsimd", sV)
        K.op("gpsimd", lambda e, h=hcall: e.dma_start(out=hl_in[h], in_=CONTRIB),
             inc=sG, k=16)
        K.wait("gpsimd", sG)
        K.op("gpsimd", lambda e, h=hcall: e.collective_compute(
            "AllGather", ALU.bypass, replica_groups=[list(range(8))],
            ins=[hl_in[h].opt()], outs=[hl_out[h].opt()]), inc=sCC)
        K.wait("gpsimd", sCC)
        GH = SM[:, 0:1152]
        for j in range(8):
            K.op("gpsimd", lambda e, j=j, h=hcall: e.dma_start(
                out=GH[:, j * 144:j * 144 + 18],
                in_=hl_out[h][j * 128:(j + 1) * 128, :]), inc=sG, k=16)
        barrier()
        HL = SM[:, 1200:1218]
        HR = SM[:, 1218:1236]
        vop(lambda e: e.memset(HL, 0.0))
        vop(lambda e: e.memset(HR, 0.0))
        for j in range(8):
            vop(lambda e, j=j: e.scalar_tensor_tensor(
                out=HL, in0=GH[:, j * 144:j * 144 + 18],
                scalar=col["mhL"][:, j:j + 1], in1=HL, op0=ALU.mult,
                op1=ALU.add))
            vop(lambda e, j=j: e.scalar_tensor_tensor(
                out=HR, in0=GH[:, j * 144:j * 144 + 18],
                scalar=col["mhR"][:, j:j + 1], in1=HR, op0=ALU.mult,
                op1=ALU.add))
        for k in range(DC):
            vop(lambda e, k=k: e.tensor_copy(out=XQHB[:, k:k + 1],
                                             in_=HL[:, 8 + k:9 + k]))
            vop(lambda e, k=k: e.tensor_copy(out=XQHB[:, 8 + k:9 + k],
                                             in_=HR[:, k:k + 1]))
        barrier()

        if FSTEP <= 3:
            return
        def SV(m):
            return LN[m // 8][:, (m % 8) * 256:(m % 8) * 256 + 256]

        def Q2(m):
            t = LN[m // 8].bitcast(BF16)
            off = (m % 8) * 512
            return t[:, off:off + 256]

        TV = SM[:, 0:258]
        CVt = SM[:, 512:768]
        AM2 = ST2X[:, 1536:1792]
        for tti in range(8):
            t0 = tti * 256
            for mg in range(4):
                for mi in range(DC):
                    m = mg * 8 + mi
                    slot = mi % 2
                    wr_load(slot, w1q_d[m])
                    wr_wait("tensor", slot)
                    bk = mi
                    K.wait_pair("tensor", bank_free[bk])
                    wt = wrb(slot, 1024)
                    if tti == 0:
                        for k in range(DC):
                            K.op("tensor", lambda e, k=k, bk=bk, wt=wt: e.matmul(
                                PS[bk][:, 1:258], wt[:, k * 128:(k + 1) * 128],
                                XQv(k)[:, 0:257], start=(k == 0),
                                stop=False, skip_group_check=True), inc=None)
                        for k in range(DC):
                            K.op("tensor", lambda e, k=k, bk=bk, wt=wt: e.matmul(
                                PS[bk][:, 0:1], wt[:, k * 128:(k + 1) * 128],
                                XQHB[:, k:k + 1], start=False,
                                stop=(k == DC - 1), skip_group_check=True),
                                inc=(sT if k == DC - 1 else None))
                    elif tti == 7:
                        for k in range(DC):
                            K.op("tensor", lambda e, k=k, bk=bk, wt=wt, t0=t0:
                                 e.matmul(
                                PS[bk][:, 0:257], wt[:, k * 128:(k + 1) * 128],
                                XQv(k)[:, t0 - 1:t0 + 256], start=(k == 0),
                                stop=False, skip_group_check=True), inc=None)
                        for k in range(DC):
                            K.op("tensor", lambda e, k=k, bk=bk, wt=wt: e.matmul(
                                PS[bk][:, 257:258], wt[:, k * 128:(k + 1) * 128],
                                XQHB[:, 8 + k:9 + k], start=False,
                                stop=(k == DC - 1), skip_group_check=True),
                                inc=(sT if k == DC - 1 else None))
                    else:
                        for k in range(DC):
                            K.op("tensor", lambda e, k=k, bk=bk, wt=wt, t0=t0:
                                 e.matmul(
                                PS[bk][:, 0:258], wt[:, k * 128:(k + 1) * 128],
                                XQv(k)[:, t0 - 1:t0 + 257], start=(k == 0),
                                stop=(k == DC - 1)),
                                inc=(sT if k == DC - 1 else None))
                    bank_ready[bk] = (sT, sT.n)
                    wr_consumed(slot)
                    consume("vector", bk)
                    K.wait("vector", sS)
                    if tti == 0:
                        vop(lambda e, bk=bk: e.tensor_tensor(
                            out=TV[:, 1:258], in0=PS[bk][:, 1:258],
                            in1=STIX[:, 0:257], op=ALU.mult))
                        vop(lambda e, bk=bk: e.tensor_scalar(
                            out=TV[:, 0:1], in0=PS[bk][:, 0:1],
                            scalar1=HL[:, 17:18], scalar2=None, op0=ALU.mult))
                    elif tti == 7:
                        vop(lambda e, bk=bk, t0=t0: e.tensor_tensor(
                            out=TV[:, 0:257], in0=PS[bk][:, 0:257],
                            in1=STIX[:, t0 - 1:t0 + 256], op=ALU.mult))
                        vop(lambda e, bk=bk: e.tensor_scalar(
                            out=TV[:, 257:258], in0=PS[bk][:, 257:258],
                            scalar1=HR[:, 16:17], scalar2=None, op0=ALU.mult))
                    else:
                        vop(lambda e, bk=bk, t0=t0: e.tensor_tensor(
                            out=TV, in0=PS[bk][:, 0:258],
                            in1=STIX[:, t0 - 1:t0 + 257], op=ALU.mult))
                    release(bk, sV)
                    vop(lambda e, m=m: e.tensor_scalar(
                        out=CVt, in0=TV[:, 0:256], scalar1=CW[0][:, m:m + 1],
                        scalar2=col["f1_cb"][:, m:m + 1], op0=ALU.mult,
                        op1=ALU.add))
                    vop(lambda e, m=m: e.scalar_tensor_tensor(
                        out=CVt, in0=TV[:, 1:257], scalar=CW[1][:, m:m + 1],
                        in1=CVt, op0=ALU.mult, op1=ALU.add))
                    vop(lambda e, m=m: e.scalar_tensor_tensor(
                        out=CVt, in0=TV[:, 2:258], scalar=CW[2][:, m:m + 1],
                        in1=CVt, op0=ALU.mult, op1=ALU.add))
                    K.wait("scalar", sV)
                    sop(lambda e, m=m: e.activation(out=SV(m), in_=CVt,
                                                    func=AF.Silu))
            if FSTEP <= 4:
                return
            K.wait("vector", sS)
            AMS2 = SM[:, 768:1024]
            K.wait("scalar", sV)
            sop(lambda e: e.activation(out=AM2, in_=SV(0), func=AF.Abs))
            K.wait("vector", sS)
            for m in range(1, 32):
                K.wait("scalar", sV)
                sop(lambda e, m=m: e.activation(out=AMS2, in_=SV(m),
                                                func=AF.Abs))
                K.wait("vector", sS)
                vop(lambda e: e.tensor_max(out=AM2, in0=AM2, in1=AMS2))
            S2C = XTRA[:, 36:38]
            for j in range(2):
                b = j % 2
                K.wait_pair("tensor", bank_free[b])
                K.wait("tensor", sV)
                K.op("tensor", lambda e, j=j, b=b: e.transpose(
                    PS[b][:, 0:128], AM2[:, j * 128:(j + 1) * 128], ident),
                    inc=sT)
                bank_ready[b] = (sT, sT.n)
                consume("vector", b)
                vop(lambda e, j=j, b=b: e.tensor_reduce(
                    out=S2C[:, j:j + 1], in_=PS[b][:, 0:128],
                    axis=mybir.AxisListType.X, op=ALU.max))
                release(b, sV)
            MX2 = XTRA[:, 38:40]
            vop(lambda e: e.tensor_scalar_max(out=MX2, in0=S2C, scalar1=1e-5))
            K.wait("scalar", sV)
            sop(lambda e: e.activation(out=XTRA[:, 60:62], in_=MX2, func=AF.Ln))
            sop(lambda e: e.activation(out=XTRA[:, 62:64], in_=XTRA[:, 60:62],
                                       func=AF.Exp, scale=f32(-1.0)))
            K.wait("vector", sS)
            ST2 = XTRA[:, 60:62]
            vop(lambda e: e.tensor_scalar(out=ST2, in0=XTRA[:, 62:64],
                                          scalar1=127.0, scalar2=None,
                                          op0=ALU.mult))
            ST2I = XTRA[:, 62:64]
            vop(lambda e: e.tensor_scalar(out=ST2I, in0=MX2,
                                          scalar1=1.0 / 127.0, scalar2=None,
                                          op0=ALU.mult))
            R2 = ST2X[0:1, 1024:1280]
            RI2 = ST2X[0:1, 1280:1536]
            for j in range(2):
                b = j % 2
                K.wait_pair("tensor", bank_free[b])
                K.wait("tensor", sV)
                K.op("tensor", lambda e, j=j, b=b: e.matmul(
                    PS[b][0:1, 0:128], ST2[:, j:j + 1], ident, start=True,
                    stop=True), inc=sT)
                bank_ready[b] = (sT, sT.n)
                consume("vector", b)
                vop(lambda e, j=j, b=b: e.tensor_copy(
                    out=R2[:, j * 128:(j + 1) * 128], in_=PS[b][0:1, 0:128]))
                release(b, sV)
                b2 = 2 + j % 2
                K.wait_pair("tensor", bank_free[b2])
                K.wait("tensor", sV)
                K.op("tensor", lambda e, j=j, b2=b2: e.matmul(
                    PS[b2][0:1, 0:128], ST2I[:, j:j + 1], ident, start=True,
                    stop=True), inc=sT)
                bank_ready[b2] = (sT, sT.n)
                consume("vector", b2)
                vop(lambda e, j=j, b2=b2: e.tensor_copy(
                    out=RI2[:, j * 128:(j + 1) * 128], in_=PS[b2][0:1, 0:128]))
                release(b2, sV)
            ST2B = ST2X[:, 0:256]
            ST2IB = ST2X[:, 256:512]
            for which, row, dstb, bb in ((0, R2, ST2B, 4), (1, RI2, ST2IB, 5)):
                K.wait_pair("tensor", bank_free[bb])
                K.wait("tensor", sV)
                K.op("tensor", lambda e, row=row, bb=bb: e.matmul(
                    PS[bb][:, 0:256], ones[0:1, :], row, start=True,
                    stop=True), inc=sT)
                bank_ready[bb] = (sT, sT.n)
                consume("vector", bb)
                vop(lambda e, dstb=dstb, bb=bb: e.tensor_copy(
                    out=dstb, in_=PS[bb][:, 0:256]))
                release(bb, sV)
            Q2F = ST2X[:, 512:768]
            Q2I = ST2X[:, 768:1024].bitcast(I32)[:, 0:256]
            Q2O = ST2X[:, 1792:2048]
            for m in range(32):
                vop(lambda e, m=m: e.tensor_tensor(out=Q2F, in0=SV(m),
                                                   in1=ST2B, op=ALU.mult))
                emit_round(Q2O, Q2F, Q2I, SM[:, 0:256], SM[:, 256:512])
                vop(lambda e, m=m: e.tensor_copy(out=Q2(m), in_=Q2O))
            if FSTEP <= 5:
                return
            for m2 in range(DC):
                bk = 4 + m2 % 2
                for sub in range(2):
                    slot = 2 + sub
                    wr_load(slot, w2q_d[m2][:, sub * 16:(sub + 1) * 16, :])
                    wr_wait("tensor", slot)
                    if sub == 0:
                        K.wait_pair("tensor", bank_free[bk])
                        K.wait("tensor", sV)
                    wt = wrb(slot, 2048)
                    for k2 in range(16):
                        K.op("tensor", lambda e, k2=k2, sub=sub, bk=bk, wt=wt:
                             e.matmul(
                                 PS[bk][:, 0:256], wt[:, k2 * 128:(k2 + 1) * 128],
                                 Q2(sub * 16 + k2), start=(sub == 0 and k2 == 0),
                                 stop=(sub == 1 and k2 == 15)),
                             inc=(sT if k2 == 15 else None))
                    wr_consumed(slot)
                bank_ready[bk] = (sT, sT.n)
                consume("vector", bk)
                vop(lambda e, bk=bk: e.tensor_tensor(
                    out=Q2F, in0=PS[bk][:, 0:256], in1=ST2IB, op=ALU.mult))
                vop(lambda e, m2=m2, t0=t0: e.scalar_tensor_tensor(
                    out=X[m2][:, t0:t0 + 256], in0=Q2F,
                    scalar=SW2I05, in1=X[m2][:, t0:t0 + 256],
                    op0=ALU.mult, op1=ALU.add))
                release(bk, sV)
            barrier()
        barrier()


    def l1_attn_block():
        layernorm(X, LN, 1e-5, sq2=(S[0], S[1]))
        barrier()
        for tt in range(NTT):
            sl = slice(tt * TT, (tt + 1) * TT)
            for m in range(DC):
                slot = m % 2
                wr_load(slot, dr["l1_Win"][m])
                wr_wait("tensor", slot)
                mmgroup(m, [(wr_tile(slot, k), LN[k][:, sl], PS[m][:])
                            for k in range(DC)])
                wr_consumed(slot)
            K.wait("vector", sT)
            for m in range(DC):
                vop(lambda e, m=m, sl=sl: e.tensor_copy(out=LN[m][:, sl],
                                                        in_=PS[m][:]))
                release(m, sV)
        barrier()
        layernorm(LN, LN, 1e-5, g_col=col["l1_lng"], b_col=col["l1_lnb"],
                  sq2=(S[0], S[1]))
        barrier()
        A_ = ST2X[0:16, 0:2048]
        B_ = STIX[0:16, 0:2048]
        D_ = S[1][0:16, 0:2048]
        E_ = S[2][0:16, 0:2048]
        F_ = S[3][0:16, 0:2048]
        CUMM = ST2X[32:48, 0:2048]
        GROM = ST2X[64:80, 0:2048]
        MMB = A_
        ALPH = A_
        BETA = D_
        K.wait_pair("sync", wr_free[0])
        wg = dr["l1_WgateT"].rearrange("(a p) m -> p a m", p=128)
        K.op("sync", lambda e: e.dma_start(
            out=WR[:, 0:256].rearrange("p (a m) -> p a m", m=32), in_=wg),
            inc=sWR[0], k=16)
        wr_ready[0] = (sWR[0], sWR[0].n)
        wr_wait("tensor", 0)
        sop(lambda e: e.activation(out=XTRA[0:16, 60:61],
                                   in_=col["l1_eigraw"], func=AF.Tanh))
        for tt in range(NTT):
            sl = slice(tt * TT, (tt + 1) * TT)
            for who, out_rows, bcol in ((0, ALPH, "l1_bga"), (1, BETA, "l1_bgb")):
                b = who * 2 + tt % 2
                mmgroup(b, [(WR[:, k * 32 + who * 16:k * 32 + who * 16 + 16],
                             LN[k][:, sl], PS[b][0:16, :]) for k in range(DC)])
                consume("scalar", b)
                K.wait("scalar", sV)
                sop(lambda e, out_rows=out_rows, sl=sl, b=b, bcol=bcol:
                    e.activation(out=out_rows[:, sl], in_=PS[b][0:16, :],
                                 func=AF.Sigmoid, bias=col[bcol]))
                release(b, sS)
        wr_free[0] = (sT, sT.n)
        K.wait("vector", sS)
        vop(lambda e: e.tensor_scalar(out=ALPH, in0=ALPH,
                                      scalar1=XTRA[0:16, 60:61], scalar2=None,
                                      op0=ALU.mult))
        def v3_16(ap):
            return ap.rearrange("p (c i) -> p c i", i=16)
        cmv16 = cm128[0:16, 0:16].unsqueeze(1).broadcast_to((16, 128, 16))
        icmv16 = icm128[0:16, 0:16].unsqueeze(1).broadcast_to((16, 128, 16))
        vop(lambda e: e.tensor_tensor(out=v3_16(B_), in0=v3_16(ALPH),
                                      in1=cmv16, op=ALU.mult))
        vop(lambda e: e.tensor_tensor(out=v3_16(E_), in0=v3_16(ALPH),
                                      in1=icmv16, op=ALU.mult))
        vop(lambda e: e.tensor_tensor_scan(out=A_, data0=B_, data1=E_,
                                           initial=0.0, op0=ALU.mult,
                                           op1=ALU.add))
        K.wait("gpsimd", sV)
        K.op("gpsimd", lambda e: e.dma_start(out=CUMM, in_=A_), inc=sG, k=16)
        vop(lambda e: e.tensor_copy(out=B_[:, 1:2048], in_=A_[:, 0:2047]))
        vop(lambda e: e.memset(B_[:, 0:1], 0.0))
        vop(lambda e: e.tensor_tensor(out=v3_16(B_), in0=v3_16(B_),
                                      in1=cmv16, op=ALU.mult))
        vop(lambda e: e.tensor_tensor(out=v3_16(B_), in0=v3_16(B_),
                                      in1=icmv16, op=ALU.add))
        vop(lambda e: e.tensor_single_scalar(out=E_, in_=B_, scalar=1e-8,
                                             op=ALU.is_gt))
        K.wait("vector", sG)
        vop(lambda e: e.tensor_scalar_max(out=A_, in0=B_, scalar1=1e-9))
        K.wait("scalar", sV)
        sop(lambda e: e.activation(out=F_, in_=A_, func=AF.Ln))
        sop(lambda e: e.activation(out=A_, in_=F_, func=AF.Exp,
                                   scale=f32(-1.0)))
        K.wait("vector", sS)
        vop(lambda e: e.tensor_mul(out=B_, in0=E_, in1=A_))
        K.wait("gpsimd", sV)
        K.op("gpsimd", lambda e: e.dma_start(out=GROM, in_=B_), inc=sG, k=16)
        K.wait("vector", sG)
        K.wait("tensor", sG)
        barrier()

        def selmm(band, c, tt, bank):
            K.wait_pair("tensor", bank_free[bank])
            K.wait("tensor", sV)
            K.wait("tensor", sS)
            K.op("tensor", lambda e, band=band, c=c, tt=tt, bank=bank: e.matmul(
                PS[bank][:], col["sel"][:, c * 128:(c + 1) * 128],
                band[:, tt * TT:(tt + 1) * TT], start=True, stop=True), inc=sT)
            bank_ready[bank] = (sT, sT.n)

        for c in range(DC):
            for tt in range(NTT):
                bank = (c * NTT + tt) % 4
                selmm(BETA, c, tt, bank)
                consume("vector", bank)
                vop(lambda e, c=c, tt=tt, bank=bank: e.tensor_mul(
                    out=LN[c][:, tt * TT:(tt + 1) * TT],
                    in0=LN[c][:, tt * TT:(tt + 1) * TT], in1=PS[bank][:]))
                release(bank, sV)
        barrier()

        CMF = S[3]
        cmv128 = cm128[:, 0:16].unsqueeze(1).broadcast_to((128, 128, 16))
        vop(lambda e: e.tensor_copy(
            out=CMF[:, :].rearrange("p (c i) -> p c i", i=16), in_=cmv128))
        ZCOL = SM[:, 0:128]
        vop(lambda e: e.memset(ZCOL, 0.0))
        GV = S[0]
        D1F = S[1]
        CX = S[2]

        def build_cx(c):
            K.wait("gpsimd", sT)
            K.op("gpsimd", lambda e: e.dma_start(out=MMB, in_=CUMM),
                 inc=sG, k=16)
            K.wait("tensor", sG)
            for tt in range(NTT):
                bank = 4 + tt % 2
                selmm(MMB, c, tt, bank)
                consume("vector", bank)
                vop(lambda e, tt=tt, bank=bank: e.tensor_copy(
                    out=CX[:, tt * TT:(tt + 1) * TT], in_=PS[bank][:]))
                release(bank, sV)

        for c in range(DC):
            build_cx(c)
            K.wait("gpsimd", sT)
            K.op("gpsimd", lambda e: e.dma_start(out=MMB, in_=GROM),
                 inc=sG, k=16)
            K.wait("tensor", sG)
            for tt in range(NTT):
                bank = tt % 4
                selmm(MMB, c, tt, bank)
                consume("vector", bank)
                vop(lambda e, c=c, tt=tt, bank=bank: e.tensor_mul(
                    out=GV[:, tt * TT:(tt + 1) * TT],
                    in0=LN[c][:, tt * TT:(tt + 1) * TT], in1=PS[bank][:]))
                release(bank, sV)
            vop(lambda e: e.tensor_copy(out=D1F[:, 1:2048], in_=GV[:, 0:2047]))
            vop(lambda e: e.memset(D1F[:, 0:1], 0.0))
            vop(lambda e: e.tensor_mul(out=D1F[:, :], in0=D1F[:, :],
                                       in1=CMF[:, :]))
            vop(lambda e: e.tensor_tensor_scan(out=GV[:, :], data0=CMF[:, :],
                                               data1=D1F[:, :], initial=0.0,
                                               op0=ALU.mult, op1=ALU.add))
            vop(lambda e: e.tensor_mul(out=D1F[:, :], in0=CX[:, :],
                                       in1=GV[:, :]))
            vop(lambda e, c=c: e.tensor_add(out=LN[c][:, :], in0=LN[c][:, :],
                                            in1=D1F[:, :]))
            vop(lambda e, c=c: e.tensor_tensor_scan(
                out=STIX[:, c * 128:(c + 1) * 128], data0=CX[:, 15::16],
                data1=LN[c][:, 15::16], initial=0.0, op0=ALU.mult, op1=ALU.add))
            vop(lambda e, c=c: e.tensor_tensor_scan(
                out=STIX[:, 1024 + c * 128:1024 + (c + 1) * 128],
                data0=CX[:, 15::16], data1=ZCOL, initial=1.0,
                op0=ALU.mult, op1=ALU.add))
        barrier()

        CB2 = SM[:, 0:16]
        for c in range(DC):
            vop(lambda e, c=c: e.tensor_copy(
                out=CB2[:, c:c + 1], in_=STIX[:, c * 128 + 127:c * 128 + 128]))
            vop(lambda e, c=c: e.tensor_copy(
                out=CB2[:, 8 + c:9 + c],
                in_=STIX[:, 1024 + c * 128 + 127:1024 + c * 128 + 128]))
        K.wait("gpsimd", sV)
        K.op("gpsimd", lambda e: e.dma_start(out=cg_in, in_=CB2), inc=sG, k=16)
        K.wait("gpsimd", sG)
        K.op("gpsimd", lambda e: e.collective_compute(
            "AllGather", ALU.bypass, replica_groups=[list(range(8))],
            ins=[cg_in.opt()], outs=[cg_out.opt()]), inc=sCC)
        K.wait("gpsimd", sCC)
        GC = SM[:, 128:256]
        for j in range(8):
            K.op("gpsimd", lambda e, j=j: e.dma_start(
                out=GC[:, j * 16:(j + 1) * 16],
                in_=cg_out[j * 128:(j + 1) * 128, :]), inc=sG, k=16)
        barrier()
        HIN = SM[:, 256:264]
        PE_ = SM[:, 264:272]
        EE_ = SM[:, 272:280]
        SC8 = SM[:, 280:288]
        for c in range(DC):
            pg = GC[:, 8 + c::16]
            eg = GC[:, c::16]
            vop(lambda e, pg=pg: e.tensor_mul(out=PE_, in0=pg,
                                              in1=col["mexp"]))
            vop(lambda e: e.tensor_add(out=PE_, in0=PE_, in1=col["imexp"]))
            vop(lambda e, eg=eg: e.tensor_mul(out=EE_, in0=eg,
                                              in1=col["mexp"]))
            vop(lambda e: e.tensor_tensor_scan(out=SC8, data0=PE_, data1=EE_,
                                               initial=0.0, op0=ALU.mult,
                                               op1=ALU.add))
            vop(lambda e, c=c: e.tensor_copy(out=HIN[:, c:c + 1],
                                             in_=SC8[:, 7:8]))
        for c in range(DC):
            vop(lambda e, c=c: e.scalar_tensor_tensor(
                out=STIX[:, c * 128:(c + 1) * 128],
                in0=STIX[:, 1024 + c * 128:1024 + (c + 1) * 128],
                scalar=HIN[:, c:c + 1],
                in1=STIX[:, c * 128:(c + 1) * 128],
                op0=ALU.mult, op1=ALU.add))
        for c in range(DC):
            vop(lambda e, c=c: e.tensor_copy(
                out=STIX[:, 1024 + c * 128 + 1:1024 + (c + 1) * 128],
                in_=STIX[:, c * 128:c * 128 + 127]))
            vop(lambda e, c=c: e.tensor_copy(
                out=STIX[:, 1024 + c * 128:1024 + c * 128 + 1],
                in_=HIN[:, c:c + 1]))
        barrier()
        for c in range(DC):
            build_cx(c)
            hp3 = STIX[:, 1024 + c * 128:1024 + (c + 1) * 128].unsqueeze(
                2).broadcast_to((128, 128, 16))
            vop(lambda e, c=c, hp3=hp3: e.tensor_tensor(
                out=D1F[:, :].rearrange("p (n i) -> p n i", i=16),
                in0=CX[:, :].rearrange("p (n i) -> p n i", i=16),
                in1=hp3, op=ALU.mult))
            vop(lambda e, c=c: e.tensor_add(out=LN[c][:, :], in0=LN[c][:, :],
                                            in1=D1F[:, :]))
        barrier()
        for tt in range(NTT):
            sl = slice(tt * TT, (tt + 1) * TT)
            for m in range(DC):
                slot = m % 2
                wr_load(slot, dr["l1_Wout"][m])
                wr_wait("tensor", slot)
                b = m % 4
                K.wait("tensor", sV)
                mmgroup(b, [(wr_tile(slot, k), LN[k][:, sl], PS[b][:])
                            for k in range(DC)])
                wr_consumed(slot)
                consume("vector", b)
                vop(lambda e, m=m, sl=sl, b=b: e.tensor_add(
                    out=X[m][:, sl], in0=X[m][:, sl], in1=PS[b][:]))
                release(b, sV)
        barrier()

    load_consts()
    barrier()
    if STAGE >= 1 and not SKIPL0:
        moe_block()
    if STAGE >= 2 and not SKIPL0:
        l0_attn_block()
    if STAGE >= 3 and not SKIPL0:
        moe_block()
    if STAGE >= 4:
        l1_ffn_block()
    if STAGE >= 5:
        l1_attn_block()
    if STAGE >= 6:
        l1_ffn_block()

    barrier()
    for i in range(DC):
        K.op("gpsimd", lambda e, i=i: e.dma_start(
            out=out[i * 128:(i + 1) * 128, :], in_=X[i][:, :]), inc=sST, k=16)
    if DBG is not None:
        dbg = nc.dram_tensor("dbg", [128, T], F32, kind="ExternalOutput").ap()
        which, idx = DBG
        src_ap = {"ln": lambda: LN[idx][:, :],
                  "x": lambda: X[idx][:, :],
                  "stix": lambda: STIX[:, 0:T],
                  "st2x": lambda: ST2X[:, 0:T],
                  "sm": lambda: SM[:, 0:1920],
                  "s": lambda: S[idx][:, :]}[which]()
        K.op("gpsimd", lambda e, a=src_ap: e.dma_start(
            out=dbg[:, 0:a.shape[-1]] if a.shape[-1] != T else dbg, in_=a),
            inc=sST, k=16)
    K.wait("gpsimd", sST)

    block = K.E(nc.Block())
    K.finish(block)
    K.stack.close()
    return nc


def kernel(**inputs):
    ins = _prep_inputs(inputs)
    nc = build_program()
    res = run_bass_kernel_spmd(nc, ins, core_ids=list(range(NCORES)))
    x = np.asarray(inputs["L0_attn_x"])
    outp = np.empty_like(x)
    for c in range(NCORES):
        b, q = c // 4, c % 4
        outp[b, q * T:(q + 1) * T, :] = res.results[c]["o"].T
    return outp


# version 04
# speedup vs baseline: 1.1237x; 1.0650x over previous
import numpy as np
from contextlib import ExitStack
import concourse.bass as bass
import concourse.mybir as mybir
from concourse.bass_utils import run_bass_kernel_spmd

F32 = mybir.dt.float32
BF16 = mybir.dt.bfloat16
I32 = mybir.dt.int32
ALU = mybir.AluOpType
AF = mybir.ActivationFunctionType
f32 = np.float32

NCORES = 8
T = 2048
D = 1024
DC = 8
NTT = 4
TT = 512

STAGE = 6
DBG = None
SKIP_EXPERTS = False
import os
CSTEP = int(os.environ.get("CSTEP", "99"))
SKIPL0 = int(os.environ.get("SKIPL0", "0"))
ASTEP = int(os.environ.get("ASTEP", "99"))
FSTEP = int(os.environ.get("FSTEP", "99"))


def _tile_major(w):
    K, M = w.shape
    return np.ascontiguousarray(
        w.reshape(K // 128, 128, M // 128, 128).transpose(2, 1, 0, 3))


def _cols(v):
    return np.ascontiguousarray(np.asarray(v).reshape(-1, 128).T)


def _prep_inputs(I):
    g = {}
    for nm in ("Wq", "Wk", "Wv", "Wo"):
        g["l0_" + nm] = _tile_major(np.asarray(I["L0_attn_" + nm]))
    g["l0_W1"] = _tile_major(np.asarray(I["L0_attn_W1"]))
    g["l0_W2"] = _tile_major(np.asarray(I["L0_attn_W2"]))
    g["l0_WgT"] = np.ascontiguousarray(np.asarray(I["L0_ffn_Wg"]).T)
    w1 = np.asarray(I["L0_ffn_w1"])
    w2 = np.asarray(I["L0_ffn_w2"])
    g["moe_w1"] = np.stack([_tile_major(w1[e].T) for e in range(8)])
    g["moe_w2"] = np.stack([_tile_major(w2[e].T) for e in range(8)])
    g["l1_Win"] = _tile_major(np.asarray(I["L1_attn_W_in"]).T)
    g["l1_Wout"] = _tile_major(np.asarray(I["L1_attn_W_out"]).T)
    g["l1_WgateT"] = np.ascontiguousarray(np.asarray(I["L1_attn_W_gate"]).T)
    g["f1_w1T"] = _tile_major(np.asarray(I["L1_ffn_w1"]).T)
    g["f1_w2T"] = _tile_major(np.asarray(I["L1_ffn_w2"]).T)
    for nm in ("bq", "bk", "bv", "bo", "g1", "be1", "b2", "g2", "be2"):
        g["l0_" + nm] = _cols(I["L0_attn_" + nm])
    g["l0_b1"] = _cols(I["L0_attn_b1"])
    g["l0_bg"] = np.ascontiguousarray(np.asarray(I["L0_ffn_bg"]).reshape(8, 1))
    g["l1_lng"] = _cols(I["L1_attn_ln_g"])
    g["l1_lnb"] = _cols(I["L1_attn_ln_b"])
    bgt = np.asarray(I["L1_attn_b_gate"])
    g["l1_bga"] = np.ascontiguousarray(bgt[0:16].reshape(16, 1))
    g["l1_bgb"] = np.ascontiguousarray(bgt[16:32].reshape(16, 1))
    g["l1_eigraw"] = np.ascontiguousarray(np.asarray(I["L1_attn_eig_raw"]).reshape(16, 1))
    cw = np.asarray(I["L1_ffn_conv_w"])
    for t in range(3):
        g[f"f1_cw{t}"] = _cols(cw[:, 0, t])
    g["f1_cb"] = _cols(I["L1_ffn_conv_b"])
    g["oh8"] = np.eye(8, dtype=np.float32)
    g["ones"] = np.ones((128, 128), np.float32)
    g["ident"] = np.eye(128, dtype=np.float32)
    bd = np.zeros((128, 128), np.float32)
    bd[:64, :64] = 1.0
    bd[64:, 64:] = 1.0
    g["bd64"] = bd
    sel = np.zeros((16, 8 * 128), np.float32)
    for c in range(8):
        for p in range(128):
            sel[2 * c + p // 64, c * 128 + p] = 1.0
    g["sel"] = sel
    cm = np.ones((128, 16), np.float32)
    cm[:, 0] = 0.0
    g["cm128"] = cm
    g["icm128"] = np.ascontiguousarray(1.0 - cm)

    x = np.asarray(I["L0_attn_x"])
    ins = []
    for c in range(NCORES):
        b, q = c // 4, c % 4
        d = dict(g)
        d["x"] = np.ascontiguousarray(x[b, q * T:(q + 1) * T, :].T)
        bm = np.zeros((128, 2), np.float32)
        bm[:, b] = 1.0
        d["bmask"] = bm
        me = np.zeros((128, 8), np.float32)
        for j in range(4 * b, c):
            me[:, j] = 1.0
        d["mexp"] = me
        d["imexp"] = np.ascontiguousarray(1.0 - me)
        mL = np.zeros((128, 8), np.float32)
        mR = np.zeros((128, 8), np.float32)
        if q > 0:
            mL[:, c - 1] = 1.0
        if q < 3:
            mR[:, c + 1] = 1.0
        d["mhL"] = mL
        d["mhR"] = mR
        ins.append(d)
    return ins


ENGS = ("sync", "tensor", "vector", "scalar", "gpsimd")


class Sem:
    __slots__ = ("h", "n")

    def __init__(self, h):
        self.h = h
        self.n = 0


class KB:
    def __init__(self):
        self.nc = bass.Bass("TRN2", target_bir_lowering=False)
        self.stack = ExitStack()
        self.streams = {e: [] for e in ENGS}

    def E(self, cm):
        return self.stack.enter_context(cm)

    def sem(self, name):
        return Sem(self.E(self.nc.semaphore(name)))

    def sbuf(self, name, shape, dtype=F32):
        return self.E(self.nc.sbuf_tensor(name, list(shape), dtype))

    def psum(self, name, shape, dtype=F32):
        return self.E(self.nc.psum_tensor(name, list(shape), dtype))

    def op(self, eng, build, inc=None, k=1):
        if eng in ("vector", "scalar"):
            self.streams[eng].append(lambda e, b=build: b(e))
            if inc is None:
                self.streams[eng].append(lambda e: e.drain())
            else:
                inc.n += k
                self.streams[eng].append(
                    lambda e, h=inc.h, kk=k: e.drain().then_inc(h, kk))
        elif inc is None:
            self.streams[eng].append(lambda e, b=build: b(e))
        else:
            inc.n += k
            self.streams[eng].append(
                lambda e, b=build, h=inc.h, kk=k: b(e).then_inc(h, kk))

    def wait(self, eng, sem, val=None):
        v = sem.n if val is None else val
        if v > 0:
            self.streams[eng].append(lambda e, h=sem.h, vv=v: e.wait_ge(h, vv))

    def wait_pair(self, eng, pair):
        if pair is not None:
            self.wait(eng, pair[0], pair[1])

    def finish(self, block):
        for engname in ENGS:
            stream = self.streams[engname]
            getattr(block, engname)(lambda e, s=stream: [fn(e) for fn in s])


def build_program():
    K = KB()
    nc = K.nc
    dr = {}

    def din(name, shape, dtype=F32):
        dr[name] = nc.dram_tensor(name, list(shape), dtype, kind="ExternalInput").ap()
        return dr[name]

    din("x", [D, T])
    for nm in ("Wq", "Wk", "Wv", "Wo"):
        din("l0_" + nm, [8, 128, 8, 128])
    din("l0_W1", [32, 128, 8, 128])
    din("l0_W2", [8, 128, 32, 128])
    din("l0_WgT", [D, 8])
    din("moe_w1", [8, 16, 128, 8, 128])
    din("moe_w2", [8, 8, 128, 8, 128])
    din("l1_Win", [8, 128, 8, 128])
    din("l1_Wout", [8, 128, 8, 128])
    din("l1_WgateT", [D, 32])
    din("f1_w1T", [32, 128, 8, 128])
    din("f1_w2T", [8, 128, 32, 128])
    for nm in ("l0_bq", "l0_bk", "l0_bv", "l0_bo", "l0_g1", "l0_be1", "l0_b2",
               "l0_g2", "l0_be2", "l1_lng", "l1_lnb"):
        din(nm, [128, 8])
    din("l0_b1", [128, 32])
    din("l0_bg", [8, 1])
    din("l1_bga", [16, 1])
    din("l1_bgb", [16, 1])
    din("l1_eigraw", [16, 1])
    for t in range(3):
        din(f"f1_cw{t}", [128, 32])
    din("f1_cb", [128, 32])
    din("oh8", [8, 8])
    din("ones", [128, 128])
    din("ident", [128, 128])
    din("bd64", [128, 128])
    din("sel", [16, 1024])
    din("cm128", [128, 16])
    din("icm128", [128, 16])
    din("bmask", [128, 2])
    din("mexp", [128, 8])
    din("imexp", [128, 8])
    din("mhL", [128, 8])
    din("mhR", [128, 8])
    out = nc.dram_tensor("o", [D, T], F32, kind="ExternalOutput").ap()

    kv_in = nc.dram_tensor("kv_in", [128, 2048], F32).ap()
    kv_out = nc.dram_tensor("kv_out", [128, 2048], F32).ap()
    cg_in = nc.dram_tensor("cg_in", [128, 16], F32).ap()
    cg_out = nc.dram_tensor("cg_out", [8 * 128, 16], F32).ap()
    hl_in = [nc.dram_tensor(f"hl_in{i}", [128, 18], F32).ap() for i in range(2)]
    hl_out = [nc.dram_tensor(f"hl_out{i}", [8 * 128, 18], F32).ap() for i in range(2)]
    w1q_d = nc.dram_tensor("w1q_d", [32, 128, 8, 128], BF16).ap()
    mw1h = nc.dram_tensor("mw1h", [8, 16, 128, 8, 128], BF16).ap()
    mw1l = nc.dram_tensor("mw1l", [8, 16, 128, 8, 128], BF16).ap()
    mw2h = nc.dram_tensor("mw2h", [8, 8, 128, 8, 128], BF16).ap()
    mw2l = nc.dram_tensor("mw2l", [8, 8, 128, 8, 128], BF16).ap()
    w2q_d = nc.dram_tensor("w2q_d", [8, 128, 32, 128], BF16).ap()

    X = [K.sbuf(f"x{i}", [128, T]) for i in range(DC)]
    LN = [K.sbuf(f"ln{i}", [128, T]) for i in range(DC)]
    S = [K.sbuf(f"s{i}", [128, T]) for i in range(4)]
    WR = K.sbuf("wring", [128, 4096])
    SM = K.sbuf("smalls", [128, 1920])
    CONST = K.sbuf("const", [128, 416])
    COLS = K.sbuf("cols", [128, 296])
    H16 = K.sbuf("h16", [16, 1024])
    STIX = K.sbuf("stix", [128, 2048])
    ST2X = K.sbuf("st2x", [128, 2048])
    XTRA = K.sbuf("xtra", [128, 64])
    XQHB = K.sbuf("xqhb", [128, 16], BF16)
    PS = [K.psum(f"ps{i}", [128, 512]) for i in range(8)]

    ones = CONST[:, 0:128]
    ident = CONST[:, 128:256]
    bd64 = CONST[:, 256:384]
    cm128 = CONST[:, 384:400]
    icm128 = CONST[:, 400:416]
    col = {}

    sW = K.sem("sW")
    sG = K.sem("sG")
    sT = K.sem("sT")
    sV = K.sem("sV")
    sS = K.sem("sS")
    sCC = K.sem("sCC")
    sST = K.sem("sST")
    ALLSEMS = None

    bank_free = [None] * 8
    bank_ready = [None] * 8

    def barrier():
        for e in ENGS:
            for s in (sW, sG, sT, sV, sS, sCC) + tuple(sWR):
                K.wait(e, s)

    def mmgroup(bank, mms, out_slice=None):
        K.wait_pair("tensor", bank_free[bank])
        n = len(mms)
        for i, item in enumerate(mms):
            lhsT, rhs, outap = item
            K.op("tensor",
                 lambda e, o=outap, l=lhsT, r=rhs, st=(i == 0), sp=(i == n - 1):
                 e.matmul(o, l, r, start=st, stop=sp),
                 inc=(sT if i == n - 1 else None))
        bank_ready[bank] = (sT, sT.n)

    def consume(eng, bank):
        K.wait_pair(eng, bank_ready[bank])

    def release(bank, sem):
        bank_free[bank] = (sem, sem.n)

    def vop(build):
        K.op("vector", build, inc=sV)

    def sop(build):
        K.op("scalar", build, inc=sS)

    wr_free = [None] * 4
    wr_ready = [None] * 4
    sWR = [K.sem(f"sWR{i}") for i in range(4)]

    def wr_load(slot, src3d, off=0):
        K.wait_pair("sync", wr_free[slot])
        src = src3d.rearrange("p a q -> p (a q)") if src3d.ndim == 3 else src3d
        base = slot * 1024 + off
        if src.dtype == BF16:
            nel = src.shape[-1]
            dst = WR[:, base:base + nel // 2].bitcast(BF16)
        else:
            dst = WR[:, base:base + src.shape[-1]]
        K.op("sync", lambda e, d=dst, s=src: e.dma_start(out=d, in_=s),
             inc=sWR[slot], k=16)
        wr_ready[slot] = (sWR[slot], sWR[slot].n)

    def wr_bt(slot, k, lohalf):
        base = (slot * 1024 + lohalf * 512) * 2 + k * 128
        return WR.bitcast(BF16)[:, base:base + 128]

    def wrb(slot, nel):
        return WR[:, slot * 1024:slot * 1024 + nel // 2].bitcast(BF16)

    def wr_wait(eng, *slots):
        for s in slots:
            K.wait_pair(eng, wr_ready[s])

    def wr_tile(slot, k):
        return WR[:, slot * 1024 + k * 128: slot * 1024 + (k + 1) * 128]

    def wr_consumed(slot):
        wr_free[slot] = (sT, sT.n)

    def load_consts():
        def ld(dst, src):
            K.op("sync", lambda e, d=dst, s=src: e.dma_start(out=d, in_=s),
                 inc=sW, k=16)
        ld(ones, dr["ones"])
        ld(ident, dr["ident"])
        ld(bd64, dr["bd64"])
        ld(cm128, dr["cm128"])
        ld(icm128, dr["icm128"])
        names = [("l0_bq", 8), ("l0_bk", 8), ("l0_bv", 8), ("l0_bo", 8),
                 ("l0_g1", 8), ("l0_be1", 8), ("l0_b2", 8), ("l0_g2", 8),
                 ("l0_be2", 8), ("l1_lng", 8), ("l1_lnb", 8), ("l0_b1", 32),
                 ("f1_cw0", 32), ("f1_cw1", 32), ("f1_cw2", 32), ("f1_cb", 32),
                 ("bmask", 2), ("mexp", 8), ("imexp", 8), ("mhL", 8), ("mhR", 8)]
        off = 0
        for nm, w in names:
            col[nm] = COLS[:, off:off + w]
            ld(col[nm], dr[nm])
            off += w
        col["oh8"] = COLS[0:8, off:off + 8]
        ld(col["oh8"], dr["oh8"])
        off += 8
        col["l0_bg"] = COLS[0:8, off:off + 1]
        ld(col["l0_bg"], dr["l0_bg"])
        col["l1_bga"] = COLS[0:16, off + 1:off + 2]
        ld(col["l1_bga"], dr["l1_bga"])
        col["l1_bgb"] = COLS[0:16, off + 2:off + 3]
        ld(col["l1_bgb"], dr["l1_bgb"])
        col["l1_eigraw"] = COLS[0:16, off + 3:off + 4]
        ld(col["l1_eigraw"], dr["l1_eigraw"])
        off += 4
        assert off <= 296, off
        col["sel"] = H16[0:16, 0:1024]
        ld(col["sel"], dr["sel"])
        for i in range(DC):
            K.op("sync", lambda e, d=X[i][:, :], s=dr["x"][i * 128:(i + 1) * 128, :]:
                 e.dma_start(out=d, in_=s), inc=sW, k=16)

    def layernorm(src, dst, eps, g_col=None, b_col=None, sq2=None):
        sqt = [sq2[0][:, 0:512], sq2[0][:, 512:1024], sq2[0][:, 1024:1536],
               sq2[0][:, 1536:2048], sq2[1][:, 0:512], sq2[1][:, 512:1024],
               sq2[1][:, 1024:1536], sq2[1][:, 1536:2048]]
        for tt in range(NTT):
            sl = slice(tt * TT, (tt + 1) * TT)
            b0, b1 = (tt % 2) * 2, (tt % 2) * 2 + 1
            K.wait("tensor", sV)
            mmgroup(b0, [(ones, src[k][:, sl], PS[b0][:]) for k in range(DC)])
            consume("vector", b0)
            for k in range(DC):
                vop(lambda e, k=k, sl=sl, b0=b0: e.scalar_tensor_tensor(
                    out=dst[k][:, sl], in0=PS[b0][:], scalar=-1.0 / 1024.0,
                    in1=src[k][:, sl], op0=ALU.mult, op1=ALU.add))
            release(b0, sV)
            K.wait("scalar", sV)
            K.wait_pair("tensor", bank_free[b1])
            for k in range(DC):
                sop(lambda e, k=k, sl=sl: e.activation(
                    out=sqt[k], in_=dst[k][:, sl], func=AF.Square))
                K.wait("tensor", sS)
                K.op("tensor", lambda e, k=k, b1=b1: e.matmul(
                    PS[b1][:], ones, sqt[k], start=(k == 0), stop=(k == DC - 1)),
                    inc=(sT if k == DC - 1 else None))
            bank_ready[b1] = (sT, sT.n)
            consume("scalar", b1)
            K.wait("scalar", sV)
            sop(lambda e, b1=b1: e.activation(
                out=SM[:, 0:TT], in_=PS[b1][:], func=AF.Ln,
                scale=f32(1.0 / 1024.0), bias=f32(eps)))
            release(b1, sS)
            sop(lambda e: e.activation(out=SM[:, TT:2 * TT], in_=SM[:, 0:TT],
                                       func=AF.Exp, scale=f32(-0.5)))
            K.wait("vector", sS)
            for k in range(DC):
                vop(lambda e, k=k, sl=sl: e.tensor_mul(
                    out=dst[k][:, sl], in0=dst[k][:, sl], in1=SM[:, TT:2 * TT]))
                if g_col is not None:
                    vop(lambda e, k=k, sl=sl: e.tensor_scalar(
                        out=dst[k][:, sl], in0=dst[k][:, sl],
                        scalar1=g_col[:, k:k + 1], scalar2=b_col[:, k:k + 1],
                        op0=ALU.mult, op1=ALU.add))

    def moe_block():
        layernorm(X, LN, 1e-5, sq2=(S[0], S[1]))
        barrier()

        wgt = dr["l0_WgT"].rearrange("(a p) m -> p a m", p=128)
        K.wait_pair("sync", wr_free[0])
        K.op("sync", lambda e: e.dma_start(
            out=WR[:, 0:64].rearrange("p (a m) -> p a m", m=8), in_=wgt),
            inc=sWR[0], k=16)
        wr_ready[0] = (sWR[0], sWR[0].n)
        wr_wait("tensor", 0)
        LG = ST2X[0:8, 0:2048]
        for tt in range(NTT):
            b = 4 + tt % 4
            mmgroup(b, [(WR[:, k * 8:(k + 1) * 8],
                         LN[k][:, tt * TT:(tt + 1) * TT], PS[b][0:8, :])
                        for k in range(DC)])
            consume("vector", b)
            vop(lambda e, tt=tt, b=b: e.tensor_scalar(
                out=LG[:, tt * TT:(tt + 1) * TT], in0=PS[b][0:8, :],
                scalar1=col["l0_bg"], scalar2=None, op0=ALU.add))
            release(b, sV)
        wr_free[0] = (sV, sV.n)

        LGT = S[1][:, 0:128]
        for j in range(16):
            bb = j % 2
            K.wait_pair("tensor", bank_free[bb])
            K.wait("tensor", sV)
            K.op("tensor", lambda e, j=j, bb=bb: e.transpose(
                PS[bb][:, 0:8], LG[:, j * 128:(j + 1) * 128], ident[0:8, 0:8]),
                inc=sT)
            bank_ready[bb] = (sT, sT.n)
            consume("vector", bb)
            vop(lambda e, j=j, bb=bb: e.tensor_copy(
                out=LGT[:, j * 8:(j + 1) * 8], in_=PS[bb][:, 0:8]))
            release(bb, sV)

        def b16(ap):
            return ap.unsqueeze(2).broadcast_to((128, 16, 8))
        v3 = LGT.rearrange("p (j e) -> p j e", e=8)
        MX = SM[:, 1664:1680]
        SUM2 = SM[:, 1680:1696]
        M1 = SM[:, 1696:1712]
        M2 = SM[:, 1712:1728]
        EX = S[1][:, 128:256]
        ex3 = EX.rearrange("p (j e) -> p j e", e=8)
        TMP = S[1][:, 256:384]
        PRE = S[1][:, 384:512]
        pre3 = PRE.rearrange("p (j e) -> p j e", e=8)
        def rtree(out2d, in3, op):
            t8 = SM[:, 1760:1824].rearrange("p (j e) -> p j e", e=4)
            t4 = SM[:, 1824:1856].rearrange("p (j e) -> p j e", e=2)
            vop(lambda e: e.tensor_tensor(out=t8, in0=in3[:, :, 0:4],
                                          in1=in3[:, :, 4:8], op=op))
            vop(lambda e: e.tensor_tensor(out=t4, in0=t8[:, :, 0:2],
                                          in1=t8[:, :, 2:4], op=op))
            vop(lambda e: e.tensor_tensor(out=out2d.unsqueeze(2), in0=t4[:, :, 0:1],
                                          in1=t4[:, :, 1:2], op=op))
        cstep = [0]
        def cgate():
            cstep[0] += 1
            return cstep[0] <= CSTEP
        if cgate(): rtree(MX, v3, ALU.max)
        if cgate(): vop(lambda e: e.tensor_tensor(out=pre3, in0=v3, in1=b16(MX),
                                      op=ALU.subtract))
        if cgate():
            K.wait("scalar", sV)
            sop(lambda e: e.activation(out=EX, in_=PRE, func=AF.Exp))
            K.wait("vector", sS)
        if cgate(): rtree(SUM2, ex3, ALU.add)
        if cgate():
            K.wait("scalar", sV)
            sop(lambda e: e.activation(out=SM[:, 1728:1744], in_=SUM2, func=AF.Ln))
            sop(lambda e: e.activation(out=SUM2, in_=SM[:, 1728:1744], func=AF.Exp,
                                   scale=f32(-1.0)))
            K.wait("vector", sS)
        if cgate(): vop(lambda e: e.tensor_tensor(out=ex3, in0=ex3, in1=b16(SUM2), op=ALU.mult))
        if cgate(): rtree(M1, ex3, ALU.max)
        tmp3 = TMP.rearrange("p (j e) -> p j e", e=8)
        if cgate(): vop(lambda e: e.tensor_tensor(out=tmp3, in0=ex3, in1=b16(M1), op=ALU.is_ge))
        if cgate(): vop(lambda e: e.scalar_tensor_tensor(out=TMP, in0=TMP, scalar=-2.0,
                                             in1=EX, op0=ALU.mult, op1=ALU.add))
        if cgate(): rtree(M2, tmp3, ALU.max)
        M3 = SM[:, 1744:1760]
        if cgate(): vop(lambda e: e.tensor_add(out=M3, in0=M1, in1=M2))
        if cgate():
            K.wait("scalar", sV)
            sop(lambda e: e.activation(out=SM[:, 1728:1744], in_=M3, func=AF.Ln))
            sop(lambda e: e.activation(out=M1, in_=SM[:, 1728:1744], func=AF.Exp,
                                   scale=f32(-1.0)))
            K.wait("vector", sS)
        if cgate(): vop(lambda e: e.tensor_scalar_mul(out=M1, in0=M1, scalar1=0.5))
        if cgate(): vop(lambda e: e.tensor_tensor(out=tmp3, in0=ex3, in1=b16(M2), op=ALU.is_ge))
        if cgate(): vop(lambda e: e.tensor_mul(out=TMP, in0=TMP, in1=EX))
        if cgate(): vop(lambda e: e.tensor_tensor(out=tmp3, in0=tmp3, in1=b16(M1), op=ALU.mult))
        CB = ST2X[0:8, 0:2048]
        for j in range(16):
            bb = j % 2
            K.wait_pair("tensor", bank_free[bb])
            K.wait("tensor", sV)
            K.op("tensor", lambda e, j=j, bb=bb: e.transpose(
                PS[bb][0:8, 0:128], TMP[:, j * 8:(j + 1) * 8], ident), inc=sT)
            bank_ready[bb] = (sT, sT.n)
            consume("vector", bb)
            vop(lambda e, j=j, bb=bb: e.tensor_copy(
                out=CB[:, j * 128:(j + 1) * 128], in_=PS[bb][0:8, 0:128]))
            release(bb, sV)
        barrier()

        if not moe_split_state["done"]:
            moe_split_weights()
            moe_split_state["done"] = True
        for k in range(DC):
            h_b = SM[:, 0:1024].bitcast(BF16)
            l32 = S[2][:, 0:2048]
            K.wait("vector", sT)
            vop(lambda e, k=k, h_b=h_b: e.tensor_copy(out=h_b, in_=LN[k][:, :]))
            vop(lambda e, k=k, h_b=h_b, l32=l32: e.tensor_tensor(
                out=l32, in0=LN[k][:, :], in1=h_b, op=ALU.subtract))
            lnb = LN[k].bitcast(BF16)
            vop(lambda e, k=k, h_b=h_b, lnb=lnb: e.tensor_copy(
                out=lnb[:, 0:2048], in_=h_b))
            vop(lambda e, k=k, l32=l32, lnb=lnb: e.tensor_copy(
                out=lnb[:, 2048:4096], in_=l32))
        barrier()

        def LNH(k, sl):
            return LN[k].bitcast(BF16)[:, sl]

        def LNL(k, sl):
            return LN[k].bitcast(BF16)[:, 2048 + sl.start:2048 + sl.stop]

        SG = SM[:, 0:512]
        CBXF = STIX[:, 0:2048]
        W2SC = SM[:, 512:1024]
        IND8 = SM[0:8, 1536:1664]

        def AE(mp, t2):
            t = S[mp // 2]
            base = (mp % 2) * 1024 + t2 * 512
            return t[:, base:base + 512]

        for e_idx in range(8 if not SKIP_EXPERTS else 0):
            K.wait("vector", sT)
            vop(lambda e, e_idx=e_idx: e.tensor_scalar_mul(
                out=IND8, in0=ones[0:8, 0:128],
                scalar1=col["oh8"][0:8, e_idx:e_idx + 1]))
            for tt in range(NTT):
                sl = slice(tt * TT, (tt + 1) * TT)
                bb = 4 + tt % 2
                K.wait_pair("tensor", bank_free[bb])
                K.wait("tensor", sV)
                K.op("tensor", lambda e, sl=sl, bb=bb: e.matmul(
                    PS[bb][:], IND8, CB[0:8, sl], start=True, stop=True),
                    inc=sT)
                bank_ready[bb] = (sT, sT.n)
                consume("vector", bb)
                vop(lambda e, sl=sl, bb=bb: e.tensor_copy(
                    out=CBXF[:, sl], in_=PS[bb][:]))
                release(bb, sV)
            for half in range(2):
                K.wait("vector", sT)
                for mp in range(8):
                    slot_g, slot_u = (mp % 2) * 2, (mp % 2) * 2 + 1
                    wr_load(slot_g, mw1h[e_idx, mp])
                    wr_load(slot_g, mw1l[e_idx, mp], off=512)
                    wr_load(slot_u, mw1h[e_idx, mp + 8])
                    wr_load(slot_u, mw1l[e_idx, mp + 8], off=512)
                    wr_wait("tensor", slot_g, slot_u)
                    for t2 in range(2):
                        sl = slice(half * 1024 + t2 * TT,
                                   half * 1024 + (t2 + 1) * TT)
                        bg_, bu_ = t2, 2 + t2
                        mm3g = []
                        mm3u = []
                        for k in range(DC):
                            mm3g += [(wr_bt(slot_g, k, 0), LNH(k, sl),
                                      PS[bg_][:]),
                                     (wr_bt(slot_g, k, 0), LNL(k, sl),
                                      PS[bg_][:]),
                                     (wr_bt(slot_g, k, 1), LNH(k, sl),
                                      PS[bg_][:])]
                            mm3u += [(wr_bt(slot_u, k, 0), LNH(k, sl),
                                      PS[bu_][:]),
                                     (wr_bt(slot_u, k, 0), LNL(k, sl),
                                      PS[bu_][:]),
                                     (wr_bt(slot_u, k, 1), LNH(k, sl),
                                      PS[bu_][:])]
                        mmgroup(bg_, mm3g)
                        mmgroup(bu_, mm3u)
                        consume("scalar", bg_)
                        K.wait("scalar", sV)
                        sop(lambda e, bg_=bg_: e.activation(
                            out=SG, in_=PS[bg_][:], func=AF.Silu))
                        release(bg_, sS)
                        consume("vector", bu_)
                        K.wait("vector", sS)
                        vop(lambda e, mp=mp, bu_=bu_, t2=t2: e.tensor_mul(
                            out=W2SC, in0=PS[bu_][:], in1=SG))
                        release(bu_, sV)
                        aehl = AE(mp, t2).bitcast(BF16)
                        vop(lambda e, aehl=aehl: e.tensor_copy(
                            out=aehl[:, 0:512], in_=W2SC))
                        vop(lambda e, aehl=aehl: e.tensor_tensor(
                            out=SM[:, 1024:1536], in0=W2SC,
                            in1=aehl[:, 0:512], op=ALU.subtract))
                        vop(lambda e, aehl=aehl: e.tensor_copy(
                            out=aehl[:, 512:1024], in_=SM[:, 1024:1536]))
                    wr_consumed(slot_g)
                    wr_consumed(slot_u)
                K.wait("tensor", sV)
                for m2 in range(8):
                    slot = 2 + m2 % 2
                    wr_load(slot, mw2h[e_idx, m2])
                    wr_load(slot, mw2l[e_idx, m2], off=512)
                    wr_wait("tensor", slot)
                    for t2 in range(2):
                        sl = slice(half * 1024 + t2 * TT,
                                   half * 1024 + (t2 + 1) * TT)
                        b = 4 + t2
                        mm3 = []
                        for k in range(DC):
                            ab = AE(k, t2).bitcast(BF16)
                            mm3 += [(wr_bt(slot, k, 0), ab[:, 0:512],
                                     PS[b][:]),
                                    (wr_bt(slot, k, 0), ab[:, 512:1024],
                                     PS[b][:]),
                                    (wr_bt(slot, k, 1), ab[:, 0:512],
                                     PS[b][:])]
                        mmgroup(b, mm3)
                        consume("vector", b)
                        vop(lambda e, b=b, sl=sl: e.tensor_mul(
                            out=W2SC, in0=PS[b][:], in1=CBXF[:, sl]))
                        vop(lambda e, m2=m2, sl=sl: e.tensor_add(
                            out=X[m2][:, sl], in0=X[m2][:, sl], in1=W2SC))
                        release(b, sV)
                    wr_consumed(slot)
        barrier()


    def ln_tile(src_c, dst_c, nch, cols, eps, banks, sqs, g_col=None, b_col=None,
                nfeat=1024.0):
        b0, b1 = banks
        K.wait("tensor", sV)
        K.wait("tensor", sS)
        mmgroup(b0, [(ones, src_c(k), PS[b0][:, 0:cols]) for k in range(nch)])
        consume("vector", b0)
        for k in range(nch):
            vop(lambda e, k=k: e.scalar_tensor_tensor(
                out=dst_c(k), in0=PS[b0][:, 0:cols], scalar=-1.0 / nfeat,
                in1=src_c(k), op0=ALU.mult, op1=ALU.add))
        release(b0, sV)
        K.wait("scalar", sV)
        K.wait_pair("tensor", bank_free[b1])
        for k in range(nch):
            sop(lambda e, k=k: e.activation(out=sqs(k), in_=dst_c(k),
                                            func=AF.Square))
            K.wait("tensor", sS)
            K.op("tensor", lambda e, k=k: e.matmul(
                PS[b1][:, 0:cols], ones, sqs(k), start=(k == 0),
                stop=(k == nch - 1)), inc=(sT if k == nch - 1 else None))
        bank_ready[b1] = (sT, sT.n)
        consume("scalar", b1)
        K.wait("scalar", sV)
        sop(lambda e: e.activation(out=SM[:, 0:cols], in_=PS[b1][:, 0:cols],
                                   func=AF.Ln, scale=f32(1.0 / nfeat),
                                   bias=f32(eps)))
        release(b1, sS)
        sop(lambda e: e.activation(out=SM[:, 512:512 + cols], in_=SM[:, 0:cols],
                                   func=AF.Exp, scale=f32(-0.5)))
        K.wait("vector", sS)
        for k in range(nch):
            vop(lambda e, k=k: e.tensor_mul(out=dst_c(k), in0=dst_c(k),
                                            in1=SM[:, 512:512 + cols]))
            if g_col is not None:
                vop(lambda e, k=k: e.tensor_scalar(
                    out=dst_c(k), in0=dst_c(k), scalar1=g_col[:, k:k + 1],
                    scalar2=b_col[:, k:k + 1], op0=ALU.mult, op1=ALU.add))

    def l0_attn_block():
        layernorm(X, LN, 1e-5, sq2=(S[0], S[1]))
        barrier()
        smA = SM[:, 1024:1536]
        smB = SM[:, 1536:1920]
        smB = ST2X[:, 0:512]
        smC = ST2X, None

        def elu_norm(dst_c, bias_col, wcol_src, sl):
            for m in range(DC):
                slot = m % 2
                wr_load(slot, wcol_src(m))
                wr_wait("tensor", slot)
                b = m % 2
                mmgroup(b, [(wr_tile(slot, k), LN[k][:, sl], PS[b][:])
                            for k in range(DC)])
                wr_consumed(slot)
                consume("vector", b)
                vop(lambda e, m=m, b=b: e.tensor_scalar(
                    out=dst_c(m), in0=PS[b][:], scalar1=bias_col[:, m:m + 1],
                    scalar2=None, op0=ALU.add))
                release(b, sV)
            smC = ST2X[:, 512:1024]
            for m in range(DC):
                K.wait("vector", sS)
                vop(lambda e, m=m: e.tensor_single_scalar(
                    out=smA, in_=dst_c(m), scalar=0.0, op=ALU.min))
                vop(lambda e, m=m: e.tensor_single_scalar(
                    out=smC, in_=dst_c(m), scalar=0.0, op=ALU.max))
                K.wait("scalar", sV)
                sop(lambda e: e.activation(out=smB, in_=smA, func=AF.Exp))
                K.wait("vector", sS)
                vop(lambda e, m=m: e.scalar_tensor_tensor(
                    out=dst_c(m), in0=smB, scalar=-1.0, in1=smC,
                    op0=ALU.add, op1=ALU.add))
                K.wait("scalar", sV)
                sop(lambda e, m=m: e.activation(out=smB, in_=dst_c(m),
                                                func=AF.Square))
                b = 2 + m % 2
                K.wait_pair("tensor", bank_free[b])
                K.wait("tensor", sS)
                K.op("tensor", lambda e, b=b: e.matmul(
                    PS[b][:], bd64, smB, start=True, stop=True), inc=sT)
                bank_ready[b] = (sT, sT.n)
                consume("scalar", b)
                K.wait("scalar", sT)
                sop(lambda e, b=b: e.activation(out=smA, in_=PS[b][:],
                                                func=AF.Ln))
                release(b, sS)
                sop(lambda e: e.activation(out=smC, in_=smA, func=AF.Exp,
                                           scale=f32(-0.5)))
                K.wait("vector", sS)
                vop(lambda e, m=m: e.tensor_mul(out=dst_c(m), in0=dst_c(m),
                                                in1=smC))

        def S4(i, c, sl0=0):
            t = S[i + c // 4]
            off = (c % 4) * 512
            return t[:, off:off + 512]

        KTT = STIX[:, 0:1024]
        VTT = STIX[:, 1024:2048]
        kv_started = [False]
        for tt in range(NTT):
            sl = slice(tt * TT, (tt + 1) * TT)
            elu_norm(lambda c: S4(0, c), col["l0_bk"],
                     lambda m: dr["l0_Wk"][m], sl)
            for m in range(DC):
                slot = m % 2
                wr_load(slot, dr["l0_Wv"][m])
                wr_wait("tensor", slot)
                b = m % 2
                mmgroup(b, [(wr_tile(slot, k), LN[k][:, sl], PS[b][:])
                            for k in range(DC)])
                wr_consumed(slot)
                consume("vector", b)
                vop(lambda e, m=m, b=b: e.tensor_scalar(
                    out=S4(2, m), in0=PS[b][:], scalar1=col["l0_bv"][:, m:m + 1],
                    scalar2=None, op0=ALU.add))
                release(b, sV)
            for s in range(4):
                for c in range(DC):
                    for which, srcv, dstv in ((0, S4(0, c), KTT), (1, S4(2, c), VTT)):
                        b = 2 + (c + which) % 2
                        K.wait_pair("tensor", bank_free[b])
                        K.wait("tensor", sV)
                        K.op("tensor", lambda e, b=b, srcv=srcv, s=s: e.transpose(
                            PS[b][:, 0:128], srcv[:, s * 128:(s + 1) * 128],
                            ident), inc=sT)
                        bank_ready[b] = (sT, sT.n)
                        consume("vector", b)
                        vop(lambda e, b=b, dstv=dstv, c=c: e.tensor_copy(
                            out=dstv[:, c * 128:(c + 1) * 128],
                            in_=PS[b][:, 0:128]))
                        release(b, sV)
                K.wait("tensor", sV)
                first = not kv_started[0]
                kv_started[0] = True
                last = (tt == NTT - 1 and s == 3)
                for c in range(DC):
                    bank = 6 + c // 4
                    fp = (c % 4) * 128
                    stt_ = first and (c % 4 == 0)
                    K.op("tensor", lambda e, c=c, bank=bank, fp=fp, stt_=stt_,
                         last=last: e.matmul(
                             PS[bank][:, fp:fp + 128],
                             KTT[:, c * 128:(c + 1) * 128],
                             VTT[:, c * 128:(c + 1) * 128],
                             start=stt_, stop=(last and c % 4 == 3),
                             skip_group_check=True),
                         inc=(sT if (last and c == DC - 1) else None))
        bank_ready[6] = (sT, sT.n)
        bank_ready[7] = (sT, sT.n)

        BDS = STIX[:, 0:1024]
        consume("vector", 6)
        consume("vector", 7)
        vop(lambda e: e.memset(BDS, 0.0))
        for c in range(DC):
            bank = 6 + c // 4
            base = (c % 4) * 128
            vop(lambda e, bank=bank, base=base, c=c: e.tensor_copy(
                out=BDS[0:64, c * 128:c * 128 + 64],
                in_=PS[bank][0:64, base:base + 64]))
            vop(lambda e, bank=bank, base=base, c=c: e.tensor_copy(
                out=BDS[64:128, c * 128 + 64:c * 128 + 128],
                in_=PS[bank][64:128, base + 64:base + 128]))
        release(6, sV)
        release(7, sV)
        KV2 = ST2X[:, 0:2048]
        vop(lambda e: e.tensor_scalar(out=KV2[:, 0:1024], in0=BDS,
                                      scalar1=col["bmask"][:, 0:1], scalar2=None,
                                      op0=ALU.mult))
        vop(lambda e: e.tensor_scalar(out=KV2[:, 1024:2048], in0=BDS,
                                      scalar1=col["bmask"][:, 1:2], scalar2=None,
                                      op0=ALU.mult))
        K.wait("gpsimd", sV)
        K.op("gpsimd", lambda e: e.dma_start(out=kv_in, in_=KV2), inc=sG, k=16)
        K.wait("gpsimd", sG)
        K.op("gpsimd", lambda e: e.collective_compute(
            "AllReduce", ALU.add, replica_groups=[list(range(8))],
            ins=[kv_in.opt()], outs=[kv_out.opt()]), inc=sCC)
        K.wait("gpsimd", sCC)
        K.op("gpsimd", lambda e: e.dma_start(out=KV2, in_=kv_out), inc=sG, k=16)
        barrier()
        vop(lambda e: e.tensor_scalar(out=BDS, in0=KV2[:, 0:1024],
                                      scalar1=col["bmask"][:, 0:1], scalar2=None,
                                      op0=ALU.mult))
        vop(lambda e: e.scalar_tensor_tensor(out=BDS, in0=KV2[:, 1024:2048],
                                             scalar=col["bmask"][:, 1:2],
                                             in1=BDS, op0=ALU.mult, op1=ALU.add))
        barrier()

        if ASTEP <= 1:
            return
        for tt in range(NTT):
            sl = slice(tt * TT, (tt + 1) * TT)
            elu_norm(lambda c: S4(0, c), col["l0_bq"],
                     lambda m: dr["l0_Wq"][m], sl)
            for c in range(DC):
                b = 4 + c % 2
                K.wait_pair("tensor", bank_free[b])
                K.wait("tensor", sV)
                K.op("tensor", lambda e, c=c, b=b: e.matmul(
                    PS[b][:], BDS[:, c * 128:(c + 1) * 128], S4(0, c),
                    start=True, stop=True), inc=sT)
                bank_ready[b] = (sT, sT.n)
                consume("vector", b)
                vop(lambda e, c=c, b=b: e.tensor_scalar(
                    out=S4(2, c), in0=PS[b][:], scalar1=0.125, scalar2=None,
                    op0=ALU.mult))
                release(b, sV)
            if ASTEP <= 2:
                return
            for m in range(DC):
                slot = m % 2
                wr_load(slot, dr["l0_Wo"][m])
                wr_wait("tensor", slot)
                K.wait("tensor", sV)
                b = m % 2
                mmgroup(b, [(wr_tile(slot, k), S4(2, k), PS[b][:])
                            for k in range(DC)])
                wr_consumed(slot)
                consume("vector", b)
                vop(lambda e, m=m, b=b, sl=sl: e.scalar_tensor_tensor(
                    out=S4(0, m), in0=PS[b][:], scalar=col["l0_bo"][:, m:m + 1],
                    in1=LN[m][:, sl], op0=ALU.add, op1=ALU.add))
                release(b, sV)
            if ASTEP <= 3:
                return
            ln_tile(lambda c: S4(0, c), lambda c, sl=sl: LN[c][:, sl], DC, TT,
                    1e-12, (2, 3), lambda c: S4(2, c), col["l0_g1"],
                    col["l0_be1"])
            barrier()
            if ASTEP <= 4:
                return
            for half in range(2):
                cols = 256
                c0 = tt * TT + half * 256

                def AO2(c, c0=c0):
                    return LN[c][:, c0:c0 + cols]

                def H1(m):
                    t = S[2 + m // 8]
                    off = (m % 8) * 256
                    return t[:, off:off + 256]

                K.wait("scalar", sV)
                for g in range(2):
                    for mi in range(16):
                        m = g * 16 + mi
                        slot = mi % 2
                        wr_load(slot, dr["l0_W1"][m])
                        wr_wait("tensor", slot)
                        b = mi % 2
                        K.wait_pair("tensor", bank_free[b])
                        for k in range(DC):
                            K.op("tensor", lambda e, k=k, b=b, slot=slot,
                                 AO2=AO2: e.matmul(
                                PS[b][:, 0:cols], wr_tile(slot, k), AO2(k),
                                start=(k == 0), stop=(k == DC - 1)),
                                inc=(sT if k == DC - 1 else None))
                        bank_ready[b] = (sT, sT.n)
                        wr_consumed(slot)
                        consume("scalar", b)
                        sop(lambda e, mi=mi, b=b, m=m: e.activation(
                            out=H1(mi), in_=PS[b][:, 0:cols], func=AF.Gelu,
                            bias=col["l0_b1"][:, m:m + 1]))
                        release(b, sS)
                    for m2 in range(DC):
                        bank = 4 + m2 // 2
                        fof = (m2 % 2) * 256
                        if g == 0:
                            K.wait_pair("tensor", bank_free[bank])
                        for sub in range(2):
                            slot = 2 + (m2 * 2 + sub) % 2
                            wr_load(slot, dr["l0_W2"][m2]
                                    [:, g * 16 + sub * 8:g * 16 + sub * 8 + 8, :])
                            wr_wait("tensor", slot)
                            if sub == 0:
                                K.wait("tensor", sS)
                            for k2 in range(8):
                                last = (g == 1 and sub == 1 and k2 == 7)
                                K.op("tensor", lambda e, k2=k2, sub=sub, bank=bank,
                                     fof=fof, slot=slot, g=g, last=last,
                                     m2=m2: e.matmul(
                                         PS[bank][:, fof:fof + 256],
                                         wr_tile(slot, k2),
                                         H1(sub * 8 + k2),
                                         start=(g == 0 and sub == 0 and k2 == 0
                                                and m2 % 2 == 0),
                                         stop=last, skip_group_check=True),
                                    inc=(sT if k2 == 7 else None))
                            wr_consumed(slot)
                for bank in (4, 5, 6, 7):
                    bank_ready[bank] = (sT, sT.n)
                for m2 in range(DC):
                    bank = 4 + m2 // 2
                    fof = (m2 % 2) * 256
                    consume("vector", bank)
                    vop(lambda e, m2=m2, bank=bank, fof=fof, AO2=AO2:
                        e.scalar_tensor_tensor(
                        out=S[2][:, m2 * 256:(m2 + 1) * 256],
                        in0=PS[bank][:, fof:fof + 256],
                        scalar=col["l0_b2"][:, m2:m2 + 1], in1=AO2(m2),
                        op0=ALU.add, op1=ALU.add))
                for bank in (4, 5, 6, 7):
                    release(bank, sV)
                ln_tile(lambda c: S[2][:, c * 256:(c + 1) * 256],
                        lambda c: S[3][:, c * 256:(c + 1) * 256], DC, 256,
                        1e-12, (2, 3), lambda c: S[1][:, c * 256:(c + 1) * 256],
                        col["l0_g2"], col["l0_be2"])
                for m in range(DC):
                    vop(lambda e, m=m, c0=c0: e.tensor_add(
                        out=X[m][:, c0:c0 + cols], in0=X[m][:, c0:c0 + cols],
                        in1=S[3][:, m * 256:(m + 1) * 256]))
                if ASTEP >= 5 and ASTEP < 99 and tt * 2 + half + 1 >= ASTEP - 4:
                    return
            barrier()
        barrier()



    def emit_round(out, vin, iview, ef, gf):
        vop(lambda e: e.tensor_copy(out=iview, in_=vin))
        vop(lambda e: e.tensor_copy(out=out, in_=iview))
        vop(lambda e: e.tensor_tensor(out=ef, in0=vin, in1=out, op=ALU.subtract))
        vop(lambda e: e.tensor_single_scalar(out=gf, in_=ef, scalar=0.5,
                                             op=ALU.is_ge))
        vop(lambda e: e.tensor_add(out=out, in0=out, in1=gf))
        vop(lambda e: e.tensor_single_scalar(out=gf, in_=ef, scalar=-0.5,
                                             op=ALU.is_lt))
        vop(lambda e: e.tensor_tensor(out=out, in0=out, in1=gf, op=ALU.subtract))

    moe_split_state = {"done": False}

    def moe_split_weights():
        hi = STIX[:, 0:512].bitcast(BF16)
        lo32 = STIX[:, 512:1536]
        lo = STIX[:, 1536:2048].bitcast(BF16)
        for e_idx in range(8):
            cols = ([(mc, dr["moe_w1"][e_idx, mc], mw1h[e_idx, mc],
                      mw1l[e_idx, mc]) for mc in range(16)] +
                    [(mc, dr["moe_w2"][e_idx, mc], mw2h[e_idx, mc],
                      mw2l[e_idx, mc]) for mc in range(8)])
            for mc, srcap, dsth, dstl in cols:
                slot = mc % 2
                wr_load(slot, srcap)
                wr_wait("vector", slot)
                K.wait("vector", sG)
                vop(lambda e, slot=slot: e.tensor_copy(
                    out=hi, in_=WR[:, slot * 1024:(slot + 1) * 1024]))
                vop(lambda e, slot=slot: e.tensor_tensor(
                    out=lo32, in0=WR[:, slot * 1024:(slot + 1) * 1024],
                    in1=hi, op=ALU.subtract))
                wr_free[slot] = (sV, sV.n)
                vop(lambda e: e.tensor_copy(out=lo, in_=lo32))
                K.wait("gpsimd", sV)
                K.op("gpsimd", lambda e, d=dsth.rearrange("p a q -> p (a q)"),
                     hi=hi: e.dma_start(out=d, in_=hi), inc=sG, k=16)
                K.op("gpsimd", lambda e, d=dstl.rearrange("p a q -> p (a q)"),
                     lo=lo: e.dma_start(out=d, in_=lo), inc=sG, k=16)
        K.wait("vector", sG)
        barrier()

    l1q_state = {"done": False}
    XQH = None

    def l1_quant_weights():
        for wi, (wsrc, wdst, ncol, scol) in enumerate((
                (dr["f1_w1T"], w1q_d, 32, 32),
                (dr["f1_w2T"], w2q_d, 8, 33))):
            nslot = 1024
            acc = XTRA[:, 40:41]
            K.wait("vector", sG)
            vop(lambda e, acc=acc: e.memset(acc, 0.0))
            if wi == 0:
                vop(lambda e, acc=acc: e.tensor_copy(out=XTRA[:, 48:49], in_=acc))
            cols_list = []
            for mc in range(ncol):
                sub = wsrc[mc]
                if sub.shape[1] * sub.shape[2] == 1024:
                    parts = [sub]
                else:
                    parts = [sub[:, q * 8:(q + 1) * 8, :]
                             for q in range(sub.shape[1] // 8)]
                for pt in parts:
                    cols_list.append(pt)
            for ci, pt in enumerate(cols_list):
                slot = ci % 2
                wr_load(slot, pt)
                wr_wait("vector", slot)
                vop(lambda e, slot=slot: e.tensor_reduce(
                    out=XTRA[:, 41:42], in_=WR[:, slot * 1024:(slot + 1) * 1024],
                    axis=mybir.AxisListType.X, op=ALU.add,
                    apply_absolute_value=True))
                vop(lambda e, acc=acc: e.tensor_add(out=acc, in0=acc,
                                                    in1=XTRA[:, 41:42]))
                if wi == 0 and ci == 0:
                    vop(lambda e, acc=acc: e.tensor_copy(out=XTRA[:, 49:50], in_=acc))
                if wi == 0 and ci == len(cols_list) - 1:
                    vop(lambda e, acc=acc: e.tensor_copy(out=XTRA[:, 50:51], in_=acc))
                wr_free[slot] = (sV, sV.n)
            b = 0
            K.wait_pair("tensor", bank_free[b])
            K.wait("tensor", sV)
            K.op("tensor", lambda e, b=b, acc=acc: e.matmul(
                PS[b][:, 0:1], ones, acc, start=True, stop=True), inc=sT)
            bank_ready[b] = (sT, sT.n)
            consume("vector", b)
            nelem = float(len(cols_list) * 1024 * 128)
            vop(lambda e, b=b, nelem=nelem: e.tensor_scalar(
                out=XTRA[:, 42:43], in0=PS[b][:, 0:1], scalar1=1.0 / nelem,
                scalar2=1e-5, op0=ALU.mult, op1=ALU.max))
            release(b, sV)
            swi = XTRA[:, scol:scol + 1]
            vop(lambda e, swi=swi: e.tensor_copy(out=swi, in_=XTRA[:, 42:43]))
            K.wait("scalar", sV)
            sop(lambda e: e.activation(out=XTRA[:, 43:44], in_=XTRA[:, 42:43],
                                       func=AF.Ln))
            sop(lambda e: e.activation(out=XTRA[:, 44:45], in_=XTRA[:, 43:44],
                                       func=AF.Exp, scale=f32(-1.0)))
            K.wait("vector", sS)
            if FSTEP == 0 and wi == 0:
                return
            ti32 = ST2X[:, 0:1024].bitcast(I32)
            tf = ST2X[:, 1024:2048]
            for ci, pt in enumerate(cols_list):
                slot = ci % 2
                wr_load(slot, pt)
                wr_wait("vector", slot)
                vop(lambda e, slot=slot: e.tensor_scalar(
                    out=tf, in0=WR[:, slot * 1024:(slot + 1) * 1024],
                    scalar1=XTRA[:, 44:45], scalar2=None, op0=ALU.mult))
                wr_free[slot] = (sV, sV.n)
                emit_round(ST2X[:, 0:1024], tf, SM[:, 0:1024].bitcast(I32),
                           STIX[:, 0:1024], STIX[:, 1024:2048])
                vop(lambda e: e.tensor_scalar(
                    out=tf, in0=ST2X[:, 0:1024], scalar1=-1.0, scalar2=1.0,
                    op0=ALU.max, op1=ALU.min))
                bq = WR[:, 2048:2560].bitcast(BF16)
                K.wait("vector", sG)
                vop(lambda e, bq=bq: e.tensor_copy(out=bq, in_=tf))
                dst = (wdst[ci // (1024 // 1024)] if False else None)
                K.wait("gpsimd", sV)
                if wi == 0:
                    ddst = w1q_d[ci].rearrange("p a q -> p (a q)")
                else:
                    mc2, qq = ci // 4, ci % 4
                    ddst = w2q_d[mc2][:, qq * 8:(qq + 1) * 8, :].rearrange(
                        "p a q -> p (a q)")
                K.op("gpsimd", lambda e, bq=bq, ddst=ddst: e.dma_start(
                    out=ddst, in_=bq), inc=sG, k=16)
        K.wait("vector", sG)
        barrier()

    def l1_ffn_block():
        layernorm(X, LN, 1e-5, sq2=(S[0], S[1]))
        barrier()
        if not l1q_state["done"]:
            l1_quant_weights()
            l1q_state["done"] = True
        if FSTEP <= 1:
            return
        cws = [XTRA[:, 45 + t:46 + t] for t in range(3)]
        CW = [SM[:, 1792 + 32 * t:1792 + 32 * (t + 1)] for t in range(3)]
        for t in range(3):
            vop(lambda e, t=t: e.tensor_scalar(
                out=CW[t], in0=col[f"f1_cw{t}"], scalar1=XTRA[:, 32:33],
                scalar2=None, op0=ALU.mult))
        SW2I05 = XTRA[:, 35:36]
        vop(lambda e: e.tensor_scalar(out=SW2I05, in0=XTRA[:, 33:34],
                                      scalar1=0.5, scalar2=None, op0=ALU.mult))

        AM = ST2X[:, 0:2048]
        AMS = STIX[:, 0:2048]
        K.wait("scalar", sV)
        sop(lambda e: e.activation(out=AM, in_=LN[0][:, :], func=AF.Abs))
        K.wait("vector", sS)
        for k in range(1, DC):
            K.wait("scalar", sV)
            sop(lambda e, k=k: e.activation(out=AMS, in_=LN[k][:, :],
                                            func=AF.Abs))
            K.wait("vector", sS)
            vop(lambda e: e.tensor_max(out=AM, in0=AM, in1=AMS))
        SCOL = XTRA[:, 0:16]
        for j in range(16):
            b = j % 2
            K.wait_pair("tensor", bank_free[b])
            K.wait("tensor", sV)
            K.op("tensor", lambda e, j=j, b=b: e.transpose(
                PS[b][:, 0:128], AM[:, j * 128:(j + 1) * 128], ident), inc=sT)
            bank_ready[b] = (sT, sT.n)
            consume("vector", b)
            vop(lambda e, j=j, b=b: e.tensor_reduce(
                out=SCOL[:, j:j + 1], in_=PS[b][:, 0:128],
                axis=mybir.AxisListType.X, op=ALU.max))
            release(b, sV)
        MXC = XTRA[:, 16:32]
        vop(lambda e: e.tensor_scalar_max(out=MXC, in0=SCOL, scalar1=1e-5))
        K.wait("scalar", sV)
        sop(lambda e: e.activation(out=XTRA[:, 40:56], in_=MXC, func=AF.Ln))
        sop(lambda e: e.activation(out=XTRA[:, 40:56], in_=XTRA[:, 40:56],
                                   func=AF.Exp, scale=f32(-1.0))
            if False else e.activation(out=SCOL, in_=XTRA[:, 40:56],
                                       func=AF.Exp, scale=f32(-1.0)))
        K.wait("vector", sS)
        ST = XTRA[:, 40:56]
        vop(lambda e: e.tensor_scalar(out=ST, in0=SCOL, scalar1=127.0,
                                      scalar2=None, op0=ALU.mult))
        STI = XTRA[:, 56:72] if False else SCOL
        vop(lambda e: e.tensor_scalar(out=STI, in0=MXC, scalar1=1.0 / 127.0,
                                      scalar2=None, op0=ALU.mult))
        SROW = ST2X[0:1, 0:2048]
        SIROW = STIX[0:1, 0:2048]
        for j in range(16):
            b = j % 2
            K.wait_pair("tensor", bank_free[b])
            K.wait("tensor", sV)
            K.op("tensor", lambda e, j=j, b=b: e.matmul(
                PS[b][0:1, 0:128], ST[:, j:j + 1], ident, start=True,
                stop=True), inc=sT)
            bank_ready[b] = (sT, sT.n)
            consume("vector", b)
            vop(lambda e, j=j, b=b: e.tensor_copy(
                out=SROW[:, j * 128:(j + 1) * 128], in_=PS[b][0:1, 0:128]))
            release(b, sV)
            b2 = 2 + j % 2
            K.wait_pair("tensor", bank_free[b2])
            K.wait("tensor", sV)
            K.op("tensor", lambda e, j=j, b2=b2: e.matmul(
                PS[b2][0:1, 0:128], STI[:, j:j + 1], ident, start=True,
                stop=True), inc=sT)
            bank_ready[b2] = (sT, sT.n)
            consume("vector", b2)
            vop(lambda e, j=j, b2=b2: e.tensor_copy(
                out=SIROW[:, j * 128:(j + 1) * 128], in_=PS[b2][0:1, 0:128]))
            release(b2, sV)
        for ttb in range(NTT):
            b = 4 + ttb % 2
            K.wait_pair("tensor", bank_free[b])
            K.wait("tensor", sV)
            K.op("tensor", lambda e, ttb=ttb, b=b: e.matmul(
                PS[b][:], ones[0:1, :], SIROW[:, ttb * TT:(ttb + 1) * TT],
                start=True, stop=True), inc=sT)
            bank_ready[b] = (sT, sT.n)
            consume("vector", b)
            vop(lambda e, ttb=ttb, b=b: e.tensor_copy(
                out=STIX[:, ttb * TT:(ttb + 1) * TT], in_=PS[b][:]))
            release(b, sV)
        def XQv(k):
            t = S[k // 2].bitcast(BF16)
            off = (k % 2) * 2048
            return t[:, off:off + 2048]
        QSF = SM[:, 0:256]
        QSI = SM[:, 256:512].bitcast(I32)[:, 0:256]
        QRO = SM[:, 512:768]
        QRE = SM[:, 768:1024]
        QRG = SM[:, 1024:1280]
        for ttb in range(NTT):
            b = 6 + ttb % 2
            K.wait_pair("tensor", bank_free[b])
            K.wait("tensor", sV)
            K.op("tensor", lambda e, ttb=ttb, b=b: e.matmul(
                PS[b][:], ones[0:1, :], SROW[:, ttb * TT:(ttb + 1) * TT],
                start=True, stop=True), inc=sT)
            bank_ready[b] = (sT, sT.n)
            consume("vector", b)
            for k in range(DC):
                for hh in range(2):
                    sl = slice(ttb * TT + hh * 256, ttb * TT + hh * 256 + 256)
                    psl = slice(hh * 256, hh * 256 + 256)
                    vop(lambda e, k=k, sl=sl, b=b, psl=psl: e.tensor_tensor(
                        out=QSF, in0=LN[k][:, sl], in1=PS[b][:, psl],
                        op=ALU.mult))
                    emit_round(QRO, QSF, QSI, QRE, QRG)
                    vop(lambda e, k=k, sl=sl: e.tensor_copy(
                        out=XQv(k)[:, sl], in_=QRO))
            release(b, sV)
        if FSTEP <= 2:
            return
        hcall = 0 if not l1q_state.get("second") else 1
        l1q_state["second"] = True
        CONTRIB = SM[:, 1024:1042]
        for k in range(DC):
            vop(lambda e, k=k: e.tensor_copy(out=CONTRIB[:, k:k + 1],
                                             in_=XQv(k)[:, 0:1]))
            vop(lambda e, k=k: e.tensor_copy(out=CONTRIB[:, 8 + k:9 + k],
                                             in_=XQv(k)[:, 2047:2048]))
        vop(lambda e: e.tensor_copy(out=CONTRIB[:, 16:17], in_=STIX[:, 0:1]))
        vop(lambda e: e.tensor_copy(out=CONTRIB[:, 17:18],
                                    in_=STIX[:, 2047:2048]))
        K.wait("gpsimd", sV)
        K.op("gpsimd", lambda e, h=hcall: e.dma_start(out=hl_in[h], in_=CONTRIB),
             inc=sG, k=16)
        K.wait("gpsimd", sG)
        K.op("gpsimd", lambda e, h=hcall: e.collective_compute(
            "AllGather", ALU.bypass, replica_groups=[list(range(8))],
            ins=[hl_in[h].opt()], outs=[hl_out[h].opt()]), inc=sCC)
        K.wait("gpsimd", sCC)
        GH = SM[:, 0:1152]
        for j in range(8):
            K.op("gpsimd", lambda e, j=j, h=hcall: e.dma_start(
                out=GH[:, j * 144:j * 144 + 18],
                in_=hl_out[h][j * 128:(j + 1) * 128, :]), inc=sG, k=16)
        barrier()
        HL = SM[:, 1200:1218]
        HR = SM[:, 1218:1236]
        vop(lambda e: e.memset(HL, 0.0))
        vop(lambda e: e.memset(HR, 0.0))
        for j in range(8):
            vop(lambda e, j=j: e.scalar_tensor_tensor(
                out=HL, in0=GH[:, j * 144:j * 144 + 18],
                scalar=col["mhL"][:, j:j + 1], in1=HL, op0=ALU.mult,
                op1=ALU.add))
            vop(lambda e, j=j: e.scalar_tensor_tensor(
                out=HR, in0=GH[:, j * 144:j * 144 + 18],
                scalar=col["mhR"][:, j:j + 1], in1=HR, op0=ALU.mult,
                op1=ALU.add))
        for k in range(DC):
            vop(lambda e, k=k: e.tensor_copy(out=XQHB[:, k:k + 1],
                                             in_=HL[:, 8 + k:9 + k]))
            vop(lambda e, k=k: e.tensor_copy(out=XQHB[:, 8 + k:9 + k],
                                             in_=HR[:, k:k + 1]))
        barrier()

        if FSTEP <= 3:
            return
        def SV(m):
            return LN[m // 8][:, (m % 8) * 256:(m % 8) * 256 + 256]

        def Q2(m):
            t = LN[m // 8].bitcast(BF16)
            off = (m % 8) * 512
            return t[:, off:off + 256]

        TV = SM[:, 0:258]
        CVt = SM[:, 512:768]
        AM2 = ST2X[:, 1536:1792]
        for tti in range(8):
            t0 = tti * 256
            for mg in range(4):
                for mi in range(DC):
                    m = mg * 8 + mi
                    slot = mi % 2
                    wr_load(slot, w1q_d[m])
                    wr_wait("tensor", slot)
                    bk = mi
                    K.wait_pair("tensor", bank_free[bk])
                    wt = wrb(slot, 1024)
                    if tti == 0:
                        for k in range(DC):
                            K.op("tensor", lambda e, k=k, bk=bk, wt=wt: e.matmul(
                                PS[bk][:, 1:258], wt[:, k * 128:(k + 1) * 128],
                                XQv(k)[:, 0:257], start=(k == 0),
                                stop=False, skip_group_check=True), inc=None)
                        for k in range(DC):
                            K.op("tensor", lambda e, k=k, bk=bk, wt=wt: e.matmul(
                                PS[bk][:, 0:1], wt[:, k * 128:(k + 1) * 128],
                                XQHB[:, k:k + 1], start=False,
                                stop=(k == DC - 1), skip_group_check=True),
                                inc=(sT if k == DC - 1 else None))
                    elif tti == 7:
                        for k in range(DC):
                            K.op("tensor", lambda e, k=k, bk=bk, wt=wt, t0=t0:
                                 e.matmul(
                                PS[bk][:, 0:257], wt[:, k * 128:(k + 1) * 128],
                                XQv(k)[:, t0 - 1:t0 + 256], start=(k == 0),
                                stop=False, skip_group_check=True), inc=None)
                        for k in range(DC):
                            K.op("tensor", lambda e, k=k, bk=bk, wt=wt: e.matmul(
                                PS[bk][:, 257:258], wt[:, k * 128:(k + 1) * 128],
                                XQHB[:, 8 + k:9 + k], start=False,
                                stop=(k == DC - 1), skip_group_check=True),
                                inc=(sT if k == DC - 1 else None))
                    else:
                        for k in range(DC):
                            K.op("tensor", lambda e, k=k, bk=bk, wt=wt, t0=t0:
                                 e.matmul(
                                PS[bk][:, 0:258], wt[:, k * 128:(k + 1) * 128],
                                XQv(k)[:, t0 - 1:t0 + 257], start=(k == 0),
                                stop=(k == DC - 1)),
                                inc=(sT if k == DC - 1 else None))
                    bank_ready[bk] = (sT, sT.n)
                    wr_consumed(slot)
                    consume("vector", bk)
                    K.wait("vector", sS)
                    if tti == 0:
                        vop(lambda e, bk=bk: e.tensor_tensor(
                            out=TV[:, 1:258], in0=PS[bk][:, 1:258],
                            in1=STIX[:, 0:257], op=ALU.mult))
                        vop(lambda e, bk=bk: e.tensor_scalar(
                            out=TV[:, 0:1], in0=PS[bk][:, 0:1],
                            scalar1=HL[:, 17:18], scalar2=None, op0=ALU.mult))
                    elif tti == 7:
                        vop(lambda e, bk=bk, t0=t0: e.tensor_tensor(
                            out=TV[:, 0:257], in0=PS[bk][:, 0:257],
                            in1=STIX[:, t0 - 1:t0 + 256], op=ALU.mult))
                        vop(lambda e, bk=bk: e.tensor_scalar(
                            out=TV[:, 257:258], in0=PS[bk][:, 257:258],
                            scalar1=HR[:, 16:17], scalar2=None, op0=ALU.mult))
                    else:
                        vop(lambda e, bk=bk, t0=t0: e.tensor_tensor(
                            out=TV, in0=PS[bk][:, 0:258],
                            in1=STIX[:, t0 - 1:t0 + 257], op=ALU.mult))
                    release(bk, sV)
                    vop(lambda e, m=m: e.tensor_scalar(
                        out=CVt, in0=TV[:, 0:256], scalar1=CW[0][:, m:m + 1],
                        scalar2=col["f1_cb"][:, m:m + 1], op0=ALU.mult,
                        op1=ALU.add))
                    vop(lambda e, m=m: e.scalar_tensor_tensor(
                        out=CVt, in0=TV[:, 1:257], scalar=CW[1][:, m:m + 1],
                        in1=CVt, op0=ALU.mult, op1=ALU.add))
                    vop(lambda e, m=m: e.scalar_tensor_tensor(
                        out=CVt, in0=TV[:, 2:258], scalar=CW[2][:, m:m + 1],
                        in1=CVt, op0=ALU.mult, op1=ALU.add))
                    K.wait("scalar", sV)
                    sop(lambda e, m=m: e.activation(out=SV(m), in_=CVt,
                                                    func=AF.Silu))
            if FSTEP <= 4:
                return
            K.wait("vector", sS)
            AMS2 = SM[:, 768:1024]
            K.wait("scalar", sV)
            sop(lambda e: e.activation(out=AM2, in_=SV(0), func=AF.Abs))
            K.wait("vector", sS)
            for m in range(1, 32):
                K.wait("scalar", sV)
                sop(lambda e, m=m: e.activation(out=AMS2, in_=SV(m),
                                                func=AF.Abs))
                K.wait("vector", sS)
                vop(lambda e: e.tensor_max(out=AM2, in0=AM2, in1=AMS2))
            S2C = XTRA[:, 36:38]
            for j in range(2):
                b = j % 2
                K.wait_pair("tensor", bank_free[b])
                K.wait("tensor", sV)
                K.op("tensor", lambda e, j=j, b=b: e.transpose(
                    PS[b][:, 0:128], AM2[:, j * 128:(j + 1) * 128], ident),
                    inc=sT)
                bank_ready[b] = (sT, sT.n)
                consume("vector", b)
                vop(lambda e, j=j, b=b: e.tensor_reduce(
                    out=S2C[:, j:j + 1], in_=PS[b][:, 0:128],
                    axis=mybir.AxisListType.X, op=ALU.max))
                release(b, sV)
            MX2 = XTRA[:, 38:40]
            vop(lambda e: e.tensor_scalar_max(out=MX2, in0=S2C, scalar1=1e-5))
            K.wait("scalar", sV)
            sop(lambda e: e.activation(out=XTRA[:, 60:62], in_=MX2, func=AF.Ln))
            sop(lambda e: e.activation(out=XTRA[:, 62:64], in_=XTRA[:, 60:62],
                                       func=AF.Exp, scale=f32(-1.0)))
            K.wait("vector", sS)
            ST2 = XTRA[:, 60:62]
            vop(lambda e: e.tensor_scalar(out=ST2, in0=XTRA[:, 62:64],
                                          scalar1=127.0, scalar2=None,
                                          op0=ALU.mult))
            ST2I = XTRA[:, 62:64]
            vop(lambda e: e.tensor_scalar(out=ST2I, in0=MX2,
                                          scalar1=1.0 / 127.0, scalar2=None,
                                          op0=ALU.mult))
            R2 = ST2X[0:1, 1024:1280]
            RI2 = ST2X[0:1, 1280:1536]
            for j in range(2):
                b = j % 2
                K.wait_pair("tensor", bank_free[b])
                K.wait("tensor", sV)
                K.op("tensor", lambda e, j=j, b=b: e.matmul(
                    PS[b][0:1, 0:128], ST2[:, j:j + 1], ident, start=True,
                    stop=True), inc=sT)
                bank_ready[b] = (sT, sT.n)
                consume("vector", b)
                vop(lambda e, j=j, b=b: e.tensor_copy(
                    out=R2[:, j * 128:(j + 1) * 128], in_=PS[b][0:1, 0:128]))
                release(b, sV)
                b2 = 2 + j % 2
                K.wait_pair("tensor", bank_free[b2])
                K.wait("tensor", sV)
                K.op("tensor", lambda e, j=j, b2=b2: e.matmul(
                    PS[b2][0:1, 0:128], ST2I[:, j:j + 1], ident, start=True,
                    stop=True), inc=sT)
                bank_ready[b2] = (sT, sT.n)
                consume("vector", b2)
                vop(lambda e, j=j, b2=b2: e.tensor_copy(
                    out=RI2[:, j * 128:(j + 1) * 128], in_=PS[b2][0:1, 0:128]))
                release(b2, sV)
            ST2B = ST2X[:, 0:256]
            ST2IB = ST2X[:, 256:512]
            for which, row, dstb, bb in ((0, R2, ST2B, 4), (1, RI2, ST2IB, 5)):
                K.wait_pair("tensor", bank_free[bb])
                K.wait("tensor", sV)
                K.op("tensor", lambda e, row=row, bb=bb: e.matmul(
                    PS[bb][:, 0:256], ones[0:1, :], row, start=True,
                    stop=True), inc=sT)
                bank_ready[bb] = (sT, sT.n)
                consume("vector", bb)
                vop(lambda e, dstb=dstb, bb=bb: e.tensor_copy(
                    out=dstb, in_=PS[bb][:, 0:256]))
                release(bb, sV)
            Q2F = ST2X[:, 512:768]
            Q2I = ST2X[:, 768:1024].bitcast(I32)[:, 0:256]
            Q2O = ST2X[:, 1792:2048]
            for m in range(32):
                vop(lambda e, m=m: e.tensor_tensor(out=Q2F, in0=SV(m),
                                                   in1=ST2B, op=ALU.mult))
                emit_round(Q2O, Q2F, Q2I, SM[:, 0:256], SM[:, 256:512])
                vop(lambda e, m=m: e.tensor_copy(out=Q2(m), in_=Q2O))
            if FSTEP <= 5:
                return
            for m2 in range(DC):
                bk = 4 + m2 % 2
                for sub in range(2):
                    slot = 2 + sub
                    wr_load(slot, w2q_d[m2][:, sub * 16:(sub + 1) * 16, :])
                    wr_wait("tensor", slot)
                    if sub == 0:
                        K.wait_pair("tensor", bank_free[bk])
                        K.wait("tensor", sV)
                    wt = wrb(slot, 2048)
                    for k2 in range(16):
                        K.op("tensor", lambda e, k2=k2, sub=sub, bk=bk, wt=wt:
                             e.matmul(
                                 PS[bk][:, 0:256], wt[:, k2 * 128:(k2 + 1) * 128],
                                 Q2(sub * 16 + k2), start=(sub == 0 and k2 == 0),
                                 stop=(sub == 1 and k2 == 15)),
                             inc=(sT if k2 == 15 else None))
                    wr_consumed(slot)
                bank_ready[bk] = (sT, sT.n)
                consume("vector", bk)
                vop(lambda e, bk=bk: e.tensor_tensor(
                    out=Q2F, in0=PS[bk][:, 0:256], in1=ST2IB, op=ALU.mult))
                vop(lambda e, m2=m2, t0=t0: e.scalar_tensor_tensor(
                    out=X[m2][:, t0:t0 + 256], in0=Q2F,
                    scalar=SW2I05, in1=X[m2][:, t0:t0 + 256],
                    op0=ALU.mult, op1=ALU.add))
                release(bk, sV)
            barrier()
        barrier()


    def l1_attn_block():
        layernorm(X, LN, 1e-5, sq2=(S[0], S[1]))
        barrier()
        for tt in range(NTT):
            sl = slice(tt * TT, (tt + 1) * TT)
            for m in range(DC):
                slot = m % 2
                wr_load(slot, dr["l1_Win"][m])
                wr_wait("tensor", slot)
                mmgroup(m, [(wr_tile(slot, k), LN[k][:, sl], PS[m][:])
                            for k in range(DC)])
                wr_consumed(slot)
            K.wait("vector", sT)
            for m in range(DC):
                vop(lambda e, m=m, sl=sl: e.tensor_copy(out=LN[m][:, sl],
                                                        in_=PS[m][:]))
                release(m, sV)
        barrier()
        layernorm(LN, LN, 1e-5, g_col=col["l1_lng"], b_col=col["l1_lnb"],
                  sq2=(S[0], S[1]))
        barrier()
        A_ = ST2X[0:16, 0:2048]
        B_ = STIX[0:16, 0:2048]
        D_ = S[1][0:16, 0:2048]
        E_ = S[2][0:16, 0:2048]
        F_ = S[3][0:16, 0:2048]
        CUMM = ST2X[32:48, 0:2048]
        GROM = ST2X[64:80, 0:2048]
        MMB = A_
        ALPH = A_
        BETA = D_
        K.wait_pair("sync", wr_free[0])
        wg = dr["l1_WgateT"].rearrange("(a p) m -> p a m", p=128)
        K.op("sync", lambda e: e.dma_start(
            out=WR[:, 0:256].rearrange("p (a m) -> p a m", m=32), in_=wg),
            inc=sWR[0], k=16)
        wr_ready[0] = (sWR[0], sWR[0].n)
        wr_wait("tensor", 0)
        sop(lambda e: e.activation(out=XTRA[0:16, 60:61],
                                   in_=col["l1_eigraw"], func=AF.Tanh))
        for tt in range(NTT):
            sl = slice(tt * TT, (tt + 1) * TT)
            for who, out_rows, bcol in ((0, ALPH, "l1_bga"), (1, BETA, "l1_bgb")):
                b = who * 2 + tt % 2
                mmgroup(b, [(WR[:, k * 32 + who * 16:k * 32 + who * 16 + 16],
                             LN[k][:, sl], PS[b][0:16, :]) for k in range(DC)])
                consume("scalar", b)
                K.wait("scalar", sV)
                sop(lambda e, out_rows=out_rows, sl=sl, b=b, bcol=bcol:
                    e.activation(out=out_rows[:, sl], in_=PS[b][0:16, :],
                                 func=AF.Sigmoid, bias=col[bcol]))
                release(b, sS)
        wr_free[0] = (sT, sT.n)
        K.wait("vector", sS)
        vop(lambda e: e.tensor_scalar(out=ALPH, in0=ALPH,
                                      scalar1=XTRA[0:16, 60:61], scalar2=None,
                                      op0=ALU.mult))
        def v3_16(ap):
            return ap.rearrange("p (c i) -> p c i", i=16)
        cmv16 = cm128[0:16, 0:16].unsqueeze(1).broadcast_to((16, 128, 16))
        icmv16 = icm128[0:16, 0:16].unsqueeze(1).broadcast_to((16, 128, 16))
        vop(lambda e: e.tensor_tensor(out=v3_16(B_), in0=v3_16(ALPH),
                                      in1=cmv16, op=ALU.mult))
        vop(lambda e: e.tensor_tensor(out=v3_16(E_), in0=v3_16(ALPH),
                                      in1=icmv16, op=ALU.mult))
        vop(lambda e: e.tensor_tensor_scan(out=A_, data0=B_, data1=E_,
                                           initial=0.0, op0=ALU.mult,
                                           op1=ALU.add))
        K.wait("gpsimd", sV)
        K.op("gpsimd", lambda e: e.dma_start(out=CUMM, in_=A_), inc=sG, k=16)
        vop(lambda e: e.tensor_copy(out=B_[:, 1:2048], in_=A_[:, 0:2047]))
        vop(lambda e: e.memset(B_[:, 0:1], 0.0))
        vop(lambda e: e.tensor_tensor(out=v3_16(B_), in0=v3_16(B_),
                                      in1=cmv16, op=ALU.mult))
        vop(lambda e: e.tensor_tensor(out=v3_16(B_), in0=v3_16(B_),
                                      in1=icmv16, op=ALU.add))
        vop(lambda e: e.tensor_single_scalar(out=E_, in_=B_, scalar=1e-8,
                                             op=ALU.is_gt))
        K.wait("vector", sG)
        vop(lambda e: e.tensor_scalar_max(out=A_, in0=B_, scalar1=1e-9))
        K.wait("scalar", sV)
        sop(lambda e: e.activation(out=F_, in_=A_, func=AF.Ln))
        sop(lambda e: e.activation(out=A_, in_=F_, func=AF.Exp,
                                   scale=f32(-1.0)))
        K.wait("vector", sS)
        vop(lambda e: e.tensor_mul(out=B_, in0=E_, in1=A_))
        K.wait("gpsimd", sV)
        K.op("gpsimd", lambda e: e.dma_start(out=GROM, in_=B_), inc=sG, k=16)
        K.wait("vector", sG)
        K.wait("tensor", sG)
        barrier()

        def selmm(band, c, tt, bank):
            K.wait_pair("tensor", bank_free[bank])
            K.wait("tensor", sV)
            K.wait("tensor", sS)
            K.op("tensor", lambda e, band=band, c=c, tt=tt, bank=bank: e.matmul(
                PS[bank][:], col["sel"][:, c * 128:(c + 1) * 128],
                band[:, tt * TT:(tt + 1) * TT], start=True, stop=True), inc=sT)
            bank_ready[bank] = (sT, sT.n)

        for c in range(DC):
            for tt in range(NTT):
                bank = (c * NTT + tt) % 4
                selmm(BETA, c, tt, bank)
                consume("vector", bank)
                vop(lambda e, c=c, tt=tt, bank=bank: e.tensor_mul(
                    out=LN[c][:, tt * TT:(tt + 1) * TT],
                    in0=LN[c][:, tt * TT:(tt + 1) * TT], in1=PS[bank][:]))
                release(bank, sV)
        barrier()

        CMF = S[3]
        cmv128 = cm128[:, 0:16].unsqueeze(1).broadcast_to((128, 128, 16))
        vop(lambda e: e.tensor_copy(
            out=CMF[:, :].rearrange("p (c i) -> p c i", i=16), in_=cmv128))
        ZCOL = SM[:, 0:128]
        vop(lambda e: e.memset(ZCOL, 0.0))
        GV = S[0]
        D1F = S[1]
        CX = S[2]

        def build_cx(c):
            K.wait("gpsimd", sT)
            K.op("gpsimd", lambda e: e.dma_start(out=MMB, in_=CUMM),
                 inc=sG, k=16)
            K.wait("tensor", sG)
            for tt in range(NTT):
                bank = 4 + tt % 2
                selmm(MMB, c, tt, bank)
                consume("vector", bank)
                vop(lambda e, tt=tt, bank=bank: e.tensor_copy(
                    out=CX[:, tt * TT:(tt + 1) * TT], in_=PS[bank][:]))
                release(bank, sV)

        for c in range(DC):
            build_cx(c)
            K.wait("gpsimd", sT)
            K.op("gpsimd", lambda e: e.dma_start(out=MMB, in_=GROM),
                 inc=sG, k=16)
            K.wait("tensor", sG)
            for tt in range(NTT):
                bank = tt % 4
                selmm(MMB, c, tt, bank)
                consume("vector", bank)
                vop(lambda e, c=c, tt=tt, bank=bank: e.tensor_mul(
                    out=GV[:, tt * TT:(tt + 1) * TT],
                    in0=LN[c][:, tt * TT:(tt + 1) * TT], in1=PS[bank][:]))
                release(bank, sV)
            vop(lambda e: e.tensor_copy(out=D1F[:, 1:2048], in_=GV[:, 0:2047]))
            vop(lambda e: e.memset(D1F[:, 0:1], 0.0))
            vop(lambda e: e.tensor_mul(out=D1F[:, :], in0=D1F[:, :],
                                       in1=CMF[:, :]))
            vop(lambda e: e.tensor_tensor_scan(out=GV[:, :], data0=CMF[:, :],
                                               data1=D1F[:, :], initial=0.0,
                                               op0=ALU.mult, op1=ALU.add))
            vop(lambda e: e.tensor_mul(out=D1F[:, :], in0=CX[:, :],
                                       in1=GV[:, :]))
            vop(lambda e, c=c: e.tensor_add(out=LN[c][:, :], in0=LN[c][:, :],
                                            in1=D1F[:, :]))
            vop(lambda e, c=c: e.tensor_tensor_scan(
                out=STIX[:, c * 128:(c + 1) * 128], data0=CX[:, 15::16],
                data1=LN[c][:, 15::16], initial=0.0, op0=ALU.mult, op1=ALU.add))
            vop(lambda e, c=c: e.tensor_tensor_scan(
                out=STIX[:, 1024 + c * 128:1024 + (c + 1) * 128],
                data0=CX[:, 15::16], data1=ZCOL, initial=1.0,
                op0=ALU.mult, op1=ALU.add))
        barrier()

        CB2 = SM[:, 0:16]
        for c in range(DC):
            vop(lambda e, c=c: e.tensor_copy(
                out=CB2[:, c:c + 1], in_=STIX[:, c * 128 + 127:c * 128 + 128]))
            vop(lambda e, c=c: e.tensor_copy(
                out=CB2[:, 8 + c:9 + c],
                in_=STIX[:, 1024 + c * 128 + 127:1024 + c * 128 + 128]))
        K.wait("gpsimd", sV)
        K.op("gpsimd", lambda e: e.dma_start(out=cg_in, in_=CB2), inc=sG, k=16)
        K.wait("gpsimd", sG)
        K.op("gpsimd", lambda e: e.collective_compute(
            "AllGather", ALU.bypass, replica_groups=[list(range(8))],
            ins=[cg_in.opt()], outs=[cg_out.opt()]), inc=sCC)
        K.wait("gpsimd", sCC)
        GC = SM[:, 128:256]
        for j in range(8):
            K.op("gpsimd", lambda e, j=j: e.dma_start(
                out=GC[:, j * 16:(j + 1) * 16],
                in_=cg_out[j * 128:(j + 1) * 128, :]), inc=sG, k=16)
        barrier()
        HIN = SM[:, 256:264]
        PE_ = SM[:, 264:272]
        EE_ = SM[:, 272:280]
        SC8 = SM[:, 280:288]
        for c in range(DC):
            pg = GC[:, 8 + c::16]
            eg = GC[:, c::16]
            vop(lambda e, pg=pg: e.tensor_mul(out=PE_, in0=pg,
                                              in1=col["mexp"]))
            vop(lambda e: e.tensor_add(out=PE_, in0=PE_, in1=col["imexp"]))
            vop(lambda e, eg=eg: e.tensor_mul(out=EE_, in0=eg,
                                              in1=col["mexp"]))
            vop(lambda e: e.tensor_tensor_scan(out=SC8, data0=PE_, data1=EE_,
                                               initial=0.0, op0=ALU.mult,
                                               op1=ALU.add))
            vop(lambda e, c=c: e.tensor_copy(out=HIN[:, c:c + 1],
                                             in_=SC8[:, 7:8]))
        for c in range(DC):
            vop(lambda e, c=c: e.scalar_tensor_tensor(
                out=STIX[:, c * 128:(c + 1) * 128],
                in0=STIX[:, 1024 + c * 128:1024 + (c + 1) * 128],
                scalar=HIN[:, c:c + 1],
                in1=STIX[:, c * 128:(c + 1) * 128],
                op0=ALU.mult, op1=ALU.add))
        for c in range(DC):
            vop(lambda e, c=c: e.tensor_copy(
                out=STIX[:, 1024 + c * 128 + 1:1024 + (c + 1) * 128],
                in_=STIX[:, c * 128:c * 128 + 127]))
            vop(lambda e, c=c: e.tensor_copy(
                out=STIX[:, 1024 + c * 128:1024 + c * 128 + 1],
                in_=HIN[:, c:c + 1]))
        barrier()
        for c in range(DC):
            build_cx(c)
            hp3 = STIX[:, 1024 + c * 128:1024 + (c + 1) * 128].unsqueeze(
                2).broadcast_to((128, 128, 16))
            vop(lambda e, c=c, hp3=hp3: e.tensor_tensor(
                out=D1F[:, :].rearrange("p (n i) -> p n i", i=16),
                in0=CX[:, :].rearrange("p (n i) -> p n i", i=16),
                in1=hp3, op=ALU.mult))
            vop(lambda e, c=c: e.tensor_add(out=LN[c][:, :], in0=LN[c][:, :],
                                            in1=D1F[:, :]))
        barrier()
        for tt in range(NTT):
            sl = slice(tt * TT, (tt + 1) * TT)
            for m in range(DC):
                slot = m % 2
                wr_load(slot, dr["l1_Wout"][m])
                wr_wait("tensor", slot)
                b = m % 4
                K.wait("tensor", sV)
                mmgroup(b, [(wr_tile(slot, k), LN[k][:, sl], PS[b][:])
                            for k in range(DC)])
                wr_consumed(slot)
                consume("vector", b)
                vop(lambda e, m=m, sl=sl, b=b: e.tensor_add(
                    out=X[m][:, sl], in0=X[m][:, sl], in1=PS[b][:]))
                release(b, sV)
        barrier()

    load_consts()
    barrier()
    if STAGE >= 1 and not SKIPL0:
        moe_block()
    if STAGE >= 2 and not SKIPL0:
        l0_attn_block()
    if STAGE >= 3 and not SKIPL0:
        moe_block()
    if STAGE >= 4:
        l1_ffn_block()
    if STAGE >= 5:
        l1_attn_block()
    if STAGE >= 6:
        l1_ffn_block()

    barrier()
    for i in range(DC):
        K.op("gpsimd", lambda e, i=i: e.dma_start(
            out=out[i * 128:(i + 1) * 128, :], in_=X[i][:, :]), inc=sST, k=16)
    if DBG is not None:
        dbg = nc.dram_tensor("dbg", [128, T], F32, kind="ExternalOutput").ap()
        which, idx = DBG
        src_ap = {"ln": lambda: LN[idx][:, :],
                  "x": lambda: X[idx][:, :],
                  "stix": lambda: STIX[:, 0:T],
                  "st2x": lambda: ST2X[:, 0:T],
                  "sm": lambda: SM[:, 0:1920],
                  "s": lambda: S[idx][:, :]}[which]()
        K.op("gpsimd", lambda e, a=src_ap: e.dma_start(
            out=dbg[:, 0:a.shape[-1]] if a.shape[-1] != T else dbg, in_=a),
            inc=sST, k=16)
    K.wait("gpsimd", sST)

    block = K.E(nc.Block())
    K.finish(block)
    K.stack.close()
    return nc


def kernel(**inputs):
    ins = _prep_inputs(inputs)
    nc = build_program()
    res = run_bass_kernel_spmd(nc, ins, core_ids=list(range(NCORES)))
    x = np.asarray(inputs["L0_attn_x"])
    outp = np.empty_like(x)
    for c in range(NCORES):
        b, q = c // 4, c % 4
        outp[b, q * T:(q + 1) * T, :] = res.results[c]["o"].T
    return outp
